# version 01
# baseline (speedup 1.0000x reference)
import contextlib
import numpy as np
import ml_dtypes
import concourse.bass as bass
import concourse.mybir as mybir
from concourse.bass_utils import run_bass_kernel_spmd

F32 = mybir.dt.float32
BF16 = mybir.dt.bfloat16
I32 = mybir.dt.int32
AF = mybir.ActivationFunctionType
ALU = mybir.AluOpType
AX = mybir.AxisListType

ENGINES = ("tensor", "vector", "scalar", "gpsimd", "sync")
DMA_QUEUES = ("sync", "gpsimd", "scalar", "vector")
B, T, D = 4, 4096, 1024
HT = T // 2
NTC = HT // 128
GROUPS = [[0, 1], [2, 3], [4, 5], [6, 7]]
F32EPS = float(np.finfo(np.float32).eps)
BF = ml_dtypes.bfloat16


class Op:
    __slots__ = ("fn", "waits", "incs", "tag")

    def __init__(self, fn, tag=""):
        self.fn = fn
        self.waits = []
        self.incs = []
        self.tag = tag


class Prog:
    NDMASEM = 4

    def __init__(self, nc):
        self.nc = nc
        self.streams = {e: [] for e in ENGINES}
        self.counts = {e: 0 for e in ENGINES}
        self.dma_counts = {}
        self.dma_rr = {q: 0 for q in DMA_QUEUES}
        for q in DMA_QUEUES:
            for i in range(self.NDMASEM):
                self.dma_counts[f"dma_{q}{i}"] = 0
        self.dma_counts["cc"] = 0
        self.sem_handles = {}

    def op(self, eng, fn, tag=""):
        o = Op(fn, tag)
        self.streams[eng].append(o)
        return o

    def _tok(self, eng, o):
        self.counts[eng] += 1
        o.incs.append((eng, 1))
        return (eng, self.counts[eng])

    def step(self, eng, fn, deps=(), tag=""):
        o = self.op(eng, fn, tag)
        for d in deps:
            if d is not None:
                o.waits.append(d)
        return self._tok(eng, o)

    def mm(self, out, lhsT, rhs, start, stop, deps=(), tag=""):
        return self.step("tensor",
                         lambda t: t.matmul(out, lhsT, rhs, start=start, stop=stop,
                                            skip_group_check=True),
                         deps, tag)

    def mmgroup(self, out, ops, deps_first=(), tag=""):
        n = len(ops)
        tok = None
        for i, (lh, rh, dd) in enumerate(ops):
            deps = list(dd)
            if i == 0:
                deps += list(deps_first)
            tok = self.mm(out, lh, rh, i == 0, i == n - 1, deps, tag=f"{tag}[{i}]")
        return tok

    def tr(self, out_ps, in_, ident, deps=(), tag=""):
        return self.step("tensor", lambda t: t.transpose(out_ps, in_, ident), deps, tag)

    def act(self, out, in_, func, bias=0.0, scale=1.0, accum_out=None, deps=(), tag=""):
        return self.step("scalar", lambda s: s.activation(out, in_, func, bias=bias,
                                                          scale=scale, accum_out=accum_out),
                         deps, tag)

    def scopy(self, out, in_, deps=(), tag=""):
        return self.act(out, in_, AF.Copy, deps=deps, tag=tag)

    def vcopy(self, out, in_, deps=(), tag=""):
        return self.step("vector", lambda v: v.tensor_copy(out, in_), deps, tag)

    def gcopy(self, out, in_, deps=(), tag=""):
        return self.step("gpsimd", lambda g: g.tensor_copy(out, in_), deps, tag)

    def tt(self, out, a, b, alu, deps=(), tag=""):
        return self.step("vector", lambda v: v.tensor_tensor(out, a, b, alu), deps, tag)

    def ts(self, out, in0, s1, s2=None, op0=ALU.mult, op1=ALU.bypass, accum_out=None,
           deps=(), tag=""):
        return self.step("vector", lambda v: v.tensor_scalar(out, in0, s1, s2, op0, op1,
                                                             accum_out=accum_out), deps, tag)

    def stt(self, out, in0, scalar, in1, op0, op1, deps=(), tag=""):
        return self.step("vector", lambda v: v.scalar_tensor_tensor(out, in0, scalar, in1,
                                                                    op0, op1), deps, tag)

    def red(self, out, in_, alu, deps=(), negate=False, absv=False, tag=""):
        return self.step("vector", lambda v: v.tensor_reduce(out, in_, AX.X, alu,
                                                             apply_absolute_value=absv,
                                                             negate=negate), deps, tag)

    def recip(self, out, in_, deps=(), tag=""):
        return self.step("vector", lambda v: v.reciprocal(out, in_), deps, tag)

    def memset(self, ap, val, deps=(), tag=""):
        return self.step("gpsimd", lambda g: g.memset(ap, val), deps, tag)

    def scan(self, out, d0, d1, init, deps=(), tag=""):
        return self.step("vector", lambda v: v.tensor_tensor_scan(out, d0, d1, init,
                                                                  ALU.mult, ALU.add), deps, tag)

    def dma(self, q, out, in_, deps=(), tag="", sem=None, **kw):
        def fn(e):
            return e.dma_start(out=out, in_=in_, **kw)
        o = self.op(q, fn, tag)
        for d in deps:
            if d is not None:
                o.waits.append(d)
        if sem is None:
            i = self.dma_rr[q]
            self.dma_rr[q] = (i + 1) % self.NDMASEM
            sem = f"dma_{q}{i}"
        if self.dma_counts[sem] > 0:
            o.waits.append((sem, self.dma_counts[sem]))
        self.dma_counts[sem] += 16
        o.incs.append((sem, 16))
        return (sem, self.dma_counts[sem])

    def cc(self, kind, alu, ins_, outs_, deps=(), tag=""):
        def fn(e):
            return e.collective_compute(kind, alu, GROUPS, ins_, outs_)
        o = self.op("gpsimd", fn, tag)
        for d in deps:
            if d is not None:
                o.waits.append(d)
        if self.dma_counts["cc"] > 0:
            o.waits.append(("cc", self.dma_counts["cc"]))
        self.dma_counts["cc"] += 16
        o.incs.append(("cc", 16))
        return ("cc", self.dma_counts["cc"])

    def barrier(self, tag="bar"):
        toks = self.all_fences()
        for e in ENGINES:
            o = self.op(e, lambda en: en.nop(), tag=f"{tag}_{e}")
            o.waits += [t for t in toks if t[0] != e]

    def fence(self, eng):
        return (eng, self.counts[eng])

    def dma_fence(self, q):
        return [(f"dma_{q}{i}", self.dma_counts[f"dma_{q}{i}"])
                for i in range(self.NDMASEM) if self.dma_counts[f"dma_{q}{i}"] > 0]

    def all_fences(self):
        toks = [self.fence(e) for e in ENGINES if self.counts[e] > 0]
        for q in DMA_QUEUES:
            toks += self.dma_fence(q)
        if self.dma_counts["cc"] > 0:
            toks.append(("cc", self.dma_counts["cc"]))
        return toks

    def emit(self, stack):
        nc = self.nc
        for n in list(ENGINES) + list(self.dma_counts.keys()):
            self.sem_handles[n] = stack.enter_context(nc.semaphore("sem_" + n))
        blk = stack.enter_context(nc.Block())

        def play(eng_name):
            def run(e):
                for o in self.streams[eng_name]:
                    for (sn, v) in o.waits:
                        e.wait_ge(self.sem_handles[sn], v)
                    inst = o.fn(e)
                    if o.incs:
                        assert inst is not None, f"op {o.tag} returned None but has incs"
                        for (sn, k) in o.incs:
                            inst.then_inc(self.sem_handles[sn], k)
            return run

        blk.tensor(play("tensor"))
        blk.vector(play("vector"))
        blk.scalar(play("scalar"))
        blk.gpsimd(play("gpsimd"))
        blk.sync(play("sync"))



CONST_SPECS = {
    "c_ident": ((128, 128), BF16), "c_identf": ((128, 128), F32),
    "c_su_f": ((128, 128), F32), "c_uti_f": ((128, 128), F32),
    "c_su_b": ((128, 128), BF16), "c_attnmask": ((128, 2048), BF16),
    "c_ones_col_b": ((128, 1), BF16), "c_ones_col_f": ((128, 1), F32),
    "c_ones_row_b": ((1, 128), BF16), "c_ones_row_f": ((1, 128), F32),
    "c_halfsel": ((128, 2), BF16), "c_iotac": ((128, 4), F32),
    "c_iotarow": ((128, 128), F32), "c_pow2": ((18, 2), BF16),
    "c_ones_row512f": ((1, 512), F32),
}


def _consts():
    c = {}
    eye = np.eye(128, dtype=np.float32)
    c["c_ident"] = eye.astype(BF)
    c["c_identf"] = eye.copy()
    iu = np.triu(np.ones((128, 128), np.float32), 1)
    c["c_su_f"] = iu.copy()
    c["c_uti_f"] = np.triu(np.ones((128, 128), np.float32), 0)
    c["c_su_b"] = iu.astype(BF)
    ki = np.arange(128)[:, None]
    qi = np.arange(512)[None, :]
    m = np.stack([np.where(o * 128 + ki > qi, -1e9, 0.0) for o in range(4)])
    c["c_attnmask"] = np.ascontiguousarray(m.transpose(1, 0, 2).reshape(128, 2048)).astype(BF)
    c["c_ones_col_b"] = np.ones((128, 1), BF)
    c["c_ones_col_f"] = np.ones((128, 1), np.float32)
    c["c_ones_row_b"] = np.ones((1, 128), BF)
    c["c_ones_row_f"] = np.ones((1, 128), np.float32)
    hs = np.zeros((128, 2), np.float32)
    hs[:64, 0] = 1.0
    hs[64:, 1] = 1.0
    c["c_halfsel"] = hs.astype(BF)
    c["c_iotac"] = (np.arange(128, dtype=np.float32)[:, None]
                    + 128.0 * np.arange(4, dtype=np.float32)[None, :])
    c["c_iotarow"] = np.broadcast_to(np.arange(128, dtype=np.float32), (128, 128)).copy()
    c["c_ones_row512f"] = np.ones((1, 512), np.float32)
    ps = np.zeros((18, 2), np.float32)
    for i in range(18):
        ps[i, i // 9] = float(2 ** (i % 9))
    c["c_pow2"] = ps.astype(BF)
    for k in c:
        assert c[k].shape == CONST_SPECS[k][0], k
    return c


def host_prep(ins):
    f32 = np.float32
    ct = lambda a: np.ascontiguousarray(a, dtype=f32)
    x = ct(ins["L0_attn_hidden_states"])
    consts = _consts()
    maps = []
    for c in range(8):
        b, r = c // 2, c % 2
        m = dict(consts)
        xb = x[b]
        m["i_xT"] = ct(xb.T)
        m["i_x_own"] = ct(xb[r * HT:(r + 1) * HT])
        m["w_qT"] = ct(ins["L0_attn_Wq"][512 * r:512 * (r + 1)].T)
        m["w_kT"] = ct(ins["L0_attn_Wk"][128 * r:128 * (r + 1)].T)
        m["w_vT"] = ct(ins["L0_attn_Wv"][128 * r:128 * (r + 1)].T)
        m["w_oT"] = ct(ins["L0_attn_Wo"].T[512 * r:512 * (r + 1)])
        m["w_f0w1"] = ct(ins["L0_ffn_w1"].reshape(8 * 1024, 2048))
        m["w_f0w2"] = ct(ins["L0_ffn_w2"].reshape(8 * 2048, 1024))
        m["w_f0b1T"] = ct(ins["L0_ffn_b1"].T)
        m["w_f0b2"] = ct(ins["L0_ffn_b2"])
        m["w_f0gateT"] = ct(ins["L0_ffn_gate_w"].T)
        m["w_f0gateb"] = ct(ins["L0_ffn_gate_b"].reshape(1, 8))
        m["w_f0rmsw"] = ct(ins["L0_ffn_rms_w"].reshape(1, 8))
        maps.append(m)
    return maps


def add_launchC_inputs(maps, resB, ins):
    f32 = np.float32
    ct = lambda a: np.ascontiguousarray(a, dtype=f32)
    x2 = {}
    for b in range(4):
        x2[b] = np.concatenate([resB[2 * b]["d_x2_own"], resB[2 * b + 1]["d_x2_own"]], axis=0)
    for c in range(8):
        b, r = c // 2, c % 2
        m = maps[c]
        m["i_x2T"] = ct(x2[b].T)
        m["w_conv"] = ct(ins["L1_attn_conv_w"].reshape(8, 128, 4).transpose(1, 0, 2).reshape(128, 32))
        m["w_l1gT"] = ct(ins["L1_attn_gate_w"].T)
        m["w_l1gb"] = ct(ins["L1_attn_gate_b"].reshape(8, 128).T)
        m["w_sinT"] = ct(ins["L1_attn_state_in_w"][2048 * r:2048 * (r + 1)].T)
        m["w_soT"] = ct(ins["L1_attn_state_out_w"].T[2048 * r:2048 * (r + 1)])
        m["w_mydecay"] = ct(ins["L1_attn_state_decay_logit"][2 * r:2 * (r + 1)].reshape(1, 2))
        m["w_keysT"] = ct(np.concatenate([ins["L1_attn_qkey_w"], ins["L1_attn_wkey_w"],
                                          ins["L1_attn_value_w"]], axis=0).T)
        m["w_bitsT"] = ct(np.concatenate([ins["L1_attn_read_bits_w"],
                                          ins["L1_attn_write_bits_w"]], axis=0).T)
        pR = np.zeros((36, 1), f32)
        pW = np.zeros((36, 1), f32)
        for i in range(9):
            pR[9 * r + i, 0] = float(2 ** i)
            pW[18 + 9 * r + i, 0] = float(2 ** i)
        m["w_powR"] = pR
        m["w_powW"] = pW
        m["w_g3T"] = ct(np.concatenate([ins["L1_attn_write_gate_w"], ins["L1_attn_gate_long_w"],
                                        ins["L1_attn_gate_mem_w"]], axis=0).T)
        m["w_g3b"] = ct(np.concatenate([ins["L1_attn_write_gate_b"], ins["L1_attn_gate_long_b"],
                                        ins["L1_attn_gate_mem_b"]]).reshape(1, 3))


def add_launchD_inputs(maps, resB, resC, ins):
    f32 = np.float32
    ct = lambda a: np.ascontiguousarray(a, dtype=f32)
    for c in range(8):
        b, r = c // 2, c % 2
        m = maps[c]
        sl = slice(r * HT, (r + 1) * HT)
        m["i_x2own2"] = resB[c]["d_x2_own"]
        m["i_lpa"] = ct(resC[2 * b]["d_longpart"][sl])
        m["i_lpb"] = ct(resC[2 * b + 1]["d_longpart"][sl])
        m["i_ms0"] = ct(resC[2 * b]["d_mem"][sl])
        m["i_ms1"] = ct(resC[2 * b + 1]["d_mem"][sl])
        m["i_glgm"] = ct(resC[c]["d_rows"][:, sl].T)
        m["w_mupT"] = ct(ins["L1_attn_mlp_up_w"].T)
        m["w_mupb"] = ct(ins["L1_attn_mlp_up_b"].reshape(32, 128).T)
        m["w_mdownT"] = ct(ins["L1_attn_mlp_down_w"].T)
        m["w_mdownb"] = ct(ins["L1_attn_mlp_down_b"].reshape(1, D))
        m["w_moT"] = ct(ins["L1_attn_mem_out_w"].T)


def add_launchE_inputs(maps, resD, ins):
    f32 = np.float32
    ct = lambda a: np.ascontiguousarray(a, dtype=f32)
    x3 = {}
    for b in range(4):
        x3[b] = np.concatenate([resD[2 * b]["d_x3_own"], resD[2 * b + 1]["d_x3_own"]], axis=0)
    for c in range(8):
        b, r = c // 2, c % 2
        m = maps[c]
        own = x3[b][r * HT:(r + 1) * HT]
        oth = x3[b][(1 - r) * HT:(2 - r) * HT]
        m["i_x3T"] = ct(np.concatenate([own, oth], axis=0).T)
        m["i_x3own"] = ct(own)
        m["w_f1gT"] = ct(ins["L1_ffn_gate_w"].T)
        m["w_f1w1"] = ct(ins["L1_ffn_w1"].reshape(8 * 1024, 4096))
        m["w_f1w2"] = ct(ins["L1_ffn_w2"].reshape(8 * 4096, 1024))


def add_launchB_inputs(maps, resA):
    for c in range(8):
        b, r = c // 2, c % 2
        pa = resA[2 * b]["d_projpart"][r * HT:(r + 1) * HT]
        pb = resA[2 * b + 1]["d_projpart"][r * HT:(r + 1) * HT]
        maps[c]["i_proj_a"] = np.ascontiguousarray(pa)
        maps[c]["i_proj_b"] = np.ascontiguousarray(pb)



class Banks:

    def __init__(self):
        self.readers = {}

    def wdeps(self, name):
        return self.readers.pop(name, [])

    def read(self, name, tok):
        self.readers.setdefault(name, []).append(tok)


class Build:
    def __init__(self, nc, stack, taps=()):
        self.nc = nc
        self.p = Prog(nc)
        self.st = stack
        self.taps = set(taps)
        self.C = {}
        self.bk = Banks()
        self.tap_pending = []

    @contextlib.contextmanager
    def scoped(self):
        outer = self.st
        with contextlib.ExitStack() as sst:
            self.st = sst
            try:
                yield
            finally:
                self.st = outer

    def din(self, name, shape, dtype=F32):
        return self.nc.dram_tensor(name, list(shape), dtype, kind="ExternalInput").ap()

    def dint(self, name, shape, dtype=F32):
        ap = self.nc.dram_tensor(name, list(shape), dtype, kind="Internal").ap()
        if name in self.taps:
            tap = self.nc.dram_tensor("tap_" + name, list(shape), dtype,
                                      kind="ExternalOutput").ap()
            self.tap_pending.append((tap, ap))
        return ap

    def dout(self, name, shape, dtype=F32):
        return self.nc.dram_tensor(name, list(shape), dtype, kind="ExternalOutput").ap()

    def sb(self, name, shape, dtype=F32):
        return self.st.enter_context(self.nc.sbuf_tensor(name, list(shape), dtype))

    def ps(self, name, shape, dtype=F32):
        return self.st.enter_context(self.nc.psum_tensor(name, list(shape), dtype))

    def load_consts(self):
        deps = []
        for n, (shp, dt) in CONST_SPECS.items():
            d = self.din(n, shp, dt)
            self.C[n] = self.sb("s" + n, shp, dt)
            deps.append(self.p.dma("sync", self.C[n][:], d))
        self.cdeps = deps
        self.ktile = self.sb("sk_consts", (128, 32), F32)
        self.kvals = {}

    def kcol(self, val):
        if val not in self.kvals:
            i = len(self.kvals)
            assert i < 32
            ap = self.ktile[:, i:i + 1]
            tok = self.p.memset(ap, float(val), tag=f"kcol{i}")
            self.kvals[val] = (ap, tok)
        return self.kvals[val]


def stage_S1(bd):
    p, C = bd.p, bd.C
    i_xT = bd.din("i_xT", (D, T))
    w_qT = bd.din("w_qT", (D, 512))
    w_kT = bd.din("w_kT", (D, 128))
    w_vT = bd.din("w_vT", (D, 128))
    w_oT = bd.din("w_oT", (512, D))
    d_projpart = bd.dout("d_projpart", (T, D))

    xT = [bd.sb(f"s1_xT{i}", (128, T), BF16) for i in range(8)]
    qp = [bd.sb(f"s1_qp{i}", (128, T), BF16) for i in range(4)]
    kaug = [bd.sb(f"s1_ka{i}", (65, T), BF16) for i in range(2)]
    vt = bd.sb("s1_vt", (128, 130 * 32), BF16)
    wq = bd.sb("s1_wq", (128, 8 * 512), BF16)
    wk = bd.sb("s1_wk", (128, 8 * 128), BF16)
    wv = bd.sb("s1_wv", (128, 8 * 128), BF16)
    wo = bd.sb("s1_wo", (128, 4 * 1024), BF16)
    qaug = bd.sb("s1_qaug", (65, 512), BF16)
    ptile = [bd.sb(f"s1_pt{i}", (128, 512), BF16) for i in range(2)]
    sq = bd.sb("s1_sq", (128, 512), BF16)
    mhat = bd.sb("s1_mhat", (1, 8), F32)
    mtmp = bd.sb("s1_mtmp", (1, 1), F32)
    augrow = bd.sb("s1_augrow", (1, 512), BF16)
    rrow = bd.sb("s1_rrow", (1, 512), F32)
    rb = bd.sb("s1_rb", (64, 512), F32)
    ao = [bd.sb(f"s1_ao{i}", (128, T), BF16) for i in range(4)]
    stg = [bd.sb(f"s1_stg{i}", (128, 1024), F32) for i in range(2)]

    psA = bd.ps("s1psA", (128, 512))
    psB = bd.ps("s1psB", (128, 512))
    psC = bd.ps("s1psC", (128, 512))
    psD = bd.ps("s1psD", (128, 512))

    dx = [p.dma("gpsimd", xT[i][:], i_xT[128 * i:128 * (i + 1), :]) for i in range(8)]
    dwq = [p.dma("gpsimd", wq[:, 512 * i:512 * (i + 1)], w_qT[128 * i:128 * (i + 1), :])
           for i in range(8)]
    dwk = [p.dma("gpsimd", wk[:, 128 * i:128 * (i + 1)], w_kT[128 * i:128 * (i + 1), :])
           for i in range(8)]
    dwv = [p.dma("gpsimd", wv[:, 128 * i:128 * (i + 1)], w_vT[128 * i:128 * (i + 1), :])
           for i in range(8)]
    dwo = [p.dma("gpsimd", wo[:, 1024 * i:1024 * (i + 1)], w_oT[128 * i:128 * (i + 1), :])
           for i in range(4)]

    tk1 = p.memset(kaug[0][64:65, :], 1.0)
    tk2 = p.memset(kaug[1][64:65, :], 1.0)
    tv0 = p.memset(vt[:, :], 0.0)
    tvones = [p.memset(vt[:, 130 * kc + cc:130 * kc + cc + 1], 1.0, deps=[tv0])
              for kc in range(32) for cc in (64, 129)]
    tmh = p.memset(mhat[:, :], 0.0)

    bk = bd.bk

    ev_q = {}
    mhat_tok = tmh
    sqname = "s1sq"
    for pr in range(4):
        for qc in range(8):
            ops = [(wq[:, 512 * dm + 128 * pr:512 * dm + 128 * (pr + 1)],
                    xT[dm][:, 512 * qc:512 * (qc + 1)], [dwq[dm], dx[dm]])
                   for dm in range(8)]
            ps_ = psA if (pr * 8 + qc) % 2 == 0 else psB
            tqm = p.mmgroup(ps_[:, :], ops, deps_first=bk.wdeps(ps_.name), tag=f"qproj{pr}_{qc}")
            te = p.act(qp[pr][:, 512 * qc:512 * (qc + 1)], ps_[:, :], AF.Copy,
                       deps=[tqm], tag=f"qev{pr}_{qc}")
            bk.read(ps_.name, te)
            tsq = p.act(sq[:, :], ps_[:, :], AF.Square,
                        deps=[tqm] + bk.wdeps(sqname), tag="qsq")
            bk.read(ps_.name, tsq)
            tm1 = p.mm(psD[0:1, :], C["c_ones_col_b"][:, :], sq[:, :], True, True,
                       deps=[tsq] + bk.wdeps("s1psD") + (bd.cdeps if pr == 0 and qc == 0 else []),
                       tag="qn")
            bk.read(sqname, tm1)
            tm2 = p.red(mtmp[:, :], psD[0:1, :], ALU.max, deps=[tm1, mhat_tok], tag="qmax")
            bk.read("s1psD", tm2)
            mhat_tok = p.tt(mhat[0:1, 0:1], mhat[0:1, 0:1], mtmp[:, :], ALU.max,
                            deps=[tm2], tag="qmaxacc")
            ev_q[(pr, qc)] = te

    ev_k = {}
    for qc in range(8):
        ops = [(wk[:, 128 * dm:128 * (dm + 1)], xT[dm][:, 512 * qc:512 * (qc + 1)],
                [dwk[dm], dx[dm]]) for dm in range(8)]
        ps_ = psA if qc % 2 == 0 else psB
        tkm = p.mmgroup(ps_[:, :], ops, deps_first=bk.wdeps(ps_.name), tag=f"kproj{qc}")
        te0 = p.act(kaug[0][0:64, 512 * qc:512 * (qc + 1)], ps_[0:64, :], AF.Copy,
                    deps=[tkm, tk1 if qc == 0 else None])
        te1 = p.act(kaug[1][0:64, 512 * qc:512 * (qc + 1)], ps_[64:128, :], AF.Copy,
                    deps=[tkm, tk2 if qc == 0 else None])
        bk.read(ps_.name, te0)
        bk.read(ps_.name, te1)
        tsq = p.act(sq[:, :], ps_[:, :], AF.Square, deps=[tkm] + bk.wdeps(sqname), tag="ksq")
        bk.read(ps_.name, tsq)
        tm1 = p.mm(psD[0:1, :], C["c_ones_col_b"][:, :], sq[:, :], True, True,
                   deps=[tsq] + bk.wdeps("s1psD"))
        bk.read(sqname, tm1)
        tm2 = p.red(mtmp[:, :], psD[0:1, :], ALU.max, deps=[tm1, mhat_tok])
        bk.read("s1psD", tm2)
        mhat_tok = p.tt(mhat[0:1, 1:2], mhat[0:1, 1:2], mtmp[:, :], ALU.max, deps=[tm2])
        ev_k[qc] = (te0, te1)

    vdeps = list(tvones)
    for tc in range(32):
        ops = [(xT[dm][:, 128 * tc:128 * (tc + 1)], wv[:, 128 * dm:128 * (dm + 1)],
                [dwv[dm], dx[dm]]) for dm in range(8)]
        tvm = p.mmgroup(psC[:, 0:128], ops,
                        deps_first=bk.wdeps("s1psC") + ([tv0] if tc == 0 else []),
                        tag=f"vproj{tc}")
        tva = p.act(vt[:, 130 * tc:130 * tc + 64], psC[:, 0:64], AF.Copy, deps=[tvm])
        tvb = p.act(vt[:, 130 * tc + 65:130 * tc + 129], psC[:, 64:128], AF.Copy, deps=[tvm])
        bk.read("s1psC", tva)
        bk.read("s1psC", tvb)
        vdeps += [tva, tvb]

    t1 = p.tt(mhat[0:1, 2:3], mhat[0:1, 0:1], mhat[0:1, 1:2], ALU.mult, deps=[mhat_tok])
    t2 = p.act(mhat[0:1, 3:4], mhat[0:1, 2:3], AF.Sqrt, deps=[t1])
    t3 = p.ts(mhat[0:1, 4:5], mhat[0:1, 3:4], -1.0, None, ALU.mult, deps=[t2])
    t4 = p.memset(augrow[:, :], 1.0)
    t5 = p.ts(augrow[:, :], augrow[:, :], mhat[0:1, 4:5], None, ALU.mult, deps=[t3, t4])
    taug = p.vcopy(qaug[64:65, 0:512], augrow[:, :], deps=[t5])

    first_att = True
    tpv = None
    for h in range(8):
        pr, lo = h // 2, (h % 2) * 64
        kv = h // 4
        for qc in range(8):
            tq = p.vcopy(qaug[0:64, :], qp[pr][lo:lo + 64, 512 * qc:512 * (qc + 1)],
                         deps=[ev_q[(pr, qc)], taug] + bk.wdeps("qaug"), tag=f"qa{h}_{qc}")
            nkc = 4 * qc + 4
            for kc in range(nkc):
                ps_ = psA if kc % 2 == 0 else psB
                diag = kc >= 4 * qc
                sdeps = [tq, ev_k[kc // 4][kv]] + bk.wdeps(ps_.name)
                tsm = p.mm(ps_[:, :], kaug[kv][:, 128 * kc:128 * (kc + 1)], qaug[:, :],
                           True, not diag, deps=sdeps, tag=f"att_s{h}_{qc}_{kc}")
                if diag:
                    tsm = p.mm(ps_[:, :], C["c_ident"][:, :],
                               C["c_attnmask"][:, 512 * (kc - 4 * qc):512 * (kc - 4 * qc + 1)],
                               False, True, deps=[], tag=f"att_m{h}_{qc}_{kc}")
                pt = ptile[kc % 2]
                ptn = f"s1pt{kc % 2}"
                texp = p.act(pt[:, :], ps_[:, :], AF.Exp, scale=0.125,
                             deps=[tsm] + bk.wdeps(ptn), tag=f"exp{h}_{qc}_{kc}")
                bk.read(ps_.name, texp)
                bk.read("qaug", texp)
                pvdeps = [texp]
                if kc == 0:
                    pvdeps += bk.wdeps("s1psC")
                    if first_att:
                        pvdeps += vdeps
                        first_att = False
                tpv = p.mm(psC[0:65, :], vt[:, 130 * kc + 65 * kv:130 * kc + 65 * kv + 65],
                           pt[:, :], kc == 0, kc == nkc - 1, deps=pvdeps,
                           tag=f"pv{h}_{qc}_{kc}")
                bk.read(ptn, tpv)
            trc = p.recip(rrow[:, :], psC[64:65, :], deps=[tpv] + bk.wdeps("rrow"),
                          tag=f"rc{h}_{qc}")
            bk.read("s1psC", trc)
            tbc = p.mm(psD[0:64, :], C["c_ones_row_f"][0:1, 0:64], rrow[:, :], True, True,
                       deps=[trc] + bk.wdeps("s1psD"), tag=f"bc{h}_{qc}")
            bk.read("rrow", tbc)
            tev = p.vcopy(rb[:, :], psD[0:64, :], deps=[tbc] + bk.wdeps("rb"))
            bk.read("s1psD", tev)
            tao = p.tt(ao[pr][lo:lo + 64, 512 * qc:512 * (qc + 1)], psC[0:64, :],
                       rb[:, :], ALU.mult, deps=[tev], tag=f"aoev{h}_{qc}")
            bk.read("s1psC", tao)
            bk.read("rb", tao)
            bk.read("qaug", tao)

    ao_done = [("vector", p.counts["vector"])]
    for tc in range(32):
        sg = stg[tc % 2]
        sgev = []
        for half in range(2):
            ps_ = psA if half == 0 else psB
            ops = [(ao[pr][:, 128 * tc:128 * (tc + 1)],
                    wo[:, 1024 * pr + 512 * half:1024 * pr + 512 * (half + 1)],
                    [dwo[pr]]) for pr in range(4)]
            tpm = p.mmgroup(ps_[:, :], ops,
                            deps_first=bk.wdeps(ps_.name) + (ao_done if tc == 0 else []),
                            tag=f"oproj{tc}_{half}")
            tse = p.scopy(sg[:, 512 * half:512 * (half + 1)], ps_[:, :],
                          deps=[tpm] + bk.wdeps((f"stg{tc % 2}", half)))
            bk.read(ps_.name, tse)
            sgev.append(tse)
        tdo = p.dma("sync", d_projpart[128 * tc:128 * (tc + 1), :], sg[:, :],
                    deps=sgev, tag=f"projout{tc}")
        bk.read((f"stg{tc % 2}", 0), tdo)
        bk.read((f"stg{tc % 2}", 1), tdo)
    bd.s1_out = d_projpart
    bd.s1_done = p.all_fences()


def stage_S2(bd):
    p, C = bd.p, bd.C
    i_pa = bd.din("i_proj_a", (HT, D))
    i_pb = bd.din("i_proj_b", (HT, D))
    i_x_own = bd.din("i_x_own", (HT, D))
    d_x1_own = bd.dint("d_x1_own", (HT, D))

    pa = [bd.sb(f"s2_pa{i}", (128, 1024), F32) for i in range(2)]
    pb = [bd.sb(f"s2_pb{i}", (128, 1024), F32) for i in range(2)]
    xo = [bd.sb(f"s2_xo{i}", (128, 1024), F32) for i in range(2)]
    s_ = bd.sb("s2_s", (128, 1024), F32)
    sqt = bd.sb("s2_sqt", (128, 1024), BF16)
    x1f = bd.sb("s2_x1f", (128, 1024), F32)
    colm = bd.sb("s2_colm", (128, 4), F32)
    x1T = bd.x1T
    x1tm = bd.sb("s2_x1tm", (128, 1024), BF16)
    psT = bd.ps("s2psT", (128, 512), BF16)
    kb, kbt = bd.kcol(1e-5)

    done = {}
    trtoks = []
    for tc in range(NTC):
        i = tc % 2
        dpa = p.dma("sync", pa[i][:, :], i_pa[128 * tc:128 * (tc + 1), :],
                    deps=[done.get(("pa", i))])
        dpb = p.dma("sync", pb[i][:, :], i_pb[128 * tc:128 * (tc + 1), :],
                    deps=[done.get(("pa", i))])
        dxo = p.dma("sync", xo[i][:, :], i_x_own[128 * tc:128 * (tc + 1), :],
                    deps=[done.get(("pa", i))])
        t0 = p.tt(s_[:, :], pa[i][:, :], pb[i][:, :], ALU.add,
                  deps=[dpa, dpb, done.get("s")])
        t1 = p.tt(s_[:, :], s_[:, :], xo[i][:, :], ALU.add, deps=[dxo, t0])
        t2 = p.red(colm[:, 0:1], s_[:, :], ALU.add, deps=[t1], negate=True)
        t3 = p.ts(colm[:, 0:1], colm[:, 0:1], 1.0 / D, None, ALU.mult, deps=[t2])
        t4 = p.act(sqt[:, :], s_[:, :], AF.Square, bias=colm[:, 0:1], accum_out=colm[:, 1:2],
                   deps=[t3, done.get("sq")])
        t5 = p.act(colm[:, 2:3], colm[:, 1:2], AF.Ln, bias=kb, scale=1.0 / D,
                   deps=[t4, kbt])
        t6 = p.act(colm[:, 2:3], colm[:, 2:3], AF.Exp, scale=-0.5, deps=[t5])
        t7 = p.ts(x1f[:, :], s_[:, :], colm[:, 0:1], colm[:, 2:3], ALU.add, ALU.mult,
                  deps=[t6, done.get("x1f")])
        t8 = p.vcopy(x1tm[:, :], x1f[:, :], deps=[t7, done.get("x1tm")])
        d1 = p.dma("sync", d_x1_own[128 * tc:128 * (tc + 1), :], x1f[:, :], deps=[t7])
        trt = []
        for dm in range(8):
            tt1 = p.tr(psT[:, 0:128], x1tm[:, 128 * dm:128 * (dm + 1)], C["c_ident"][:, :],
                       deps=[t8] + bd.bk.wdeps("s2psT"))
            tt2 = p.vcopy(x1T[dm][:, 128 * tc:128 * (tc + 1)], psT[:, 0:128], deps=[tt1])
            bd.bk.read("s2psT", tt2)
            trt.append(tt2)
        done[("pa", i)] = t1
        done["s"] = t7
        done["sq"] = t5
        done["x1f"] = d1
        done["x1tm"] = trt[-1]
        trtoks += trt
    bd.d_x1_own = d_x1_own
    bd.x1T = x1T
    bd.x1T_toks = trtoks


def stage_S3(bd):
    p, C = bd.p, bd.C
    w1d = bd.din("w_f0w1", (8 * 1024, 2048))
    w2d = bd.din("w_f0w2", (8 * 2048, 1024))
    b1d = bd.din("w_f0b1T", (2048, 8))
    x1T = bd.x1T
    yacc = bd.yacc
    bk = bd.bk

    w1sb = bd.sb("s3_w1", (128, 16384), BF16)
    w2sb = bd.sb("s3_w2", (128, 16384), BF16)
    hsb = bd.sb("s3_h", (128, 8192), BF16)
    b1sb = bd.sb("s3_b1", (128, 128), F32)
    psH = [bd.ps("s3psH0", (128, 512)), bd.ps("s3psH1", (128, 512))]
    psY = [bd.ps("s3psY0", (128, 512)), bd.ps("s3psY1", (128, 512))]

    tz = p.step("vector", lambda v: v.memset(yacc[:, :], 0.0), tag="yzero")
    db1 = p.dma("sync", b1sb[:, :].rearrange("p (c e) -> p c e", e=8),
                b1d.rearrange("(c p) e -> p c e", p=128), tag="b1")
    ydep = {"last": tz}
    for e in range(8):
        wfence = [p.fence("tensor")]
        dw1 = [p.dma("gpsimd", w1sb[:, 2048 * dm:2048 * (dm + 1)],
                     w1d[1024 * e + 128 * dm:1024 * e + 128 * (dm + 1), :],
                     deps=wfence if dm == 0 else (), tag=f"w1_{e}")
               for dm in range(8)]
        dw2 = [p.dma("gpsimd", w2sb[:, 1024 * hc:1024 * (hc + 1)],
                     w2d[2048 * e + 128 * hc:2048 * e + 128 * (hc + 1), :],
                     deps=wfence if hc == 0 else (), tag=f"w2_{e}")
               for hc in range(16)]
        for tg in range(4):
            prev_pe = [p.fence("tensor")]
            for hc in range(16):
                ps_ = psH[hc % 2]
                ops = [(w1sb[:, 2048 * dm + 128 * hc:2048 * dm + 128 * (hc + 1)],
                        x1T[dm][:, 512 * tg:512 * (tg + 1)],
                        [dw1[dm]] if tg == 0 else ()) for dm in range(8)]
                th = p.mmgroup(ps_[:, :], ops, deps_first=bk.wdeps(ps_.name),
                               tag=f"f0h{e}_{tg}_{hc}")
                tg_ = p.act(hsb[:, 512 * hc:512 * (hc + 1)], ps_[:, :], AF.Gelu,
                            bias=b1sb[:, 8 * hc + e:8 * hc + e + 1],
                            deps=[th, db1] + prev_pe, tag=f"f0g{e}_{tg}_{hc}")
                bk.read(ps_.name, tg_)
            for sub in range(4):
                for half in range(2):
                    ps_ = psY[(sub * 2 + half) % 2]
                    ops = [(hsb[:, 512 * hc + 128 * sub:512 * hc + 128 * (sub + 1)],
                            w2sb[:, 1024 * hc + 512 * half:1024 * hc + 512 * (half + 1)],
                            [dw2[hc]] if tg == 0 and e == 0 else ()) for hc in range(16)]
                    ty = p.mmgroup(ps_[:, :], ops,
                                   deps_first=bk.wdeps(ps_.name) + [p.fence("scalar")],
                                   tag=f"f0y{e}_{tg}_{sub}_{half}")
                    ysl = yacc[:, 1024 * (4 * tg + sub) + 512 * half:
                               1024 * (4 * tg + sub) + 512 * (half + 1)]
                    ta = p.tt(ysl, ysl, ps_[:, :], ALU.add, deps=[ty, ydep["last"]],
                              tag=f"f0a{e}_{tg}_{sub}_{half}")
                    bk.read(ps_.name, ta)
                    ydep["last"] = ta


def stage_S4(bd):
    p, C = bd.p, bd.C
    gTd = bd.din("w_f0gateT", (1024, 8))
    gbd = bd.din("w_f0gateb", (1, 8))
    rwd = bd.din("w_f0rmsw", (1, 8))
    b2d = bd.din("w_f0b2", (8, 1024))
    d_x2_own = bd.dout("d_x2_own", (HT, D))
    x1T, yacc = bd.x1T, bd.yacc
    bk = bd.bk

    gsb = bd.sb("s4_g", (128, 64), BF16)
    gbr = bd.sb("s4_gb", (1, 8), F32)
    rwr = bd.sb("s4_rw", (1, 8), F32)
    b2sb = bd.sb("s4_b2", (8, 1024), F32)
    ones8 = bd.sb("s4_on8", (8, 1), F32)
    b2row = bd.sb("s4_b2r", (1, 1024), F32)
    rwbc = bd.sb("s4_rwbc", (128, 8), F32)
    lg = bd.sb("s4_lg", (128, 8), F32)
    e2 = bd.sb("s4_e2", (128, 8), F32)
    cols = bd.sb("s4_cols", (128, 8), F32)
    x1c = [bd.sb(f"s4_x1c{i}", (128, 1024), F32) for i in range(2)]
    yb = bd.sb("s4_yb", (128, 1024), F32)
    sqt = bd.sb("s4_sqt", (128, 1024), BF16)
    x2f = bd.sb("s4_x2f", (128, 1024), F32)
    psA = bd.ps("s4psA", (128, 512))
    psB = bd.ps("s4psB", (128, 512))
    keps, kepst = bd.kcol(F32EPS)
    kb5, kb5t = bd.kcol(1e-5)

    dg = [p.dma("gpsimd", gsb[:, 8 * dm:8 * (dm + 1)], gTd[128 * dm:128 * (dm + 1), :])
          for dm in range(8)]
    dgb = p.dma("sync", gbr[:, :], gbd)
    drw = p.dma("sync", rwr[:, :], rwd)
    db2 = p.dma("sync", b2sb[:, :], b2d)
    t1s = p.memset(ones8[:, :], 1.0)
    tb1 = p.mm(psA[0:128, 0:8], C["c_ones_row_f"][:, :], rwr[:, :], True, True,
               deps=[drw] + bd.cdeps, tag="rwbc")
    tb2 = p.vcopy(rwbc[:, :], psA[0:128, 0:8], deps=[tb1])
    bk.read("s4psA", tb2)
    tb3 = p.mm(psB[0:1, 0:512], ones8[:, :], b2sb[:, 0:512], True, True,
               deps=[db2, t1s] + bk.wdeps("s4psB"), tag="b2r0")
    tb5 = p.vcopy(b2row[:, 0:512], psB[0:1, 0:512], deps=[tb3])
    tb4 = p.mm(psB[0:1, 0:512], ones8[:, :], b2sb[:, 512:1024], True, True,
               deps=[tb5], tag="b2r1")
    tb6 = p.vcopy(b2row[:, 512:1024], psB[0:1, 0:512], deps=[tb4])
    bk.read("s4psB", tb6)

    done = {}
    for tc in range(NTC):
        i = tc % 2
        dx1 = p.dma("sync", x1c[i][:, :], bd.d_x1_own[128 * tc:128 * (tc + 1), :],
                    deps=[done.get(("x1", i))])
        ops = [(x1T[dm][:, 128 * tc:128 * (tc + 1)], gsb[:, 8 * dm:8 * (dm + 1)],
                [dg[dm]] if tc == 0 else ()) for dm in range(8)]
        tm = p.mmgroup(psA[0:128, 0:8], ops, deps_first=bk.wdeps("s4psA"), tag=f"lg{tc}")
        tm2 = p.mm(psA[0:128, 0:8], C["c_ones_row_f"][0:1, :], gbr[:, :], False, True,
                   deps=[dgb], tag=f"lgb{tc}")
        te = p.vcopy(lg[:, :], psA[0:128, 0:8], deps=[tm2] + bk.wdeps("s4lg"))
        bk.read("s4psA", te)
        t4 = p.act(sqt[:, 0:8], lg[:, :], AF.Square, accum_out=cols[:, 0:1],
                   deps=[te, done.get("sq8")], tag=f"sq8{tc}")
        t5 = p.act(cols[:, 1:2], cols[:, 0:1], AF.Ln, bias=keps, scale=1.0 / 8,
                   deps=[t4, kepst])
        t6 = p.act(cols[:, 1:2], cols[:, 1:2], AF.Exp, scale=-0.5, deps=[t5])
        t7 = p.ts(lg[:, :], lg[:, :], cols[:, 1:2], None, ALU.mult, deps=[t6])
        bk.read("s4lg", t7)
        t8 = p.tt(lg[:, :], lg[:, :], rwbc[:, :], ALU.mult, deps=[t7, tb2])
        t9 = p.red(cols[:, 2:3], lg[:, :], ALU.max, deps=[t8])
        t10 = p.ts(cols[:, 2:3], cols[:, 2:3], -2.0, None, ALU.mult, deps=[t9])
        t11 = p.act(e2[:, :], lg[:, :], AF.Exp, bias=cols[:, 2:3], scale=2.0,
                    accum_out=cols[:, 3:4], deps=[t10, done.get("e2"), done.get("ng")],
                    tag=f"sm{tc}")
        done["e2"] = t11
        bk.read("s4lg", t11)
        t12 = p.ts(cols[:, 3:4], cols[:, 3:4], 1e-6, 1.0, ALU.mult, ALU.add, deps=[t11])
        t13 = p.recip(cols[:, 3:4], cols[:, 3:4], deps=[t12])
        ysl = yacc[:, 1024 * tc:1024 * (tc + 1)]
        h0 = p.mm(psA[:, 0:512], C["c_ones_row_f"][0:1, :], b2row[:, 0:512], True, True,
                  deps=[tb5] + bk.wdeps("s4psA"), tag=f"b2bc0_{tc}")
        h1 = p.mm(psB[:, 0:512], C["c_ones_row_f"][0:1, :], b2row[:, 512:1024], True, True,
                  deps=[tb6] + bk.wdeps("s4psB"), tag=f"b2bc1_{tc}")
        ta0 = p.tt(yb[:, 0:512], ysl[:, 0:512], psA[:, 0:512], ALU.add,
                   deps=[h0, done.get("yb")])
        ta1 = p.tt(yb[:, 512:1024], ysl[:, 512:1024], psB[:, 0:512], ALU.add, deps=[h1])
        bk.read("s4psA", ta0)
        bk.read("s4psB", ta1)
        ts_ = p.stt(yb[:, :], yb[:, :], cols[:, 3:4], x1c[i][:, :], ALU.mult, ALU.add,
                    deps=[ta0, ta1, t13, dx1], tag=f"comb{tc}")
        done[("x1", i)] = ts_
        done["yb"] = ts_
        done["ng"] = ts_
        l2 = p.red(cols[:, 4:5], yb[:, :], ALU.add, deps=[ts_, done.get("lncols")],
                   negate=True)
        l3 = p.ts(cols[:, 4:5], cols[:, 4:5], 1.0 / D, None, ALU.mult, deps=[l2])
        l4 = p.act(sqt[:, :], yb[:, :], AF.Square, bias=cols[:, 4:5],
                   accum_out=cols[:, 5:6], deps=[l3, done.get("sqt")])
        done["sqt"] = l4
        done["sq8"] = l4
        l5 = p.act(cols[:, 5:6], cols[:, 5:6], AF.Ln, bias=kb5, scale=1.0 / D,
                   deps=[l4, kb5t])
        l6 = p.act(cols[:, 5:6], cols[:, 5:6], AF.Exp, scale=-0.5, deps=[l5])
        l7 = p.ts(x2f[:, :], yb[:, :], cols[:, 4:5], cols[:, 5:6], ALU.add, ALU.mult,
                  deps=[l6, done.get("x2f")], tag=f"ln2_{tc}")
        done["lncols"] = l7
        d2 = p.dma("sync", d_x2_own[128 * tc:128 * (tc + 1), :], x2f[:, :], deps=[l7])
        done["x2f"] = d2


def stage_Ca(bd):
    p, C = bd.p, bd.C
    i_x2T = bd.din("i_x2T", (D, T))
    w_conv = bd.din("w_conv", (128, 32))
    w_gT = bd.din("w_l1gT", (D, D))
    w_gb = bd.din("w_l1gb", (128, 8))
    uT = bd.uT
    bk = bd.bk
    keps, kepst = bd.kcol(1e-6)

    xb = [bd.sb(f"ca_xb{i}", (128, 512), BF16) for i in range(2)]
    xnp = [bd.sb(f"ca_xnp{i}", (128, T + 4), BF16) for i in range(8)]
    gwT = bd.sb("ca_gwT", (128, 8 * 1024), BF16)
    gbc = bd.sb("ca_gbc", (128, 8), F32)
    cwc = bd.sb("ca_cw", (128, 32), F32)
    sq = bd.sb("ca_sq", (128, 512), BF16)
    rrow = bd.sb("ca_rrow", (1, T), F32)
    gsig = bd.sb("ca_gsig", (128, 512), BF16)
    cvf = bd.sb("ca_cvf", (128, 512), F32)
    cvs = bd.sb("ca_cvs", (128, 512), BF16)
    psA = bd.ps("capsA", (128, 512))
    psB = bd.ps("capsB", (128, 512))
    psR = bd.ps("capsR", (1, 512))

    dgw = [p.dma("gpsimd", gwT[:, 1024 * i:1024 * (i + 1)], w_gT[128 * i:128 * (i + 1), :])
           for i in range(8)]
    dgb = p.dma("sync", gbc[:, :], w_gb)
    dcw = p.dma("sync", cwc[:, :], w_conv)
    tpad = [p.memset(xnp[i][:, 0:4], 0.0) for i in range(8)]

    for tc in range(8):
        mmlast = None
        for dm in range(8):
            ib = dm % 2
            dx = p.dma("gpsimd", xb[ib][:, :],
                       i_x2T[128 * dm:128 * (dm + 1), 512 * tc:512 * (tc + 1)],
                       deps=bk.wdeps(("ca_xb", ib)), tag=f"x2p1_{tc}_{dm}")
            tq = p.act(sq[:, :], xb[ib][:, :], AF.Square,
                       deps=[dx] + bk.wdeps("ca_sq") + (bd.cdeps if dm == 0 and tc == 0 else []),
                       tag=f"sq{tc}_{dm}")
            bk.read(("ca_xb", ib), tq)
            mmlast = p.mm(psR[:, :], C["c_ones_col_b"][:, :], sq[:, :], dm == 0, dm == 7,
                          deps=[tq] + (bk.wdeps("capsR") if dm == 0 else []),
                          tag=f"ssq{tc}_{dm}")
            bk.read("ca_sq", mmlast)
        t1 = p.act(rrow[:, 512 * tc:512 * (tc + 1)], psR[:, :], AF.Ln, bias=keps[0:1, :],
                   scale=1.0 / D, deps=[mmlast, kepst])
        bk.read("capsR", t1)
        t2 = p.act(rrow[:, 512 * tc:512 * (tc + 1)], rrow[:, 512 * tc:512 * (tc + 1)],
                   AF.Exp, scale=-0.5, deps=[t1], tag=f"rstd{tc}")
    rstd_done = p.fence("scalar")

    for dm in range(8):
        for tc in range(8):
            ib = (dm * 8 + tc) % 2
            dx = p.dma("gpsimd", xb[ib][:, :],
                       i_x2T[128 * dm:128 * (dm + 1), 512 * tc:512 * (tc + 1)],
                       deps=bk.wdeps(("ca_xb", ib)), tag=f"x2p2_{dm}_{tc}")
            ps_ = psA if tc % 2 == 0 else psB
            tb = p.mm(ps_[:, :], C["c_ones_row_f"][:, :],
                      rrow[0:1, 512 * tc:512 * (tc + 1)], True, True,
                      deps=[rstd_done] + bk.wdeps(ps_.name), tag=f"rbc{dm}_{tc}")
            tx = p.tt(xnp[dm][:, 4 + 512 * tc:4 + 512 * (tc + 1)],
                      xb[ib][:, :], ps_[:, :], ALU.mult,
                      deps=[tb, dx], tag=f"xn{dm}_{tc}")
            bk.read(ps_.name, tx)
            bk.read(("ca_xb", ib), tx)

    xnp_done = [p.fence("vector")] + tpad
    for dm in range(8):
        for tc in range(8):
            gdeps = xnp_done
            ps_ = psA if tc % 2 == 0 else psB
            ops = [(gwT[:, 1024 * dmi + 128 * dm:1024 * dmi + 128 * (dm + 1)],
                    xnp[dmi][:, 4 + 512 * tc:4 + 512 * (tc + 1)],
                    [dgw[dmi]] if tc == 0 else ()) for dmi in range(8)]
            tgm = p.mmgroup(ps_[:, :], ops, deps_first=bk.wdeps(ps_.name) + gdeps,
                            tag=f"gate{dm}_{tc}")
            tsg = p.act(gsig[:, :], ps_[:, :], AF.Sigmoid, bias=gbc[:, dm:dm + 1],
                        deps=[tgm, dgb] + bk.wdeps("ca_gsig"), tag=f"sig{dm}_{tc}")
            bk.read(ps_.name, tsg)
            base = 4 + 512 * tc
            tcv = p.ts(cvf[:, :], xnp[dm][:, base - 3:base - 3 + 512], cwc[:, 4 * dm:4 * dm + 1],
                       None, ALU.mult, deps=[dcw] + gdeps + bk.wdeps("ca_cvf"),
                       tag=f"cv0_{dm}_{tc}")
            for j in range(1, 4):
                tcv = p.stt(cvf[:, :], xnp[dm][:, base - 3 + j:base - 3 + j + 512],
                            cwc[:, 4 * dm + j:4 * dm + j + 1], cvf[:, :], ALU.mult, ALU.add,
                            deps=[tcv], tag=f"cv{j}_{dm}_{tc}")
            tsl = p.act(cvs[:, :], cvf[:, :], AF.Silu, deps=[tcv] + bk.wdeps("ca_cvs"))
            bk.read("ca_cvf", tsl)
            tu = p.tt(uT[dm][:, 512 * tc:512 * (tc + 1)], cvs[:, :], gsig[:, :], ALU.mult,
                      deps=[tsl, tsg], tag=f"u{dm}_{tc}")
            bk.read("ca_cvs", tu)
            bk.read("ca_gsig", tu)


def stage_Cb(bd):
    p, C = bd.p, bd.C
    w_keysT = bd.din("w_keysT", (D, 96))
    w_bitsT = bd.din("w_bitsT", (D, 36))
    w_powR = bd.din("w_powR", (36, 1))
    w_powW = bd.din("w_powW", (36, 1))
    w_g3T = bd.din("w_g3T", (D, 3))
    w_g3b = bd.din("w_g3b", (1, 3))
    d_rows = bd.dout("d_rows", (2, T))
    d_wgl = bd.dint("d_wgl", (2, T))
    uT, bk = bd.uT, bd.bk

    kT = bd.sb("cb_kT", (128, 8 * 96), BF16)
    bT = bd.sb("cb_bT", (128, 8 * 36), BF16)
    g3 = bd.sb("cb_g3", (128, 8 * 3), BF16)
    powR = bd.sb("cb_powR", (36, 1), BF16)
    powW = bd.sb("cb_powW", (36, 1), BF16)
    g3b = bd.sb("cb_g3b", (1, 3), F32)
    bits01 = bd.sb("cb_b01", (36, 512), BF16)
    sg3 = bd.sb("cb_sg3", (128, 512), F32)
    zrow = bd.sb("cb_zr", (1, 512), F32)
    lrow = bd.sb("cb_lr", (1, 512), F32)
    trow = bd.sb("cb_tr", (1, 512), F32)
    psA = bd.ps("cbpsA", (128, 512))
    psB = bd.ps("cbpsB", (128, 512))

    dk = [p.dma("gpsimd", kT[:, 96 * i:96 * (i + 1)], w_keysT[128 * i:128 * (i + 1), :])
          for i in range(8)]
    db = [p.dma("gpsimd", bT[:, 36 * i:36 * (i + 1)], w_bitsT[128 * i:128 * (i + 1), :])
          for i in range(8)]
    dg3 = [p.dma("gpsimd", g3[:, 3 * i:3 * (i + 1)], w_g3T[128 * i:128 * (i + 1), :])
           for i in range(8)]
    dpR = p.dma("gpsimd", powR[:, :], w_powR)
    dpW = p.dma("gpsimd", powW[:, :], w_powW)
    dgb = p.dma("sync", g3b[:, :], w_g3b)

    for tc in range(8):
        blk = slice(512 * tc, 512 * (tc + 1))
        ops = [(kT[:, 96 * dm:96 * (dm + 1)], uT[dm][:, blk], [dk[dm]]) for dm in range(8)]
        tk = p.mmgroup(psA[0:96, :], ops, deps_first=bk.wdeps("cbpsA") + [p.fence("vector")],
                       tag=f"keys{tc}")
        e1 = p.act(bd.qkr[:, blk], psA[0:32, :], AF.Copy, scale=float(1.0 / np.sqrt(32.0)),
                   deps=[tk])
        e2 = p.scopy(bd.wkr[:, blk], psA[32:64, :], deps=[tk])
        e3 = p.scopy(bd.wvr[:, blk], psA[64:96, :], deps=[tk])
        for t_ in (e1, e2, e3):
            bk.read("cbpsA", t_)
        ops = [(bT[:, 36 * dm:36 * (dm + 1)], uT[dm][:, blk], [db[dm]]) for dm in range(8)]
        tb = p.mmgroup(psB[0:36, :], ops, deps_first=bk.wdeps("cbpsB"), tag=f"bits{tc}")
        t01 = p.ts(bits01[:, :], psB[0:36, :], 0.0, None, ALU.is_gt,
                   deps=[tb] + bk.wdeps("cb_b01"), tag=f"b01_{tc}")
        bk.read("cbpsB", t01)
        tR = p.mm(psA[0:1, :], powR[:, :], bits01[:, :], True, True,
                  deps=[dpR, t01] + bk.wdeps("cbpsA"), tag=f"ri{tc}")
        eR = p.vcopy(zrow[:, :], psA[0:1, :], deps=[tR] + bk.wdeps("cb_zr"))
        bk.read("cbpsA", eR)
        dR = p.dma("sync", bd.d_idx[0:1, blk], zrow[:, :], deps=[eR])
        tW = p.mm(psB[0:1, :], powW[:, :], bits01[:, :], True, True,
                  deps=[dpW] + bk.wdeps("cbpsB"), tag=f"wi{tc}")
        eW = p.vcopy(trow[:, :], psB[0:1, :], deps=[tW] + bk.wdeps("cb_tr"))
        bk.read("cbpsB", eW)
        dW = p.dma("sync", bd.d_idx[1:2, blk], trow[:, :], deps=[eW])
        bk.read("cb_zr", dR)
        bk.read("cb_tr", dW)
        bk.read("cb_b01", tW)
        ops = [(g3[:, 3 * dm:3 * (dm + 1)], uT[dm][:, blk], [dg3[dm]]) for dm in range(8)]
        tg3 = p.mmgroup(psA[0:3, :], ops, deps_first=bk.wdeps("cbpsA"), tag=f"g3{tc}")
        tg3b = p.mm(psA[0:3, :], g3b[:, :], C["c_ones_row512f"][:, :], False, True,
                    deps=[dgb], tag=f"g3b{tc}")
        tz = p.vcopy(zrow[:, :], psA[0:1, :], deps=[tg3b] + bk.wdeps("cb_zr"))
        tsg = p.act(sg3[0:3, :], psA[0:3, :], AF.Sigmoid,
                    deps=[tg3b] + bk.wdeps("cb_sg3"), tag=f"sg3{tc}")
        bk.read("cbpsA", tz)
        bk.read("cbpsA", tsg)
        dw1 = p.dma("sync", d_wgl[0:1, blk], sg3[0:1, :], deps=[tsg])
        dr1 = p.dma("sync", d_rows[0:1, blk], sg3[1:2, :], deps=[tsg])
        dr2 = p.dma("sync", d_rows[1:2, blk], sg3[2:3, :], deps=[tsg])
        bk.read("cb_sg3", dw1)
        bk.read("cb_sg3", dr1)
        bk.read("cb_sg3", dr2)
        l1 = p.act(lrow[:, :], zrow[:, :], AF.Abs, deps=[tz] + bk.wdeps("cb_lr"))
        l2 = p.act(lrow[:, :], lrow[:, :], AF.Exp, scale=-1.0, deps=[l1])
        kb1, kb1t = bd.kcol(1.0)
        l3 = p.act(lrow[:, :], lrow[:, :], AF.Ln, bias=kb1[0:1, :], deps=[l2, kb1t])
        l4 = p.ts(trow[:, :], zrow[:, :], 0.0, None, ALU.max, deps=[tz] + bk.wdeps("cb_tr"))
        bk.read("cb_zr", l4)
        l5 = p.tt(lrow[:, :], lrow[:, :], trow[:, :], ALU.add, deps=[l3, l4])
        bk.read("cb_tr", l5)
        l6 = p.ts(lrow[:, :], lrow[:, :], -1.0, -30.0, ALU.mult, ALU.max, deps=[l5])
        dw2 = p.dma("sync", d_wgl[1:2, blk], lrow[:, :], deps=[l6])
        bk.read("cb_lr", dw2)
    bd.d_rows = d_rows
    bd.d_wgl = d_wgl


def stage_Cc(bd):
    p, C = bd.p, bd.C
    w_sinT = bd.din("w_sinT", (D, 2048))
    w_soT = bd.din("w_soT", (2048, D))
    w_dec = bd.din("w_mydecay", (1, 2))
    d_s = bd.dint("d_s", (2048, T), BF16)
    d_longpart = bd.dout("d_longpart", (T, D))
    uT, bk = bd.uT, bd.bk

    sinb = [bd.sb(f"cc_sin{i}", (128, 1024), BF16) for i in range(2)]
    sob = bd.sb("cc_sob", (128, 16 * 1024), BF16)
    inpf = bd.sb("cc_inpf", (128, T), F32)
    sbf = bd.sb("cc_sbf", (128, T), BF16)
    sf = bd.sb("cc_sf", (128, T), F32)
    decrow = bd.sb("cc_dec", (1, 2), F32)
    dcol = bd.sb("cc_dcol", (128, 2), F32)
    dtile = [bd.sb(f"cc_dt{i}", (128, 512), F32) for i in range(2)]
    sblk = bd.sb("cc_sblk", (128, 16 * 512), BF16)
    stg = [bd.sb(f"cc_stg{i}", (128, 1024), F32) for i in range(2)]
    psA = bd.ps("ccpsA", (128, 512))
    psB = bd.ps("ccpsB", (128, 512))
    psC = bd.ps("ccpsC", (128, 512))

    dd = p.dma("sync", decrow[:, :], w_dec)
    t0 = p.act(decrow[:, :], decrow[:, :], AF.Sigmoid, deps=[dd])
    for kl in range(2):
        tb = p.mm(psC[:, 0:1], C["c_ones_row_f"][:, :], decrow[0:1, kl:kl + 1], True, True,
                  deps=[t0] + bk.wdeps("ccpsC"), tag=f"decbc{kl}")
        tc_ = p.vcopy(dcol[:, kl:kl + 1], psC[:, 0:1], deps=[tb])
        bk.read("ccpsC", tc_)
        tm = p.memset(dtile[kl][:, :], 1.0)
        tt_ = p.ts(dtile[kl][:, :], dtile[kl][:, :], dcol[:, kl:kl + 1], None, ALU.mult,
                   deps=[tc_, tm], tag=f"dtile{kl}")
        bk.read(f"cc_dt{kl}", tt_)

    sdone = []
    for kd in range(16):
        kl = kd // 8
        ib = kd % 2
        wdeps = bk.wdeps(f"cc_sin{ib}")
        dsin = [p.dma("gpsimd", sinb[ib][:, 128 * dm:128 * (dm + 1)],
                      w_sinT[128 * dm:128 * (dm + 1), 128 * kd:128 * (kd + 1)],
                      deps=wdeps if dm == 0 else ()) for dm in range(8)]
        scant = None
        for tc in range(8):
            ps_ = psA if tc % 2 == 0 else psB
            ops = [(sinb[ib][:, 128 * dm:128 * (dm + 1)], uT[dm][:, 512 * tc:512 * (tc + 1)],
                    [dsin[dm]]) for dm in range(8)]
            tm = p.mmgroup(ps_[:, :], ops, deps_first=bk.wdeps(ps_.name), tag=f"inp{kd}_{tc}")
            bk.read(f"cc_sin{ib}", tm)
            te = p.vcopy(inpf[:, 512 * tc:512 * (tc + 1)], ps_[:, :],
                         deps=[tm] + (bk.wdeps("cc_inpf") if tc == 0 else []))
            bk.read(ps_.name, te)
            init = 0.0 if tc == 0 else sf[:, 512 * tc - 1:512 * tc]
            scant = p.scan(sf[:, 512 * tc:512 * (tc + 1)], dtile[kl][:, :],
                           inpf[:, 512 * tc:512 * (tc + 1)], init,
                           deps=[te, scant] + (bk.wdeps("cc_sf") if tc == 0 else []),
                           tag=f"scan{kd}_{tc}")
            bk.read("cc_inpf", scant)
        tcast = p.vcopy(sbf[:, :], sf[:, :], deps=[scant] + bk.wdeps("cc_sbf"))
        bk.read("cc_sf", tcast)
        ds = p.dma("sync", d_s[128 * kd:128 * (kd + 1), :], sbf[:, :], deps=[tcast])
        bk.read("cc_sbf", ds)
        sdone.append(ds)
    dso = [p.dma("gpsimd", sob[:, 1024 * i:1024 * (i + 1)], w_soT[128 * i:128 * (i + 1), :])
           for i in range(16)]
    for tg in range(8):
        dblk = [p.dma("sync", sblk[:, 512 * kd:512 * (kd + 1)],
                      d_s[128 * kd:128 * (kd + 1), 512 * tg:512 * (tg + 1)],
                      deps=[sdone[kd]] + (bk.wdeps("cc_sblk") if kd == 0 else []))
                for kd in range(16)]
        for sub in range(4):
            sg = stg[sub % 2]
            sgdeps = bk.wdeps((f"cc_stg{sub % 2}",))
            for half in range(2):
                ps_ = psA if half == 0 else psB
                ops = [(sblk[:, 512 * kd + 128 * sub:512 * kd + 128 * (sub + 1)],
                        sob[:, 1024 * kd + 512 * half:1024 * kd + 512 * (half + 1)],
                        [dblk[kd], dso[kd]]) for kd in range(16)]
                tm = p.mmgroup(ps_[:, :], ops, deps_first=bk.wdeps(ps_.name),
                               tag=f"lp{tg}_{sub}_{half}")
                if sub == 3 and half == 1:
                    for kd in range(16):
                        bk.read("cc_sblk", tm)
                te = p.scopy(sg[:, 512 * half:512 * (half + 1)], ps_[:, :],
                             deps=[tm] + sgdeps)
                bk.read(ps_.name, te)
                sgdeps = []
            dl = p.dma("sync", d_longpart[512 * tg + 128 * sub:512 * tg + 128 * (sub + 1), :],
                       sg[:, :], deps=[p.fence("scalar")], tag=f"lout{tg}_{sub}")
            bk.read((f"cc_stg{sub % 2}",), dl)


def stage_Cd(bd):
    p, C = bd.p, bd.C
    d_mem = bd.dout("d_mem", (T, 66))
    d_wgl = bd.d_wgl
    bk = bd.bk
    BIG = 30000.0
    MAGIC = 12582912.0

    MKs = bd.sb("cd_MK", (128, 256), F32)
    MVs = bd.sb("cd_MV", (128, 256), F32)
    CNT = bd.sb("cd_CNT", (128, 4), F32)
    RTf = bd.sb("cd_RTf", (128, 512), F32)
    RTb = bd.sb("cd_RTb", (128, 512), BF16)
    WTf = bd.sb("cd_WTf", (128, 512), F32)
    WTb = bd.sb("cd_WTb", (128, 512), BF16)
    Wtf = bd.sb("cd_Wtf", (128, 512), F32)
    rib = bd.sb("cd_rib", (128, 128), F32)
    rirow = bd.sb("cd_rirow", (1, 128), F32)
    wirow = bd.sb("cd_wirow", (1, 128), F32)
    wglrow = bd.sb("cd_wglrow", (2, 128), F32)
    cols = bd.sb("cd_cols", (128, 16), F32)
    Wsc = bd.sb("cd_Wsc", (128, 512), F32)
    Lcs = bd.sb("cd_Lcs", (128, 512), F32)
    Lci = bd.sb("cd_Lci", (128, 512), F32)
    prod = bd.sb("cd_prod", (128, 128), F32)
    Dsb = bd.sb("cd_D", (128, 128), F32)
    Mful = bd.sb("cd_Mf", (128, 128), F32)
    expd = bd.sb("cd_expd", (128, 128), F32)
    P2b = bd.sb("cd_P2", (128, 128), BF16)
    arow = bd.sb("cd_arow", (1, 128), F32)
    kchT = bd.sb("cd_kchT", (128, 3 * 32), F32)
    wvTb = bd.sb("cd_wvTb", (128, 32), BF16)
    Wsc2 = bd.sb("cd_Wsc2", (128, 512), F32)
    vstate = bd.sb("cd_vst", (128, 32), F32)
    stream = bd.sb("cd_strm", (128, 66), F32)
    psE = bd.ps("cdpsE", (128, 128))
    psF = bd.ps("cdpsF", (128, 128))
    psG = bd.ps("cdpsG", (128, 128))
    psH = bd.ps("cdpsH", (128, 128))
    psQ = bd.ps("cdpsQ", (128, 128))
    psW = bd.ps("cdpsW", (128, 2))
    psT = bd.ps("cdpsT", (128, 128), BF16)
    psT2 = bd.ps("cdpsT2", (128, 32), BF16)

    tz = [p.memset(MKs[:, :], 0.0), p.memset(MVs[:, :], 0.0), p.memset(CNT[:, :], 0.0)]
    tz2z = p.memset(stream[:, :], 0.0)
    zt = p.fence("gpsimd")

    for ch in range(CD_CHUNKS):
        blk = slice(128 * ch, 128 * (ch + 1))
        chf = [p.fence("tensor"), p.fence("vector"), p.fence("scalar")]
        dgr = p.dma("sync", wglrow[:, :], d_wgl[:, blk], deps=chf + bk.wdeps("cd_wglrow"))
        tgt = p.tr(psW[:, :], wglrow[:, :], C["c_identf"][0:2, 0:2],
                   deps=[dgr] + bk.wdeps("cdpsW"), tag=f"wglT{ch}")
        bk.read("cd_wglrow", tgt)
        dg = p.vcopy(cols[:, 12:14], psW[:, :], deps=[tgt])
        bk.read("cdpsW", dg)
        dl = dg
        dri = p.dma("sync", rirow[:, :], bd.d_idx[0:1, blk], deps=chf)
        dwi = p.dma("sync", wirow[:, :], bd.d_idx[1:2, blk], deps=chf)
        t1 = p.mm(psE[:, :], C["c_ones_row_f"][:, :], rirow[:, :], True, True,
                  deps=chf + [dri] + bk.wdeps("cdpsE") + ([zt] if ch == 0 else []), tag=f"ribm{ch}")
        t2 = p.vcopy(rib[:, :], psE[:, :], deps=[t1] + bk.wdeps("cd_rib"))
        bk.read("cdpsE", t2)
        tr = []
        for g in range(4):
            tr.append(p.ts(RTf[:, 128 * g:128 * (g + 1)], rib[:, :], C["c_iotac"][:, g:g + 1],
                           None, ALU.is_equal, deps=[t2] + bk.wdeps(("cd_RTf", g)), tag=f"RT{ch}_{g}"))
        t3 = p.mm(psE[:, :], C["c_ones_row_f"][:, :], wirow[:, :], True, True,
                  deps=[dwi] + bk.wdeps("cdpsE"), tag=f"wibm{ch}")
        t4 = p.vcopy(rib[:, :], psE[:, :], deps=[t3] + tr)
        bk.read("cdpsE", t4)
        tw = []
        for g in range(4):
            tw.append(p.ts(WTf[:, 128 * g:128 * (g + 1)], rib[:, :], C["c_iotac"][:, g:g + 1],
                           None, ALU.is_equal, deps=[t4] + bk.wdeps(("cd_WTf", g)), tag=f"WT{ch}_{g}"))
        bk.read("cd_rib", tw[-1])
        tcb = p.vcopy(RTb[:, :], RTf[:, :], deps=tr + bk.wdeps("cd_RTb"))
        twb = p.vcopy(WTb[:, :], WTf[:, :], deps=tw + bk.wdeps("cd_WTb"))
        twt = []
        for g in range(4):
            a_ = p.tr(psT[:, 0:128], WTb[:, 128 * g:128 * (g + 1)], C["c_ident"][:, :],
                      deps=[twb] + chf + bk.wdeps("cdpsT"), tag=f"Wt{ch}_{g}")
            b_ = p.vcopy(Wtf[:, 128 * g:128 * (g + 1)], psT[:, 0:128],
                         deps=[a_] + bk.wdeps(("cd_Wtf", g)))
            bk.read("cdpsT", b_)
            twt.append(b_)
        tks = []
        for j, rowt in enumerate((bd.qkr, bd.wkr, bd.wvr)):
            a_ = p.tr(psT2[:, :], rowt[:, blk],
                      C["c_ident"][0:32, 0:32], deps=chf + bk.wdeps("cdpsT2"), tag=f"kT{ch}_{j}")
            b_ = p.vcopy(kchT[:, 32 * j:32 * (j + 1)], psT2[:, :],
                         deps=[a_] + bk.wdeps(("cd_kchT", j)))
            bk.read("cdpsT2", b_)
            tks.append(b_)
        tvb = p.vcopy(wvTb[:, :], kchT[:, 64:96], deps=[tks[2]] + bk.wdeps("cd_wvTb"))
        bk.read(("cd_kchT", 2), tvb)
        if CD_LEVEL < 2:
            dmz = p.dma("sync", d_mem[blk, :], stream[:, :], deps=[tz2z] + [tvb, twb, tcb] + twt + tks + [dg, c1z_dummy] if False else [tz2z, tvb, twb, tcb, dg] + twt + tks)
            continue
        cw = None
        for g in range(4):
            cw = p.mm(psF[:, 0:1], WTf[:, 128 * g:128 * (g + 1)], CNT[:, g:g + 1],
                      g == 0, g == 3, deps=([twb] if g == 0 else []) + (bk.wdeps("cdpsF") if g == 0 else []),
                      tag=f"cntw{ch}_{g}")
        c1 = p.vcopy(cols[:, 0:1], psF[:, 0:1], deps=[cw])
        bk.read("cdpsF", c1)
        eq = None
        for g in range(4):
            eq = p.mm(psF[:, 0:128], WTb[:, 128 * g:128 * (g + 1)], WTb[:, 128 * g:128 * (g + 1)],
                      g == 0, g == 3, deps=[c1] if g == 0 else [], tag=f"eqw{ch}_{g}")
        m1 = p.tt(prod[:, :], psF[:, 0:128], C["c_su_f"][:, :], ALU.mult,
                  deps=[eq] + bk.wdeps("cd_prod"), tag=f"mip{ch}")
        bk.read("cdpsF", m1)
        m2 = p.mm(psF[:, 0:1], prod[:, :], C["c_ones_col_f"][:, :], True, True,
                  deps=[m1] + bk.wdeps("cdpsF"), tag=f"ipc{ch}")
        bk.read("cd_prod", m2)
        m3 = p.vcopy(cols[:, 1:2], psF[:, 0:1], deps=[m2])
        bk.read("cdpsF", m3)
        m4 = p.tt(cols[:, 2:3], cols[:, 0:1], cols[:, 1:2], ALU.add, deps=[m3, c1])
        m5 = p.ts(cols[:, 2:3], cols[:, 2:3], 0.5, None, ALU.mult, deps=[m4])
        m6 = p.ts(cols[:, 3:4], cols[:, 2:3], MAGIC, None, ALU.add, deps=[m5])
        m7 = p.ts(cols[:, 3:4], cols[:, 3:4], MAGIC, None, ALU.subtract, deps=[m6])
        m8 = p.tt(cols[:, 4:5], cols[:, 2:3], cols[:, 3:4], ALU.subtract, deps=[m7])
        m9 = p.ts(cols[:, 4:5], cols[:, 4:5], 2.0, None, ALU.mult, deps=[m8])
        m10 = p.tt(cols[:, 4:5], cols[:, 4:5], cols[:, 4:5], ALU.mult, deps=[m9],
                   tag=f"sig{ch}")
        if CD_LEVEL < 3:
            dmz = p.dma("sync", d_mem[blk, :], stream[:, :], deps=[tz2z, m10])
            continue
        eqr = None
        for g in range(4):
            eqr = p.mm(psG[:, :], WTb[:, 128 * g:128 * (g + 1)], RTb[:, 128 * g:128 * (g + 1)],
                       g == 0, g == 3, deps=([tcb, twb] + bk.wdeps("cdpsG")) if g == 0 else [],
                       tag=f"eqr{ch}_{g}")
        qw = p.mm(psQ[:, :], bd.wkr[0:32, blk], bd.qkr[0:32, blk], True, True,
                  deps=bk.wdeps("cdpsQ"), tag=f"qw{ch}")
        scv = []
        for s_ in range(2):
            if s_ == 0:
                pp = p.ts(cols[:, 5:6], cols[:, 4:5], -1.0, 1.0, ALU.mult, ALU.add, deps=[m10])
            else:
                pp = p.vcopy(cols[:, 5:6], cols[:, 4:5], deps=[m10, scv[-1]])
            pl = p.tt(cols[:, 6:7], cols[:, 5:6], cols[:, 13:14], ALU.mult, deps=[pp, dl])
            ws = p.ts(Wsc[:, :], Wtf[:, :], cols[:, 6:7], None, ALU.mult,
                      deps=[pl] + twt + bk.wdeps("cd_Wsc"), tag=f"Wsc{ch}_{s_}")
            lt = []
            for g in range(4):
                a_ = p.mm(psE[:, :], Wsc[:, 128 * g:128 * (g + 1)], C["c_su_f"][:, :],
                          True, True, deps=[ws] + bk.wdeps("cdpsE"), tag=f"lcs{ch}_{s_}_{g}")
                b_ = p.vcopy(Lcs[:, 128 * g:128 * (g + 1)], psE[:, :],
                             deps=[a_] + bk.wdeps(("cd_Lcs", g)))
                bk.read("cdpsE", b_)
                a2 = p.mm(psE[:, :], Wsc[:, 128 * g:128 * (g + 1)], C["c_uti_f"][:, :],
                          True, True, deps=[b_] + bk.wdeps("cdpsE"), tag=f"lci{ch}_{s_}_{g}")
                b2 = p.vcopy(Lci[:, 128 * g:128 * (g + 1)], psE[:, :],
                             deps=[a2] + bk.wdeps(("cd_Lci", g)))
                bk.read("cdpsE", b2)
                lt.append((b_, b2))
            bk.read("cd_Wsc", lt[-1][1])
            am = None
            am2 = None
            for g in range(4):
                pr_ = p.tt(prod[:, :], RTf[:, 128 * g:128 * (g + 1)], Lcs[:, 128 * g:128 * (g + 1)],
                           ALU.mult, deps=[lt[g][0], tr[g]] + bk.wdeps("cd_prod"), tag=f"apr{ch}_{s_}_{g}")
                am = p.mm(psF[:, 0:1], prod[:, :], C["c_ones_col_f"][:, :], g == 0, g == 3,
                          deps=[pr_] + (bk.wdeps("cdpsF") if g == 0 else []))
                am2 = p.mm(psH[0:1, :], C["c_ones_col_f"][:, :], prod[:, :],
                           g == 0, g == 3, deps=[] if g else bk.wdeps("cdpsH"),
                           tag=f"arow{ch}_{s_}_{g}")
                bk.read("cd_prod", am2)
                bk.read(("cd_Lcs", g), am)
            a1 = p.vcopy(cols[:, 7:8], psF[:, 0:1], deps=[am])
            bk.read("cdpsF", a1)
            a2r = p.vcopy(arow[:, :], psH[0:1, :], deps=[am2] + bk.wdeps("cd_arow"))
            bk.read("cdpsH", a2r)
            aex = p.act(cols[:, 8:9], cols[:, 7:8], AF.Exp, deps=[a1])
            bm = None
            for g in range(4):
                pr_ = p.tt(prod[:, :], WTf[:, 128 * g:128 * (g + 1)], Lci[:, 128 * g:128 * (g + 1)],
                           ALU.mult, deps=[lt[g][1], tw[g]] + bk.wdeps("cd_prod"), tag=f"bpr{ch}_{s_}_{g}")
                bm = p.mm(psF[:, 0:1], prod[:, :], C["c_ones_col_f"][:, :], g == 0, g == 3,
                          deps=[pr_] + (bk.wdeps("cdpsF") if g == 0 else []))
                bk.read("cd_prod", bm)
            b1 = p.vcopy(cols[:, 9:10], psF[:, 0:1], deps=[bm])
            bk.read("cdpsF", b1)
            em = None
            for g in range(4):
                em = p.mm(psF[:, 0:1], WTf[:, 128 * g:128 * (g + 1)],
                          Lci[:, 128 * g + 127:128 * (g + 1)], g == 0, g == 3,
                          deps=([b1] + bk.wdeps("cdpsF")) if g == 0 else [],
                          tag=f"em{ch}_{s_}_{g}")
                bk.read(("cd_Lci", g), em)
            e1 = p.vcopy(cols[:, 10:11], psF[:, 0:1], deps=[em])
            bk.read("cdpsF", e1)
            d1 = p.tt(cols[:, 11:12], cols[:, 10:11], cols[:, 9:10], ALU.subtract,
                      deps=[e1, b1])
            d2 = p.act(cols[:, 11:12], cols[:, 11:12], AF.Exp, deps=[d1])
            d3 = p.tt(cols[:, 11:12], cols[:, 11:12], cols[:, 12:13], ALU.mult,
                      deps=[d2, dg])
            d4 = p.tt(cols[:, 11:12], cols[:, 11:12], cols[:, 5:6], ALU.mult, deps=[d3])
            ks = None
            for g in range(4):
                ks = p.mm(psF[:, 0:32], RTf[:, 128 * g:128 * (g + 1)],
                          MKs[:, 32 * (4 * s_ + g):32 * (4 * s_ + g + 1)], g == 0, g == 3,
                          deps=(bk.wdeps("cdpsF") if g == 0 else []), tag=f"ks{ch}_{s_}_{g}")
            p1 = p.tt(prod[:, 0:32], psF[:, 0:32], kchT[:, 0:32], ALU.mult,
                      deps=[ks, tks[0]] + bk.wdeps("cd_prod"))
            bk.read("cdpsF", p1)
            p2 = p.red(cols[:, 14:15], prod[:, 0:32], ALU.add, deps=[p1])
            bk.read("cd_prod", p2)
            p3 = p.tt(cols[:, 14:15], cols[:, 14:15], cols[:, 8:9], ALU.mult, deps=[p2, aex])
            vs = None
            for g in range(4):
                vs = p.mm(psF[:, 0:32], RTf[:, 128 * g:128 * (g + 1)],
                          MVs[:, 32 * (4 * s_ + g):32 * (4 * s_ + g + 1)], g == 0, g == 3,
                          deps=([p1] + bk.wdeps("cdpsF")) if g == 0 else [], tag=f"vs{ch}_{s_}_{g}")
            v1 = p.ts(vstate[:, :], psF[:, 0:32], cols[:, 8:9], None, ALU.mult,
                      deps=[vs, aex] + bk.wdeps("cd_vst"))
            bk.read("cdpsF", v1)
            db_ = p.mm(psE[:, :], C["c_ones_row_f"][:, :], arow[:, :], True, True,
                       deps=[a2r] + bk.wdeps("cdpsE"), tag=f"dbc{ch}_{s_}")
            bk.read("cd_arow", db_)
            dd_ = p.ts(Dsb[:, :], psE[:, :], cols[:, 9:10], None, ALU.subtract,
                       deps=[db_, b1] + bk.wdeps("cd_D"))
            bk.read("cdpsE", dd_)
            mf = p.stt(Mful[:, :], psG[:, :], cols[:, 5:6], C["c_su_f"][:, :],
                       ALU.mult, ALU.mult, deps=[eqr, pp] + bk.wdeps("cd_Mf"), tag=f"mf{ch}_{s_}")
            m1_ = p.stt(Dsb[:, :], Mful[:, :], BIG, Dsb[:, :], ALU.mult, ALU.add,
                        deps=[mf, dd_])
            bk.read("cd_Mf", m1_)
            m2_ = p.ts(Dsb[:, :], Dsb[:, :], -BIG, None, ALU.add, deps=[m1_])
            ex_ = p.act(expd[:, :], Dsb[:, :], AF.Exp, deps=[m2_] + bk.wdeps("cd_expd"))
            bk.read("cd_D", ex_)
            pb_ = p.ts(P2b[:, :], expd[:, :], cols[:, 12:13], None, ALU.mult,
                       deps=[ex_, dg] + bk.wdeps("cd_P2"))
            bk.read("cd_expd", pb_)
            pq = p.tt(prod[:, :], psQ[:, :], P2b[:, :], ALU.mult,
                      deps=[qw, pb_] + bk.wdeps("cd_prod"), tag=f"p2qw{ch}_{s_}")
            sci = p.mm(psF[:, 0:1], prod[:, :], C["c_ones_col_f"][:, :], True, True,
                       deps=[pq] + bk.wdeps("cdpsF"), tag=f"sci{ch}_{s_}")
            bk.read("cd_prod", sci)
            sc1 = p.tt(stream[:, s_:s_ + 1], psF[:, 0:1], cols[:, 14:15], ALU.add,
                       deps=[sci, p3] + bk.wdeps(("cd_strm", s_)))
            bk.read("cdpsF", sc1)
            vi = p.mm(psF[:, 0:32], P2b[:, :], wvTb[:, :], True, True,
                      deps=[pb_, tvb] + bk.wdeps("cdpsF"), tag=f"vi{ch}_{s_}")
            vv = p.tt(stream[:, 2 + 32 * s_:2 + 32 * (s_ + 1)], psF[:, 0:32], vstate[:, :],
                      ALU.add, deps=[vi, v1] + bk.wdeps(("cd_strmv", s_)))
            bk.read("cdpsF", vv)
            bk.read("cd_vst", vv)
            scv.append(vv)
            if CD_LEVEL < 4:
                continue
            ws2 = p.ts(Wsc2[:, :], Wtf[:, :], cols[:, 11:12], None, ALU.mult,
                       deps=[d4] + twt + bk.wdeps("cd_Wsc2"), tag=f"Wsc2{ch}_{s_}")
            for g in range(4):
                dc_ = p.act(cols[:, 15:16], Lci[:, 128 * g + 127:128 * (g + 1)], AF.Exp,
                            deps=[lt[g][1]] + bk.wdeps("cd_col15"),
                            tag=f"decf{ch}_{s_}_{g}")
                for (MM, kv, src, tksrc) in ((MKs, 0, kchT[:, 32:64], tks[1]),
                                             (MVs, 1, kchT[:, 64:96], tks[2])):
                    blkm = MM[:, 32 * (4 * s_ + g):32 * (4 * s_ + g + 1)]
                    u1 = p.ts(blkm, blkm, cols[:, 15:16], None, ALU.mult,
                              deps=[dc_, ks if kv == 0 else vs], tag=f"dec{ch}_{s_}_{g}_{kv}")
                    bk.read("cd_col15", u1)
                    im = p.mm(psF[:, 0:32], Wsc2[:, 128 * g:128 * (g + 1)], src,
                              True, True, deps=[ws2, tksrc] + bk.wdeps("cdpsF"),
                              tag=f"inj{ch}_{s_}_{g}_{kv}")
                    u2 = p.tt(blkm, blkm, psF[:, 0:32], ALU.add, deps=[im, u1])
                    bk.read("cdpsF", u2)
                    bk.read(("cd_MM", s_, g, kv), u2)
            bk.read("cd_Wsc2", u2)
        bk.read("cdpsG", scv[-1])
        bk.read("cdpsQ", scv[-1])
        if CD_LEVEL < 5:
            dmz = p.dma("sync", d_mem[blk, :], stream[:, :], deps=[tz2z, scv[0], scv[1]])
            continue
        for g in range(4):
            cm = p.mm(psF[:, 0:1], Wtf[:, 128 * g:128 * (g + 1)], C["c_ones_col_f"][:, :],
                      True, True, deps=[scv[-1] if g == 0 else None] + bk.wdeps("cdpsF"),
                      tag=f"cnt{ch}_{g}")
            cu = p.tt(CNT[:, g:g + 1], CNT[:, g:g + 1], psF[:, 0:1], ALU.add, deps=[cm])
            bk.read("cdpsF", cu)
            bk.read(("cd_Wtf", g), cu)
        dm_ = p.dma("sync", d_mem[blk, :], stream[:, :], deps=[scv[0], scv[1]])
        bk.read(("cd_strm", 0), dm_)
        bk.read(("cd_strm", 1), dm_)
        bk.read(("cd_strmv", 0), dm_)
        bk.read(("cd_strmv", 1), dm_)
        bk.read("cd_g", dm_)


C_STAGES = ("a", "b", "c", "d")
CD_CHUNKS = 32
CD_LEVEL = 9


def launch_C(bd):
    bd.uT = [bd.sb(f"uT{i}", (128, T), BF16) for i in range(8)]
    bd.qkr = bd.sb("qkr", (32, T), BF16)
    bd.wkr = bd.sb("wkr", (32, T), BF16)
    bd.wvr = bd.sb("wvr", (32, T), BF16)
    bd.d_idx = bd.nc.dram_tensor("d_idx", [2, T], F32, kind="Internal").ap()
    with bd.scoped():
        stage_Ca(bd)
    bd.p.barrier("b_ca")
    with bd.scoped():
        stage_Cb(bd)
    bd.p.barrier("b_cb")
    if "c" in C_STAGES:
        with bd.scoped():
            stage_Cc(bd)
        bd.p.barrier("b_cc")
    if "d" in C_STAGES:
        with bd.scoped():
            stage_Cd(bd)
    else:
        d_mem = bd.dout("d_mem", (T, 66))
        z = bd.sb("zz", (128, 66), F32)
        tz = bd.p.memset(z[:, :], 0.0)
        for i in range(32):
            bd.p.dma("sync", d_mem[128 * i:128 * (i + 1), :], z[:, :], deps=[tz])
    if "c" not in C_STAGES:
        d_longpart = bd.dout("d_longpart", (T, D))
        z2 = bd.sb("zz2", (128, 1024), F32)
        tz2 = bd.p.memset(z2[:, :], 0.0)
        for i in range(32):
            bd.p.dma("sync", d_longpart[128 * i:128 * (i + 1), :], z2[:, :], deps=[tz2])


def stage_D(bd):
    p, C = bd.p, bd.C
    i_x2 = bd.din("i_x2own2", (HT, D))
    i_lpa = bd.din("i_lpa", (HT, D))
    i_lpb = bd.din("i_lpb", (HT, D))
    i_ms0 = bd.din("i_ms0", (HT, 66))
    i_ms1 = bd.din("i_ms1", (HT, 66))
    i_gg = bd.din("i_glgm", (HT, 2))
    w_mupT = bd.din("w_mupT", (D, 4096))
    w_mupb = bd.din("w_mupb", (128, 32))
    w_mdT = bd.din("w_mdownT", (4096, D))
    w_mdb = bd.din("w_mdownb", (1, D))
    w_moT = bd.din("w_moT", (32, D))
    d_x3_own = bd.dout("d_x3_own", (HT, D))
    bk = bd.bk

    mup = bd.sb("d_mup", (128, 8 * 4096), BF16)
    mdn = bd.sb("d_mdn", (128, 32 * 1024), BF16)
    mdb = bd.sb("d_mdb", (1, 1024), F32)
    mupb = bd.sb("d_mupb", (128, 32), F32)
    moT = bd.sb("d_moT", (32, 1024), BF16)
    x2c = [bd.sb(f"d_x2c{i}", (128, 1024), F32) for i in range(2)]
    lpa = [bd.sb(f"d_lpa{i}", (128, 1024), F32) for i in range(2)]
    lpb = [bd.sb(f"d_lpb{i}", (128, 1024), F32) for i in range(2)]
    ms0 = [bd.sb(f"d_ms0{i}", (128, 66), F32) for i in range(2)]
    ms1 = [bd.sb(f"d_ms1{i}", (128, 66), F32) for i in range(2)]
    ggc = [bd.sb(f"d_gg{i}", (128, 2), F32) for i in range(2)]
    lng = bd.sb("d_lng", (128, 1024), F32)
    yb = bd.sb("d_yb", (128, 1024), F32)
    sc4 = bd.sb("d_sc4", (128, 8), F32)
    racc = bd.sb("d_racc", (128, 32), F32)
    rb16 = bd.sb("d_rb16", (128, 32), BF16)
    rdT = bd.sb("d_rdT", (32, 128), BF16)
    ynb = bd.sb("d_ynb", (128, 1024), BF16)
    ynT = bd.sb("d_ynT", (128, 1024), BF16)
    u2 = bd.sb("d_u2", (128, 4096), BF16)
    sqt = bd.sb("d_sqt", (128, 1024), BF16)
    x3f = bd.sb("d_x3f", (128, 1024), F32)
    cols = bd.sb("d_cols", (128, 8), F32)
    psA = bd.ps("dpsA", (128, 512))
    psB = bd.ps("dpsB", (128, 512))
    psU = bd.ps("dpsU", (128, 128))
    psR = bd.ps("dpsR", (32, 128), BF16)
    psY = bd.ps("dpsY", (128, 128), BF16)
    kb6, kb6t = bd.kcol(1e-6)
    kb5, kb5t = bd.kcol(1e-5)

    dmu = [p.dma("gpsimd", mup[:, 4096 * i:4096 * (i + 1)], w_mupT[128 * i:128 * (i + 1), :])
           for i in range(8)]
    dmd = [p.dma("gpsimd", mdn[:, 1024 * i:1024 * (i + 1)], w_mdT[128 * i:128 * (i + 1), :])
           for i in range(32)]
    dmb = p.dma("sync", mdb[:, :], w_mdb)
    dub = p.dma("sync", mupb[:, :], w_mupb)
    dmo = [p.dma("gpsimd", moT[0:32, 1024 * 0:1024 * 1], w_moT)]
    done = {}
    for tc in range(NTC):
        i = tc % 2
        blk = slice(128 * tc, 128 * (tc + 1))
        guard = [done.get(("in", i))]
        dx2 = p.dma("sync", x2c[i][:, :], i_x2[blk, :], deps=guard)
        dla = p.dma("sync", lpa[i][:, :], i_lpa[blk, :], deps=guard)
        dlb = p.dma("sync", lpb[i][:, :], i_lpb[blk, :], deps=guard)
        dm0 = p.dma("sync", ms0[i][:, :], i_ms0[blk, :], deps=guard)
        dm1 = p.dma("sync", ms1[i][:, :], i_ms1[blk, :], deps=guard)
        dgg = p.dma("sync", ggc[i][:, :], i_gg[blk, :], deps=guard)
        t0 = p.tt(lng[:, :], lpa[i][:, :], lpb[i][:, :], ALU.add,
                  deps=[dla, dlb] + bk.wdeps("d_lng"))
        c0 = p.vcopy(sc4[:, 0:2], ms0[i][:, 0:2], deps=[dm0] + bk.wdeps("d_sc4"))
        c1 = p.vcopy(sc4[:, 2:4], ms1[i][:, 0:2], deps=[dm1])
        r0 = p.red(cols[:, 0:1], sc4[:, 0:4], ALU.max, deps=[c0, c1] + bk.wdeps("d_cols"))
        r1 = p.ts(cols[:, 0:1], cols[:, 0:1], -1.0, None, ALU.mult, deps=[r0])
        r2 = p.act(sc4[:, 4:8], sc4[:, 0:4], AF.Exp, bias=cols[:, 0:1],
                   accum_out=cols[:, 1:2], deps=[r1, done.get("e4")], tag=f"sm4_{tc}")
        done["e4"] = r2
        r3 = p.recip(cols[:, 1:2], cols[:, 1:2], deps=[r2])
        r4 = p.ts(sc4[:, 4:8], sc4[:, 4:8], cols[:, 1:2], None, ALU.mult, deps=[r3])
        bk.read("d_sc4", r4)
        v0 = p.ts(racc[:, :], ms0[i][:, 2:34], sc4[:, 4:5], None, ALU.mult,
                  deps=[r4, dm0] + bk.wdeps("d_racc"), tag=f"rd0_{tc}")
        v1 = p.stt(racc[:, :], ms0[i][:, 34:66], sc4[:, 5:6], racc[:, :], ALU.mult, ALU.add,
                   deps=[v0])
        v2 = p.stt(racc[:, :], ms1[i][:, 2:34], sc4[:, 6:7], racc[:, :], ALU.mult, ALU.add,
                   deps=[v1, dm1])
        v3 = p.stt(racc[:, :], ms1[i][:, 34:66], sc4[:, 7:8], racc[:, :], ALU.mult, ALU.add,
                   deps=[v2], tag=f"rd3_{tc}")
        done[("in", i)] = v3
        v4 = p.vcopy(rb16[:, :], racc[:, :], deps=[v3] + bk.wdeps("d_rb16"))
        bk.read("d_racc", v4)
        tr1 = p.tr(psR[:, :], rb16[:, :], C["c_ident"][:, :], deps=[v4] + bk.wdeps("dpsR"),
                   tag=f"rT{tc}")
        tr2 = p.vcopy(rdT[:, :], psR[:, :], deps=[tr1] + bk.wdeps("d_rdT"))
        bk.read("dpsR", tr2)
        bk.read("d_rb16", tr2)
        for half in range(2):
            ps_ = psA if half == 0 else psB
            tm = p.mm(ps_[:, :], rdT[:, :], moT[0:32, 512 * half:512 * (half + 1)], True, True,
                      deps=[tr2, dmo[0]] + bk.wdeps(ps_.name), tag=f"mo{tc}_{half}")
            bk.read("d_rdT", tm)
            hsl = slice(512 * half, 512 * (half + 1))
            t1 = p.stt(yb[:, hsl], ps_[:, :], ggc[i][:, 1:2], x2c[i][:, hsl],
                       ALU.mult, ALU.add, deps=[tm, dgg, dx2] + (bk.wdeps("d_yb") if half == 0 else []),
                       tag=f"ymem{tc}_{half}")
            bk.read(ps_.name, t1)
        t2 = p.stt(yb[:, :], lng[:, :], ggc[i][:, 0:1], yb[:, :], ALU.mult, ALU.add,
                   deps=[t0, t1], tag=f"ylong{tc}")
        bk.read("d_lng", t2)
        q1 = p.act(sqt[:, :], yb[:, :], AF.Square, accum_out=cols[:, 2:3],
                   deps=[t2, done.get("sqt")])
        done["sqt"] = q1
        q2 = p.act(cols[:, 2:3], cols[:, 2:3], AF.Ln, bias=kb6, scale=1.0 / D, deps=[q1, kb6t])
        q3 = p.act(cols[:, 2:3], cols[:, 2:3], AF.Exp, scale=-0.5, deps=[q2])
        q4 = p.ts(ynb[:, :], yb[:, :], cols[:, 2:3], None, ALU.mult,
                  deps=[q3] + bk.wdeps("d_ynb"), tag=f"yn{tc}")
        trt = None
        for dm in range(8):
            w1_ = p.tr(psY[:, :], ynb[:, 128 * dm:128 * (dm + 1)], C["c_ident"][:, :],
                       deps=[q4] + bk.wdeps("dpsY"), tag=f"ynT{tc}_{dm}")
            trt = p.vcopy(ynT[:, 128 * dm:128 * (dm + 1)], psY[:, :],
                          deps=[w1_] + (bk.wdeps("d_ynT") if dm == 0 else []))
            bk.read("dpsY", trt)
        bk.read("d_ynb", trt)
        for hc in range(32):
            ops = [(mup[:, 4096 * dm + 128 * hc:4096 * dm + 128 * (hc + 1)],
                    ynT[:, 128 * dm:128 * (dm + 1)],
                    [dmu[dm], trt] if hc == 0 else [dmu[dm]]) for dm in range(8)]
            tmu = p.mmgroup(psU[:, :], ops, deps_first=bk.wdeps("dpsU"), tag=f"up{tc}_{hc}")
            bk.read("d_ynT", tmu)
            tsl = p.act(u2[:, 128 * hc:128 * (hc + 1)], psU[:, :], AF.Silu,
                        bias=mupb[:, hc:hc + 1], deps=[tmu, dub] + bk.wdeps(("d_u2", hc)))
            bk.read("dpsU", tsl)
        upf = [p.fence("scalar")]
        for half in range(2):
            ps_ = psA if half == 0 else psB
            hsl = slice(512 * half, 512 * (half + 1))
            ops = [(u2[:, 128 * hc:128 * (hc + 1)], mdn[:, 1024 * hc + 512 * half:1024 * hc + 512 * (half + 1)],
                    [dmd[hc]] if tc == 0 else ()) for hc in range(32)]
            tmd = p.mmgroup(ps_[:, :], ops, deps_first=bk.wdeps(ps_.name) + upf, tag=f"dn{tc}_{half}")
            for hc in range(32):
                bk.read(("d_u2", hc), tmd)
            tb = p.mm(ps_[:, :], C["c_ones_row_f"][0:1, :], mdb[:, hsl], False, True,
                      deps=[dmb], tag=f"dnb{tc}_{half}")
            a1 = p.tt(x3f[:, hsl], ps_[:, :], yb[:, hsl], ALU.add,
                      deps=[tb] + (bk.wdeps("d_x3f") if half == 0 else []), tag=f"res1_{tc}_{half}")
            a2 = p.tt(x3f[:, hsl], x3f[:, hsl], x2c[i][:, hsl], ALU.add, deps=[a1])
            bk.read(ps_.name, a2)
            done["yb"] = a2
        bk.read("d_yb", done["yb"])
        l2 = p.red(cols[:, 4:5], x3f[:, :], ALU.add, deps=[done["yb"]], negate=True)
        l3 = p.ts(cols[:, 4:5], cols[:, 4:5], 1.0 / D, None, ALU.mult, deps=[l2])
        l4 = p.act(sqt[:, :], x3f[:, :], AF.Square, bias=cols[:, 4:5],
                   accum_out=cols[:, 5:6], deps=[l3, done.get("sqt2")])
        done["sqt2"] = l4
        l5 = p.act(cols[:, 5:6], cols[:, 5:6], AF.Ln, bias=kb5, scale=1.0 / D, deps=[l4, kb5t])
        l6 = p.act(cols[:, 5:6], cols[:, 5:6], AF.Exp, scale=-0.5, deps=[l5])
        l7 = p.ts(x3f[:, :], x3f[:, :], cols[:, 4:5], cols[:, 5:6], ALU.add, ALU.mult,
                  deps=[l6], tag=f"lnx3_{tc}")
        bk.read("d_x3f", l7)
        d3 = p.dma("sync", d_x3_own[blk, :], x3f[:, :], deps=[l7])
        bk.read("d_x3f", d3)
        bk.read("d_cols", l7)


def launch_D(bd):
    with bd.scoped():
        stage_D(bd)


def stage_E1(bd):
    p, C = bd.p, bd.C
    i_x3T = bd.din("i_x3T", (D, T))
    w_gT = bd.din("w_f1gT", (D, 8))
    bk = bd.bk
    x3F = [bd.sb(f"e1_x3F{i}", (128, T), BF16) for i in range(8)]
    gsb = bd.sb("e1_g", (128, 64), BF16)
    prc = bd.sb("e1_prc", (128, 8), F32)
    colz = bd.sb("e1_colz", (128, 2), F32)
    psA = bd.ps("e1psA", (128, 8))
    psT = bd.ps("e1psT", (8, 128))

    dx = [p.dma("gpsimd", x3F[i][:], i_x3T[128 * i:128 * (i + 1), :]) for i in range(8)]
    dg = [p.dma("gpsimd", gsb[:, 8 * i:8 * (i + 1)], w_gT[128 * i:128 * (i + 1), :])
          for i in range(8)]
    for tc in range(32):
        ops = [(x3F[dm][:, 128 * tc:128 * (tc + 1)], gsb[:, 8 * dm:8 * (dm + 1)],
                [dx[dm], dg[dm]]) for dm in range(8)]
        tm = p.mmgroup(psA[:, :], ops, deps_first=bk.wdeps("e1psA") + (bd.cdeps if tc == 0 else []),
                       tag=f"sc{tc}")
        te = p.vcopy(prc[:, :], psA[:, :], deps=[tm] + bk.wdeps("e1_prc"))
        bk.read("e1psA", te)
        r0 = p.red(colz[:, 0:1], prc[:, :], ALU.max, deps=[te] + bk.wdeps("e1_colz"))
        r1 = p.ts(colz[:, 0:1], colz[:, 0:1], -1.0, None, ALU.mult, deps=[r0])
        r2 = p.act(prc[:, :], prc[:, :], AF.Exp, bias=colz[:, 0:1], accum_out=colz[:, 1:2],
                   deps=[r1], tag=f"e1exp{tc}")
        r3 = p.recip(colz[:, 1:2], colz[:, 1:2], deps=[r2])
        r4 = p.ts(bd.probs[:, 8 * tc:8 * (tc + 1)], prc[:, :], colz[:, 1:2], None, ALU.mult,
                  deps=[r3] + (bk.wdeps("e_probs") if tc == 0 else []), tag=f"e1p{tc}")
        bk.read("e1_prc", r4)
        bk.read("e1_colz", r4)
        tt1 = p.tr(psT[:, :], bd.probs[:, 8 * tc:8 * (tc + 1)], C["c_identf"][:, :],
                   deps=[r4] + bk.wdeps("e1psT"), tag=f"e1T{tc}")
        tt2 = p.vcopy(bd.probsT[:, 128 * tc:128 * (tc + 1)], psT[:, :], deps=[tt1])
        bk.read("e1psT", tt2)
    cps = []
    for dm in range(8):
        cp = p.vcopy(bd.x3own[dm][:, :], x3F[dm][:, 0:HT], deps=[dx[dm]])
        cps.append(cp)
    bd.e1_done = cps


def stage_E2(bd):
    p, C = bd.p, bd.C
    bk = bd.bk
    lo = bd.sb("e2_lo", (8, 1), F32)
    hi = bd.sb("e2_hi", (8, 1), F32)
    tau = bd.sb("e2_tau", (8, 1), F32)
    cnt = bd.sb("e2_cnt", (8, 1), F32)
    sel = bd.sb("e2_sel", (8, 2), F32)
    dl = bd.sb("e2_dl", (8, 2), F32)
    cmp01 = bd.sb("e2_cmp", (8, T), BF16)
    t0 = p.step("vector", lambda v: v.memset(lo[:, :], 0.0))
    t1 = p.step("vector", lambda v: v.memset(hi[:, :], 2.0), deps=[t0])
    t2 = p.step("vector", lambda v: v.memset(tau[:, :], 1.0), deps=[t1])
    tprev = t2
    for it in range(40):
        c1 = p.ts(cmp01[:, :], bd.probsT[:, :], tau[:, :], None, ALU.is_ge,
                  deps=[tprev], tag=f"bis{it}")
        c2 = p.red(cnt[:, :], cmp01[:, :], ALU.add, deps=[c1])
        c3 = p.ts(sel[:, 0:1], cnt[:, :], 1024.0, None, ALU.is_ge, deps=[c2])
        c4 = p.ts(sel[:, 1:2], sel[:, 0:1], -1.0, 1.0, ALU.mult, ALU.add, deps=[c3])
        c5 = p.tt(dl[:, 0:1], tau[:, :], lo[:, :], ALU.subtract, deps=[c4])
        c6 = p.tt(dl[:, 1:2], tau[:, :], hi[:, :], ALU.subtract, deps=[c5])
        c7 = p.stt(lo[:, :], dl[:, 0:1], sel[:, 0:1], lo[:, :], ALU.mult, ALU.add, deps=[c6])
        c8 = p.stt(hi[:, :], dl[:, 1:2], sel[:, 1:2], hi[:, :], ALU.mult, ALU.add, deps=[c7])
        c9 = p.tt(tau[:, :], lo[:, :], hi[:, :], ALU.add, deps=[c8])
        tprev = p.ts(tau[:, :], tau[:, :], 0.5, None, ALU.mult, deps=[c9])
    bd.tau_lo = lo
    bd.e2_done = tprev


def stage_E3(bd):
    p, C = bd.p, bd.C
    w1d = bd.din("w_f1w1", (8 * 1024, 4096))
    w2d = bd.din("w_f1w2", (8 * 4096, 1024))
    i_x3own = bd.din("i_x3own", (HT, D))
    d_out = bd.dout("d_out", (HT, D))
    bk = bd.bk
    kb5, kb5t = bd.kcol(1e-5)

    w1sb = bd.sb("e3_w1", (128, 16384), BF16)
    w2sb = bd.sb("e3_w2", (128, 16384), BF16)
    hsb = bd.sb("e3_h", (128, 8192), BF16)
    yacc = bd.sb("e3_y", (128, 16 * 1024), BF16)
    tbc = bd.sb("e3_tbc", (128, 8), F32)
    wcol = bd.sb("e3_w", (128, 256), F32)
    taur = bd.sb("e3_taur", (1, 8), F32)
    x3c = [bd.sb(f"e3_x3c{i}", (128, 1024), F32) for i in range(2)]
    yf = bd.sb("e3_yf", (128, 1024), F32)
    sqt = bd.sb("e3_sqt", (128, 1024), BF16)
    cols = bd.sb("e3_cols", (128, 4), F32)
    psH = [bd.ps("e3psH0", (128, 512)), bd.ps("e3psH1", (128, 512))]
    psY = [bd.ps("e3psY0", (128, 512)), bd.ps("e3psY1", (128, 512))]
    psS = bd.ps("e3psS", (1, 8))
    psB2 = bd.ps("e3psB2", (128, 8))

    tz = p.step("vector", lambda v: v.memset(yacc[:, :], 0.0), deps=[])
    tt0 = p.tr(psS[:, :], bd.tau_lo[:, :], C["c_identf"][0:8, 0:8],
               deps=[bd.e2_done], tag="tauT")
    tt1 = p.vcopy(taur[:, :], psS[:, :], deps=[tt0])
    tt2 = p.mm(psB2[:, :], C["c_ones_row_f"][:, :], taur[:, :], True, True, deps=[tt1],
               tag="taubc")
    tt3 = p.vcopy(tbc[:, :], psB2[:, :], deps=[tt2])
    bk.read("e3psB2", tt3)
    wts = []
    for tc in range(NTC):
        m1 = p.tt(wcol[:, 8 * tc:8 * (tc + 1)], bd.probs[:, 8 * tc:8 * (tc + 1)],
                  tbc[:, :], ALU.is_ge, deps=[tt3], tag=f"msk{tc}")
        m2 = p.tt(wcol[:, 8 * tc:8 * (tc + 1)], wcol[:, 8 * tc:8 * (tc + 1)],
                  bd.probs[:, 8 * tc:8 * (tc + 1)], ALU.mult, deps=[m1])
        wts.append(m2)
    ydep = {"last": tz}
    for e in range(8):
        for hh in range(2):
            wfence = [p.fence("tensor")]
            dw1 = [p.dma("gpsimd", w1sb[:, 2048 * dm:2048 * (dm + 1)],
                         w1d[1024 * e + 128 * dm:1024 * e + 128 * (dm + 1),
                             2048 * hh:2048 * (hh + 1)],
                         deps=wfence if dm == 0 else (), tag=f"w1_{e}_{hh}")
                   for dm in range(8)]
            dw2 = [p.dma("gpsimd", w2sb[:, 1024 * hc:1024 * (hc + 1)],
                         w2d[4096 * e + 2048 * hh + 128 * hc:
                             4096 * e + 2048 * hh + 128 * (hc + 1), :],
                         deps=wfence if hc == 0 else (), tag=f"w2_{e}_{hh}")
                   for hc in range(16)]
            for tg in range(4):
                prev_pe = [p.fence("tensor")]
                for hc in range(16):
                    ps_ = psH[hc % 2]
                    ops = [(w1sb[:, 2048 * dm + 128 * hc:2048 * dm + 128 * (hc + 1)],
                            bd.x3own[dm][:, 512 * tg:512 * (tg + 1)],
                            [dw1[dm]] + (bd.e1_done if e == 0 and hh == 0 and tg == 0 else []))
                           for dm in range(8)]
                    th = p.mmgroup(ps_[:, :], ops, deps_first=bk.wdeps(ps_.name),
                                   tag=f"f1h{e}_{hh}_{tg}_{hc}")
                    tg_ = p.act(hsb[:, 512 * hc:512 * (hc + 1)], ps_[:, :], AF.Gelu,
                                deps=[th] + prev_pe, tag=f"f1g{e}_{hh}_{tg}_{hc}")
                    bk.read(ps_.name, tg_)
                for sub in range(4):
                    for half in range(2):
                        ps_ = psY[(sub * 2 + half) % 2]
                        ops = [(hsb[:, 512 * hc + 128 * sub:512 * hc + 128 * (sub + 1)],
                                w2sb[:, 1024 * hc + 512 * half:1024 * hc + 512 * (half + 1)],
                                [dw2[hc]] if tg == 0 else ()) for hc in range(16)]
                        ty = p.mmgroup(ps_[:, :], ops,
                                       deps_first=bk.wdeps(ps_.name) + [p.fence("scalar")],
                                       tag=f"f1y{e}_{hh}_{tg}_{sub}_{half}")
                        tcb = 4 * tg + sub
                        ysl = yacc[:, 1024 * tcb + 512 * half:1024 * tcb + 512 * (half + 1)]
                        ta = p.stt(ysl, ps_[:, :], wcol[:, 8 * tcb + e:8 * tcb + e + 1],
                                   ysl, ALU.mult, ALU.add,
                                   deps=[ty, ydep["last"], wts[tcb]],
                                   tag=f"f1a{e}_{hh}_{tg}_{sub}_{half}")
                        bk.read(ps_.name, ta)
                        ydep["last"] = ta
    done = {}
    for tc in range(NTC):
        i = tc % 2
        blk = slice(128 * tc, 128 * (tc + 1))
        dx3 = p.dma("sync", x3c[i][:, :], i_x3own[blk, :], deps=[done.get(("x3", i))])
        a1 = p.tt(yf[:, :], yacc[:, 1024 * tc:1024 * (tc + 1)], x3c[i][:, :], ALU.add,
                  deps=[dx3, ydep["last"]] + bk.wdeps("e3_yf"), tag=f"fin{tc}")
        done[("x3", i)] = a1
        l2 = p.red(cols[:, 0:1], yf[:, :], ALU.add, deps=[a1], negate=True)
        l3 = p.ts(cols[:, 0:1], cols[:, 0:1], 1.0 / D, None, ALU.mult, deps=[l2])
        l4 = p.act(sqt[:, :], yf[:, :], AF.Square, bias=cols[:, 0:1],
                   accum_out=cols[:, 1:2], deps=[l3, done.get("sqt")])
        done["sqt"] = l4
        l5 = p.act(cols[:, 1:2], cols[:, 1:2], AF.Ln, bias=kb5, scale=1.0 / D,
                   deps=[l4, kb5t])
        l6 = p.act(cols[:, 1:2], cols[:, 1:2], AF.Exp, scale=-0.5, deps=[l5])
        l7 = p.ts(yf[:, :], yf[:, :], cols[:, 0:1], cols[:, 1:2], ALU.add, ALU.mult,
                  deps=[l6], tag=f"lnx4_{tc}")
        bk.read("e3_yf", l7)
        d4 = p.dma("sync", d_out[blk, :], yf[:, :], deps=[l7])
        bk.read("e3_yf", d4)


def launch_E(bd):
    bd.x3own = [bd.sb(f"x3own{i}", (128, HT), BF16) for i in range(8)]
    bd.probs = bd.sb("e_probs", (128, 256), F32)
    bd.probsT = bd.sb("e_probsT", (8, T), F32)
    with bd.scoped():
        stage_E1(bd)
    bd.p.barrier("b_e1")
    with bd.scoped():
        stage_E2(bd)
    bd.p.barrier("b_e2")
    with bd.scoped():
        stage_E3(bd)


def _mk(builder, taps=()):
    nc = bass.Bass("TRN2", target_bir_lowering=False)
    st = contextlib.ExitStack()
    with st:
        bd = Build(nc, st, taps)
        bd.load_consts()
        builder(bd)
        deps = bd.p.all_fences()
        for (tap, ap) in bd.tap_pending:
            bd.p.dma("gpsimd", tap, ap, deps=deps, tag="tapcopy")
        fin = bd.p.op("sync", lambda e: None, tag="final")
        fin.waits += bd.p.all_fences()
        bd.p.emit(st)
    return nc


def launch_A(bd):
    with bd.scoped():
        stage_S1(bd)


def launch_B(bd):
    bd.x1T = [bd.sb(f"x1T{i}", (128, HT), BF16) for i in range(8)]
    bd.yacc = bd.sb("yacc", (128, 16 * 1024), F32)
    with bd.scoped():
        stage_S2(bd)
    bd.p.barrier("b_s2")
    with bd.scoped():
        stage_S3(bd)
    bd.p.barrier("b_s3")
    with bd.scoped():
        stage_S4(bd)


def kernel(**inputs):
    maps = host_prep(inputs)
    cores = list(range(8))
    resA = run_bass_kernel_spmd(_mk(launch_A), maps, core_ids=cores).results
    add_launchB_inputs(maps, resA)
    resB = run_bass_kernel_spmd(_mk(launch_B), maps, core_ids=cores).results
    add_launchC_inputs(maps, resB, inputs)
    resC = run_bass_kernel_spmd(_mk(launch_C), maps, core_ids=cores).results
    add_launchD_inputs(maps, resB, resC, inputs)
    resD = run_bass_kernel_spmd(_mk(launch_D), maps, core_ids=cores).results
    add_launchE_inputs(maps, resD, inputs)
    resE = run_bass_kernel_spmd(_mk(launch_E), maps, core_ids=cores).results
    out = np.empty((B, T, D), np.float32)
    for c in range(8):
        b, r = c // 2, c % 2
        out[b, r * HT:(r + 1) * HT] = resE[c]["d_out"]
    return out


# version 03
# speedup vs baseline: 1.0247x; 1.0247x over previous
import contextlib
import numpy as np
import ml_dtypes
import concourse.bass as bass
import concourse.mybir as mybir
from concourse.bass_utils import run_bass_kernel_spmd

F32 = mybir.dt.float32
BF16 = mybir.dt.bfloat16
I32 = mybir.dt.int32
AF = mybir.ActivationFunctionType
ALU = mybir.AluOpType
AX = mybir.AxisListType

ENGINES = ("tensor", "vector", "scalar", "gpsimd", "sync")
DMA_QUEUES = ("sync", "gpsimd", "scalar", "vector")
B, T, D = 4, 4096, 1024
HT = T // 2
NTC = HT // 128
GROUPS = [[0, 1], [2, 3], [4, 5], [6, 7]]
F32EPS = float(np.finfo(np.float32).eps)
BF = ml_dtypes.bfloat16


class Op:
    __slots__ = ("fn", "waits", "incs", "tag")

    def __init__(self, fn, tag=""):
        self.fn = fn
        self.waits = []
        self.incs = []
        self.tag = tag


class Prog:
    NDMASEM = 4

    def __init__(self, nc):
        self.nc = nc
        self.streams = {e: [] for e in ENGINES}
        self.counts = {e: 0 for e in ENGINES}
        self.dma_counts = {}
        self.dma_rr = {q: 0 for q in DMA_QUEUES}
        for q in DMA_QUEUES:
            for i in range(self.NDMASEM):
                self.dma_counts[f"dma_{q}{i}"] = 0
        self.dma_counts["cc"] = 0
        self.sem_handles = {}

    def op(self, eng, fn, tag=""):
        o = Op(fn, tag)
        self.streams[eng].append(o)
        return o

    def _tok(self, eng, o):
        self.counts[eng] += 1
        o.incs.append((eng, 1))
        return (eng, self.counts[eng])

    def step(self, eng, fn, deps=(), tag=""):
        o = self.op(eng, fn, tag)
        for d in deps:
            if d is not None:
                o.waits.append(d)
        return self._tok(eng, o)

    def mm(self, out, lhsT, rhs, start, stop, deps=(), tag=""):
        return self.step("tensor",
                         lambda t: t.matmul(out, lhsT, rhs, start=start, stop=stop,
                                            skip_group_check=True),
                         deps, tag)

    def mmgroup(self, out, ops, deps_first=(), tag=""):
        n = len(ops)
        tok = None
        for i, (lh, rh, dd) in enumerate(ops):
            deps = list(dd)
            if i == 0:
                deps += list(deps_first)
            tok = self.mm(out, lh, rh, i == 0, i == n - 1, deps, tag=f"{tag}[{i}]")
        return tok

    def tr(self, out_ps, in_, ident, deps=(), tag=""):
        return self.step("tensor", lambda t: t.transpose(out_ps, in_, ident), deps, tag)

    def act(self, out, in_, func, bias=0.0, scale=1.0, accum_out=None, deps=(), tag=""):
        return self.step("scalar", lambda s: s.activation(out, in_, func, bias=bias,
                                                          scale=scale, accum_out=accum_out),
                         deps, tag)

    def scopy(self, out, in_, deps=(), tag=""):
        return self.act(out, in_, AF.Copy, deps=deps, tag=tag)

    def vcopy(self, out, in_, deps=(), tag=""):
        return self.step("vector", lambda v: v.tensor_copy(out, in_), deps, tag)

    def gcopy(self, out, in_, deps=(), tag=""):
        return self.step("gpsimd", lambda g: g.tensor_copy(out, in_), deps, tag)

    def tt(self, out, a, b, alu, deps=(), tag=""):
        return self.step("vector", lambda v: v.tensor_tensor(out, a, b, alu), deps, tag)

    def ts(self, out, in0, s1, s2=None, op0=ALU.mult, op1=ALU.bypass, accum_out=None,
           deps=(), tag=""):
        return self.step("vector", lambda v: v.tensor_scalar(out, in0, s1, s2, op0, op1,
                                                             accum_out=accum_out), deps, tag)

    def stt(self, out, in0, scalar, in1, op0, op1, deps=(), tag=""):
        return self.step("vector", lambda v: v.scalar_tensor_tensor(out, in0, scalar, in1,
                                                                    op0, op1), deps, tag)

    def red(self, out, in_, alu, deps=(), negate=False, absv=False, tag=""):
        return self.step("vector", lambda v: v.tensor_reduce(out, in_, AX.X, alu,
                                                             apply_absolute_value=absv,
                                                             negate=negate), deps, tag)

    def recip(self, out, in_, deps=(), tag=""):
        return self.step("vector", lambda v: v.reciprocal(out, in_), deps, tag)

    def memset(self, ap, val, deps=(), tag=""):
        return self.step("gpsimd", lambda g: g.memset(ap, val), deps, tag)

    def scan(self, out, d0, d1, init, deps=(), tag=""):
        return self.step("vector", lambda v: v.tensor_tensor_scan(out, d0, d1, init,
                                                                  ALU.mult, ALU.add), deps, tag)

    def dma(self, q, out, in_, deps=(), tag="", sem=None, **kw):
        def fn(e):
            return e.dma_start(out=out, in_=in_, **kw)
        o = self.op(q, fn, tag)
        for d in deps:
            if d is not None:
                o.waits.append(d)
        if sem is None:
            i = self.dma_rr[q]
            self.dma_rr[q] = (i + 1) % self.NDMASEM
            sem = f"dma_{q}{i}"
        if self.dma_counts[sem] > 0:
            o.waits.append((sem, self.dma_counts[sem]))
        self.dma_counts[sem] += 16
        o.incs.append((sem, 16))
        return (sem, self.dma_counts[sem])

    def cc(self, kind, alu, ins_, outs_, deps=(), tag=""):
        def fn(e):
            return e.collective_compute(kind, alu, GROUPS, ins_, outs_)
        o = self.op("gpsimd", fn, tag)
        for d in deps:
            if d is not None:
                o.waits.append(d)
        if self.dma_counts["cc"] > 0:
            o.waits.append(("cc", self.dma_counts["cc"]))
        self.dma_counts["cc"] += 16
        o.incs.append(("cc", 16))
        return ("cc", self.dma_counts["cc"])

    def barrier(self, tag="bar"):
        toks = self.all_fences()
        for e in ENGINES:
            o = self.op(e, lambda en: en.nop(), tag=f"{tag}_{e}")
            o.waits += [t for t in toks if t[0] != e]

    def fence(self, eng):
        return (eng, self.counts[eng])

    def dma_fence(self, q):
        return [(f"dma_{q}{i}", self.dma_counts[f"dma_{q}{i}"])
                for i in range(self.NDMASEM) if self.dma_counts[f"dma_{q}{i}"] > 0]

    def all_fences(self):
        toks = [self.fence(e) for e in ENGINES if self.counts[e] > 0]
        for q in DMA_QUEUES:
            toks += self.dma_fence(q)
        if self.dma_counts["cc"] > 0:
            toks.append(("cc", self.dma_counts["cc"]))
        return toks

    def emit(self, stack):
        nc = self.nc
        for n in list(ENGINES) + list(self.dma_counts.keys()):
            self.sem_handles[n] = stack.enter_context(nc.semaphore("sem_" + n))
        blk = stack.enter_context(nc.Block())

        def play(eng_name):
            def run(e):
                for o in self.streams[eng_name]:
                    for (sn, v) in o.waits:
                        e.wait_ge(self.sem_handles[sn], v)
                    inst = o.fn(e)
                    if o.incs:
                        assert inst is not None, f"op {o.tag} returned None but has incs"
                        for (sn, k) in o.incs:
                            inst.then_inc(self.sem_handles[sn], k)
            return run

        blk.tensor(play("tensor"))
        blk.vector(play("vector"))
        blk.scalar(play("scalar"))
        blk.gpsimd(play("gpsimd"))
        blk.sync(play("sync"))



CONST_SPECS = {
    "c_ident": ((128, 128), BF16), "c_identf": ((128, 128), F32),
    "c_su_f": ((128, 128), F32), "c_uti_f": ((128, 128), F32),
    "c_su_b": ((128, 128), BF16), "c_attnmask": ((128, 2048), BF16),
    "c_ones_col_b": ((128, 1), BF16), "c_ones_col_f": ((128, 1), F32),
    "c_ones_row_b": ((1, 128), BF16), "c_ones_row_f": ((1, 128), F32),
    "c_halfsel": ((128, 2), BF16), "c_iotac": ((128, 4), F32),
    "c_iotarow": ((128, 128), F32), "c_pow2": ((18, 2), BF16),
    "c_ones_row512f": ((1, 512), F32),
}


def _consts():
    c = {}
    eye = np.eye(128, dtype=np.float32)
    c["c_ident"] = eye.astype(BF)
    c["c_identf"] = eye.copy()
    iu = np.triu(np.ones((128, 128), np.float32), 1)
    c["c_su_f"] = iu.copy()
    c["c_uti_f"] = np.triu(np.ones((128, 128), np.float32), 0)
    c["c_su_b"] = iu.astype(BF)
    ki = np.arange(128)[:, None]
    qi = np.arange(512)[None, :]
    m = np.stack([np.where(o * 128 + ki > qi, -1e9, 0.0) for o in range(4)])
    c["c_attnmask"] = np.ascontiguousarray(m.transpose(1, 0, 2).reshape(128, 2048)).astype(BF)
    c["c_ones_col_b"] = np.ones((128, 1), BF)
    c["c_ones_col_f"] = np.ones((128, 1), np.float32)
    c["c_ones_row_b"] = np.ones((1, 128), BF)
    c["c_ones_row_f"] = np.ones((1, 128), np.float32)
    hs = np.zeros((128, 2), np.float32)
    hs[:64, 0] = 1.0
    hs[64:, 1] = 1.0
    c["c_halfsel"] = hs.astype(BF)
    c["c_iotac"] = (np.arange(128, dtype=np.float32)[:, None]
                    + 128.0 * np.arange(4, dtype=np.float32)[None, :])
    c["c_iotarow"] = np.broadcast_to(np.arange(128, dtype=np.float32), (128, 128)).copy()
    c["c_ones_row512f"] = np.ones((1, 512), np.float32)
    ps = np.zeros((18, 2), np.float32)
    for i in range(18):
        ps[i, i // 9] = float(2 ** (i % 9))
    c["c_pow2"] = ps.astype(BF)
    for k in c:
        assert c[k].shape == CONST_SPECS[k][0], k
    return c


def host_prep(ins):
    f32 = np.float32
    ct = lambda a: np.ascontiguousarray(a, dtype=f32)
    x = ct(ins["L0_attn_hidden_states"])
    consts = _consts()
    maps = []
    for c in range(8):
        b, r = c // 2, c % 2
        m = dict(consts)
        xb = x[b]
        m["i_xT"] = ct(xb.T)
        m["i_x_own"] = ct(xb[r * HT:(r + 1) * HT])
        m["w_qT"] = ct(ins["L0_attn_Wq"][512 * r:512 * (r + 1)].T)
        m["w_kT"] = ct(ins["L0_attn_Wk"][128 * r:128 * (r + 1)].T)
        m["w_vT"] = ct(ins["L0_attn_Wv"][128 * r:128 * (r + 1)].T)
        m["w_oT"] = ct(ins["L0_attn_Wo"].T[512 * r:512 * (r + 1)])
        m["w_f0w1"] = ct(ins["L0_ffn_w1"].reshape(8 * 1024, 2048))
        m["w_f0w2"] = ct(ins["L0_ffn_w2"].reshape(8 * 2048, 1024))
        m["w_f0b1T"] = ct(ins["L0_ffn_b1"].T)
        m["w_f0b2"] = ct(ins["L0_ffn_b2"])
        m["w_f0gateT"] = ct(ins["L0_ffn_gate_w"].T)
        m["w_f0gateb"] = ct(ins["L0_ffn_gate_b"].reshape(1, 8))
        m["w_f0rmsw"] = ct(ins["L0_ffn_rms_w"].reshape(1, 8))
        maps.append(m)
    return maps


def add_launchC_inputs(maps, resB, ins):
    f32 = np.float32
    ct = lambda a: np.ascontiguousarray(a, dtype=f32)
    x2 = {}
    for b in range(4):
        x2[b] = np.concatenate([resB[2 * b]["d_x2_own"], resB[2 * b + 1]["d_x2_own"]], axis=0)
    for c in range(8):
        b, r = c // 2, c % 2
        m = maps[c]
        m["i_x2T"] = ct(x2[b].T)
        m["w_conv"] = ct(ins["L1_attn_conv_w"].reshape(8, 128, 4).transpose(1, 0, 2).reshape(128, 32))
        m["w_l1gT"] = ct(ins["L1_attn_gate_w"].T)
        m["w_l1gb"] = ct(ins["L1_attn_gate_b"].reshape(8, 128).T)
        m["w_sinT"] = ct(ins["L1_attn_state_in_w"][2048 * r:2048 * (r + 1)].T)
        m["w_soT"] = ct(ins["L1_attn_state_out_w"].T[2048 * r:2048 * (r + 1)])
        m["w_mydecay"] = ct(ins["L1_attn_state_decay_logit"][2 * r:2 * (r + 1)].reshape(1, 2))
        m["w_keysT"] = ct(np.concatenate([ins["L1_attn_qkey_w"], ins["L1_attn_wkey_w"],
                                          ins["L1_attn_value_w"]], axis=0).T)
        m["w_bitsT"] = ct(np.concatenate([ins["L1_attn_read_bits_w"],
                                          ins["L1_attn_write_bits_w"]], axis=0).T)
        pR = np.zeros((36, 1), f32)
        pW = np.zeros((36, 1), f32)
        for i in range(9):
            pR[9 * r + i, 0] = float(2 ** i)
            pW[18 + 9 * r + i, 0] = float(2 ** i)
        m["w_powR"] = pR
        m["w_powW"] = pW
        m["w_g3T"] = ct(np.concatenate([ins["L1_attn_write_gate_w"], ins["L1_attn_gate_long_w"],
                                        ins["L1_attn_gate_mem_w"]], axis=0).T)
        m["w_g3b"] = ct(np.concatenate([ins["L1_attn_write_gate_b"], ins["L1_attn_gate_long_b"],
                                        ins["L1_attn_gate_mem_b"]]).reshape(1, 3))


def add_launchD_inputs(maps, resB, resC, ins):
    f32 = np.float32
    ct = lambda a: np.ascontiguousarray(a, dtype=f32)
    for c in range(8):
        b, r = c // 2, c % 2
        m = maps[c]
        sl = slice(r * HT, (r + 1) * HT)
        m["i_x2own2"] = resB[c]["d_x2_own"]
        m["i_lpa"] = ct(resC[2 * b]["d_longpart"][sl])
        m["i_lpb"] = ct(resC[2 * b + 1]["d_longpart"][sl])
        m["i_ms0"] = ct(resC[2 * b]["d_mem"][sl])
        m["i_ms1"] = ct(resC[2 * b + 1]["d_mem"][sl])
        m["i_glgm"] = ct(resC[c]["d_rows"][:, sl].T)
        m["w_mupT"] = ct(ins["L1_attn_mlp_up_w"].T)
        m["w_mupb"] = ct(ins["L1_attn_mlp_up_b"].reshape(32, 128).T)
        m["w_mdownT"] = ct(ins["L1_attn_mlp_down_w"].T)
        m["w_mdownb"] = ct(ins["L1_attn_mlp_down_b"].reshape(1, D))
        m["w_moT"] = ct(ins["L1_attn_mem_out_w"].T)


def add_launchE_inputs(maps, resD, ins):
    f32 = np.float32
    ct = lambda a: np.ascontiguousarray(a, dtype=f32)
    x3 = {}
    for b in range(4):
        x3[b] = np.concatenate([resD[2 * b]["d_x3_own"], resD[2 * b + 1]["d_x3_own"]], axis=0)
    for c in range(8):
        b, r = c // 2, c % 2
        m = maps[c]
        own = x3[b][r * HT:(r + 1) * HT]
        oth = x3[b][(1 - r) * HT:(2 - r) * HT]
        m["i_x3T"] = ct(np.concatenate([own, oth], axis=0).T)
        m["i_x3own"] = ct(own)
        m["w_f1gT"] = ct(ins["L1_ffn_gate_w"].T)
        m["w_f1w1"] = ct(ins["L1_ffn_w1"].reshape(8 * 1024, 4096))
        m["w_f1w2"] = ct(ins["L1_ffn_w2"].reshape(8 * 4096, 1024))


def add_launchB_inputs(maps, resA):
    for c in range(8):
        b, r = c // 2, c % 2
        pa = resA[2 * b]["d_projpart"][r * HT:(r + 1) * HT]
        pb = resA[2 * b + 1]["d_projpart"][r * HT:(r + 1) * HT]
        maps[c]["i_proj_a"] = np.ascontiguousarray(pa)
        maps[c]["i_proj_b"] = np.ascontiguousarray(pb)



class Banks:

    def __init__(self):
        self.readers = {}

    def wdeps(self, name):
        return self.readers.pop(name, [])

    def read(self, name, tok):
        self.readers.setdefault(name, []).append(tok)


class Build:
    def __init__(self, nc, stack, taps=()):
        self.nc = nc
        self.p = Prog(nc)
        self.st = stack
        self.taps = set(taps)
        self.C = {}
        self.bk = Banks()
        self.tap_pending = []

    @contextlib.contextmanager
    def scoped(self):
        outer = self.st
        with contextlib.ExitStack() as sst:
            self.st = sst
            try:
                yield
            finally:
                self.st = outer

    def din(self, name, shape, dtype=F32):
        return self.nc.dram_tensor(name, list(shape), dtype, kind="ExternalInput").ap()

    def dint(self, name, shape, dtype=F32):
        ap = self.nc.dram_tensor(name, list(shape), dtype, kind="Internal").ap()
        if name in self.taps:
            tap = self.nc.dram_tensor("tap_" + name, list(shape), dtype,
                                      kind="ExternalOutput").ap()
            self.tap_pending.append((tap, ap))
        return ap

    def dout(self, name, shape, dtype=F32):
        return self.nc.dram_tensor(name, list(shape), dtype, kind="ExternalOutput").ap()

    def sb(self, name, shape, dtype=F32):
        return self.st.enter_context(self.nc.sbuf_tensor(name, list(shape), dtype))

    def ps(self, name, shape, dtype=F32):
        return self.st.enter_context(self.nc.psum_tensor(name, list(shape), dtype))

    def load_consts(self):
        deps = []
        for n, (shp, dt) in CONST_SPECS.items():
            d = self.din(n, shp, dt)
            self.C[n] = self.sb("s" + n, shp, dt)
            deps.append(self.p.dma("sync", self.C[n][:], d))
        self.cdeps = deps
        self.ktile = self.sb("sk_consts", (128, 32), F32)
        self.kvals = {}

    def kcol(self, val):
        if val not in self.kvals:
            i = len(self.kvals)
            assert i < 32
            ap = self.ktile[:, i:i + 1]
            tok = self.p.memset(ap, float(val), tag=f"kcol{i}")
            self.kvals[val] = (ap, tok)
        return self.kvals[val]


def stage_S1(bd):
    p, C = bd.p, bd.C
    i_xT = bd.din("i_xT", (D, T))
    w_qT = bd.din("w_qT", (D, 512))
    w_kT = bd.din("w_kT", (D, 128))
    w_vT = bd.din("w_vT", (D, 128))
    w_oT = bd.din("w_oT", (512, D))
    d_projpart = bd.dout("d_projpart", (T, D))

    xT = [bd.sb(f"s1_xT{i}", (128, T), BF16) for i in range(8)]
    qp = [bd.sb(f"s1_qp{i}", (128, T), BF16) for i in range(4)]
    kaug = [bd.sb(f"s1_ka{i}", (65, T), BF16) for i in range(2)]
    vt = bd.sb("s1_vt", (128, 130 * 32), BF16)
    wq = bd.sb("s1_wq", (128, 8 * 512), BF16)
    wk = bd.sb("s1_wk", (128, 8 * 128), BF16)
    wv = bd.sb("s1_wv", (128, 8 * 128), BF16)
    wo = bd.sb("s1_wo", (128, 4 * 1024), BF16)
    qaug = bd.sb("s1_qaug", (65, 512), BF16)
    ptile = [bd.sb(f"s1_pt{i}", (128, 512), BF16) for i in range(2)]
    sq = bd.sb("s1_sq", (128, 512), BF16)
    mhat = bd.sb("s1_mhat", (1, 8), F32)
    mtmp = bd.sb("s1_mtmp", (1, 1), F32)
    augrow = bd.sb("s1_augrow", (1, 512), BF16)
    rrow = bd.sb("s1_rrow", (1, 512), F32)
    rb = bd.sb("s1_rb", (64, 512), F32)
    ao = [bd.sb(f"s1_ao{i}", (128, T), BF16) for i in range(4)]
    stg = [bd.sb(f"s1_stg{i}", (128, 1024), F32) for i in range(2)]

    psA = bd.ps("s1psA", (128, 512))
    psB = bd.ps("s1psB", (128, 512))
    psC = bd.ps("s1psC", (128, 512))
    psD = bd.ps("s1psD", (128, 512))

    dx = [p.dma("gpsimd", xT[i][:], i_xT[128 * i:128 * (i + 1), :]) for i in range(8)]
    dwq = [p.dma("gpsimd", wq[:, 512 * i:512 * (i + 1)], w_qT[128 * i:128 * (i + 1), :])
           for i in range(8)]
    dwk = [p.dma("gpsimd", wk[:, 128 * i:128 * (i + 1)], w_kT[128 * i:128 * (i + 1), :])
           for i in range(8)]
    dwv = [p.dma("gpsimd", wv[:, 128 * i:128 * (i + 1)], w_vT[128 * i:128 * (i + 1), :])
           for i in range(8)]
    dwo = [p.dma("gpsimd", wo[:, 1024 * i:1024 * (i + 1)], w_oT[128 * i:128 * (i + 1), :])
           for i in range(4)]

    tk1 = p.memset(kaug[0][64:65, :], 1.0)
    tk2 = p.memset(kaug[1][64:65, :], 1.0)
    tv0 = p.memset(vt[:, :], 0.0)
    tvones = [p.memset(vt[:, 130 * kc + cc:130 * kc + cc + 1], 1.0, deps=[tv0])
              for kc in range(32) for cc in (64, 129)]
    tmh = p.memset(mhat[:, :], 0.0)

    bk = bd.bk

    ev_q = {}
    mhat_tok = tmh
    sqname = "s1sq"
    for pr in range(4):
        for qc in range(8):
            ops = [(wq[:, 512 * dm + 128 * pr:512 * dm + 128 * (pr + 1)],
                    xT[dm][:, 512 * qc:512 * (qc + 1)], [dwq[dm], dx[dm]])
                   for dm in range(8)]
            ps_ = psA if (pr * 8 + qc) % 2 == 0 else psB
            tqm = p.mmgroup(ps_[:, :], ops, deps_first=bk.wdeps(ps_.name), tag=f"qproj{pr}_{qc}")
            te = p.act(qp[pr][:, 512 * qc:512 * (qc + 1)], ps_[:, :], AF.Copy,
                       deps=[tqm], tag=f"qev{pr}_{qc}")
            bk.read(ps_.name, te)
            tsq = p.act(sq[:, :], ps_[:, :], AF.Square,
                        deps=[tqm] + bk.wdeps(sqname), tag="qsq")
            bk.read(ps_.name, tsq)
            tm1 = p.mm(psD[0:1, :], C["c_ones_col_b"][:, :], sq[:, :], True, True,
                       deps=[tsq] + bk.wdeps("s1psD") + (bd.cdeps if pr == 0 and qc == 0 else []),
                       tag="qn")
            bk.read(sqname, tm1)
            tm2 = p.red(mtmp[:, :], psD[0:1, :], ALU.max, deps=[tm1, mhat_tok], tag="qmax")
            bk.read("s1psD", tm2)
            mhat_tok = p.tt(mhat[0:1, 0:1], mhat[0:1, 0:1], mtmp[:, :], ALU.max,
                            deps=[tm2], tag="qmaxacc")
            ev_q[(pr, qc)] = te

    ev_k = {}
    for qc in range(8):
        ops = [(wk[:, 128 * dm:128 * (dm + 1)], xT[dm][:, 512 * qc:512 * (qc + 1)],
                [dwk[dm], dx[dm]]) for dm in range(8)]
        ps_ = psA if qc % 2 == 0 else psB
        tkm = p.mmgroup(ps_[:, :], ops, deps_first=bk.wdeps(ps_.name), tag=f"kproj{qc}")
        te0 = p.act(kaug[0][0:64, 512 * qc:512 * (qc + 1)], ps_[0:64, :], AF.Copy,
                    deps=[tkm, tk1 if qc == 0 else None])
        te1 = p.act(kaug[1][0:64, 512 * qc:512 * (qc + 1)], ps_[64:128, :], AF.Copy,
                    deps=[tkm, tk2 if qc == 0 else None])
        bk.read(ps_.name, te0)
        bk.read(ps_.name, te1)
        tsq = p.act(sq[:, :], ps_[:, :], AF.Square, deps=[tkm] + bk.wdeps(sqname), tag="ksq")
        bk.read(ps_.name, tsq)
        tm1 = p.mm(psD[0:1, :], C["c_ones_col_b"][:, :], sq[:, :], True, True,
                   deps=[tsq] + bk.wdeps("s1psD"))
        bk.read(sqname, tm1)
        tm2 = p.red(mtmp[:, :], psD[0:1, :], ALU.max, deps=[tm1, mhat_tok])
        bk.read("s1psD", tm2)
        mhat_tok = p.tt(mhat[0:1, 1:2], mhat[0:1, 1:2], mtmp[:, :], ALU.max, deps=[tm2])
        ev_k[qc] = (te0, te1)

    vdeps = list(tvones)
    for tc in range(32):
        ops = [(xT[dm][:, 128 * tc:128 * (tc + 1)], wv[:, 128 * dm:128 * (dm + 1)],
                [dwv[dm], dx[dm]]) for dm in range(8)]
        tvm = p.mmgroup(psC[:, 0:128], ops,
                        deps_first=bk.wdeps("s1psC") + ([tv0] if tc == 0 else []),
                        tag=f"vproj{tc}")
        tva = p.act(vt[:, 130 * tc:130 * tc + 64], psC[:, 0:64], AF.Copy, deps=[tvm])
        tvb = p.act(vt[:, 130 * tc + 65:130 * tc + 129], psC[:, 64:128], AF.Copy, deps=[tvm])
        bk.read("s1psC", tva)
        bk.read("s1psC", tvb)
        vdeps += [tva, tvb]

    t1 = p.tt(mhat[0:1, 2:3], mhat[0:1, 0:1], mhat[0:1, 1:2], ALU.mult, deps=[mhat_tok])
    t2 = p.act(mhat[0:1, 3:4], mhat[0:1, 2:3], AF.Sqrt, deps=[t1])
    t3 = p.ts(mhat[0:1, 4:5], mhat[0:1, 3:4], -1.0, None, ALU.mult, deps=[t2])
    t4 = p.memset(augrow[:, :], 1.0)
    t5 = p.ts(augrow[:, :], augrow[:, :], mhat[0:1, 4:5], None, ALU.mult, deps=[t3, t4])
    taug = p.vcopy(qaug[64:65, 0:512], augrow[:, :], deps=[t5])

    first_att = True
    tpv = None
    for h in range(8):
        pr, lo = h // 2, (h % 2) * 64
        kv = h // 4
        for qc in range(8):
            tq = p.vcopy(qaug[0:64, :], qp[pr][lo:lo + 64, 512 * qc:512 * (qc + 1)],
                         deps=[ev_q[(pr, qc)], taug] + bk.wdeps("qaug"), tag=f"qa{h}_{qc}")
            nkc = 4 * qc + 4
            for kc in range(nkc):
                ps_ = psA if kc % 2 == 0 else psB
                diag = kc >= 4 * qc
                sdeps = [tq, ev_k[kc // 4][kv]] + bk.wdeps(ps_.name)
                tsm = p.mm(ps_[:, :], kaug[kv][:, 128 * kc:128 * (kc + 1)], qaug[:, :],
                           True, not diag, deps=sdeps, tag=f"att_s{h}_{qc}_{kc}")
                if diag:
                    tsm = p.mm(ps_[:, :], C["c_ident"][:, :],
                               C["c_attnmask"][:, 512 * (kc - 4 * qc):512 * (kc - 4 * qc + 1)],
                               False, True, deps=[], tag=f"att_m{h}_{qc}_{kc}")
                pt = ptile[kc % 2]
                ptn = f"s1pt{kc % 2}"
                texp = p.act(pt[:, :], ps_[:, :], AF.Exp, scale=0.125,
                             deps=[tsm] + bk.wdeps(ptn), tag=f"exp{h}_{qc}_{kc}")
                bk.read(ps_.name, texp)
                bk.read("qaug", texp)
                pvdeps = [texp]
                if kc == 0:
                    pvdeps += bk.wdeps("s1psC")
                    if first_att:
                        pvdeps += vdeps
                        first_att = False
                tpv = p.mm(psC[0:65, :], vt[:, 130 * kc + 65 * kv:130 * kc + 65 * kv + 65],
                           pt[:, :], kc == 0, kc == nkc - 1, deps=pvdeps,
                           tag=f"pv{h}_{qc}_{kc}")
                bk.read(ptn, tpv)
            trc = p.recip(rrow[:, :], psC[64:65, :], deps=[tpv] + bk.wdeps("rrow"),
                          tag=f"rc{h}_{qc}")
            bk.read("s1psC", trc)
            tbc = p.mm(psD[0:64, :], C["c_ones_row_f"][0:1, 0:64], rrow[:, :], True, True,
                       deps=[trc] + bk.wdeps("s1psD"), tag=f"bc{h}_{qc}")
            bk.read("rrow", tbc)
            tev = p.vcopy(rb[:, :], psD[0:64, :], deps=[tbc] + bk.wdeps("rb"))
            bk.read("s1psD", tev)
            tao = p.tt(ao[pr][lo:lo + 64, 512 * qc:512 * (qc + 1)], psC[0:64, :],
                       rb[:, :], ALU.mult, deps=[tev], tag=f"aoev{h}_{qc}")
            bk.read("s1psC", tao)
            bk.read("rb", tao)
            bk.read("qaug", tao)

    ao_done = [("vector", p.counts["vector"])]
    for tc in range(32):
        sg = stg[tc % 2]
        sgev = []
        for half in range(2):
            ps_ = psA if half == 0 else psB
            ops = [(ao[pr][:, 128 * tc:128 * (tc + 1)],
                    wo[:, 1024 * pr + 512 * half:1024 * pr + 512 * (half + 1)],
                    [dwo[pr]]) for pr in range(4)]
            tpm = p.mmgroup(ps_[:, :], ops,
                            deps_first=bk.wdeps(ps_.name) + (ao_done if tc == 0 else []),
                            tag=f"oproj{tc}_{half}")
            tse = p.scopy(sg[:, 512 * half:512 * (half + 1)], ps_[:, :],
                          deps=[tpm] + bk.wdeps((f"stg{tc % 2}", half)))
            bk.read(ps_.name, tse)
            sgev.append(tse)
        tdo = p.dma("sync", d_projpart[128 * tc:128 * (tc + 1), :], sg[:, :],
                    deps=sgev, tag=f"projout{tc}")
        bk.read((f"stg{tc % 2}", 0), tdo)
        bk.read((f"stg{tc % 2}", 1), tdo)
    bd.s1_out = d_projpart
    bd.s1_done = p.all_fences()


def stage_S2(bd):
    p, C = bd.p, bd.C
    i_pa = bd.din("i_proj_a", (HT, D))
    i_pb = bd.din("i_proj_b", (HT, D))
    i_x_own = bd.din("i_x_own", (HT, D))
    d_x1_own = bd.dint("d_x1_own", (HT, D))

    pa = [bd.sb(f"s2_pa{i}", (128, 1024), F32) for i in range(2)]
    pb = [bd.sb(f"s2_pb{i}", (128, 1024), F32) for i in range(2)]
    xo = [bd.sb(f"s2_xo{i}", (128, 1024), F32) for i in range(2)]
    s_ = bd.sb("s2_s", (128, 1024), F32)
    sqt = bd.sb("s2_sqt", (128, 1024), BF16)
    x1f = bd.sb("s2_x1f", (128, 1024), F32)
    colm = bd.sb("s2_colm", (128, 4), F32)
    x1T = bd.x1T
    x1tm = bd.sb("s2_x1tm", (128, 1024), BF16)
    psT = bd.ps("s2psT", (128, 512), BF16)
    kb, kbt = bd.kcol(1e-5)

    done = {}
    trtoks = []
    for tc in range(NTC):
        i = tc % 2
        dpa = p.dma("sync", pa[i][:, :], i_pa[128 * tc:128 * (tc + 1), :],
                    deps=[done.get(("pa", i))])
        dpb = p.dma("sync", pb[i][:, :], i_pb[128 * tc:128 * (tc + 1), :],
                    deps=[done.get(("pa", i))])
        dxo = p.dma("sync", xo[i][:, :], i_x_own[128 * tc:128 * (tc + 1), :],
                    deps=[done.get(("pa", i))])
        t0 = p.tt(s_[:, :], pa[i][:, :], pb[i][:, :], ALU.add,
                  deps=[dpa, dpb, done.get("s")])
        t1 = p.tt(s_[:, :], s_[:, :], xo[i][:, :], ALU.add, deps=[dxo, t0])
        t2 = p.red(colm[:, 0:1], s_[:, :], ALU.add, deps=[t1], negate=True)
        t3 = p.ts(colm[:, 0:1], colm[:, 0:1], 1.0 / D, None, ALU.mult, deps=[t2])
        t4 = p.act(sqt[:, :], s_[:, :], AF.Square, bias=colm[:, 0:1], accum_out=colm[:, 1:2],
                   deps=[t3, done.get("sq")])
        t5 = p.act(colm[:, 2:3], colm[:, 1:2], AF.Ln, bias=kb, scale=1.0 / D,
                   deps=[t4, kbt])
        t6 = p.act(colm[:, 2:3], colm[:, 2:3], AF.Exp, scale=-0.5, deps=[t5])
        t7 = p.ts(x1f[:, :], s_[:, :], colm[:, 0:1], colm[:, 2:3], ALU.add, ALU.mult,
                  deps=[t6, done.get("x1f")])
        t8 = p.vcopy(x1tm[:, :], x1f[:, :], deps=[t7, done.get("x1tm")])
        d1 = p.dma("sync", d_x1_own[128 * tc:128 * (tc + 1), :], x1f[:, :], deps=[t7])
        trt = []
        for dm in range(8):
            tt1 = p.tr(psT[:, 0:128], x1tm[:, 128 * dm:128 * (dm + 1)], C["c_ident"][:, :],
                       deps=[t8] + bd.bk.wdeps("s2psT"))
            tt2 = p.vcopy(x1T[dm][:, 128 * tc:128 * (tc + 1)], psT[:, 0:128], deps=[tt1])
            bd.bk.read("s2psT", tt2)
            trt.append(tt2)
        done[("pa", i)] = t1
        done["s"] = t7
        done["sq"] = t5
        done["x1f"] = d1
        done["x1tm"] = trt[-1]
        trtoks += trt
    bd.d_x1_own = d_x1_own
    bd.x1T = x1T
    bd.x1T_toks = trtoks


def stage_S3(bd):
    p, C = bd.p, bd.C
    w1d = bd.din("w_f0w1", (8 * 1024, 2048))
    w2d = bd.din("w_f0w2", (8 * 2048, 1024))
    b1d = bd.din("w_f0b1T", (2048, 8))
    x1T = bd.x1T
    yacc = bd.yacc
    bk = bd.bk

    w1sb = bd.sb("s3_w1", (128, 16384), BF16)
    w2sb = bd.sb("s3_w2", (128, 16384), BF16)
    hsb = bd.sb("s3_h", (128, 8192), BF16)
    b1sb = bd.sb("s3_b1", (128, 128), F32)
    psH = [bd.ps("s3psH0", (128, 512)), bd.ps("s3psH1", (128, 512))]
    psY = [bd.ps("s3psY0", (128, 512)), bd.ps("s3psY1", (128, 512))]

    tz = p.step("vector", lambda v: v.memset(yacc[:, :], 0.0), tag="yzero")
    db1 = p.dma("sync", b1sb[:, :].rearrange("p (c e) -> p c e", e=8),
                b1d.rearrange("(c p) e -> p c e", p=128), tag="b1")
    ydep = {"last": tz}
    for e in range(8):
        wfence = [p.fence("tensor")]
        dw1 = [p.dma("gpsimd", w1sb[:, 2048 * dm:2048 * (dm + 1)],
                     w1d[1024 * e + 128 * dm:1024 * e + 128 * (dm + 1), :],
                     deps=wfence if dm == 0 else (), tag=f"w1_{e}")
               for dm in range(8)]
        dw2 = [p.dma("gpsimd", w2sb[:, 1024 * hc:1024 * (hc + 1)],
                     w2d[2048 * e + 128 * hc:2048 * e + 128 * (hc + 1), :],
                     deps=wfence if hc == 0 else (), tag=f"w2_{e}")
               for hc in range(16)]
        for tg in range(4):
            prev_pe = [p.fence("tensor")]
            for hc in range(16):
                ps_ = psH[hc % 2]
                ops = [(w1sb[:, 2048 * dm + 128 * hc:2048 * dm + 128 * (hc + 1)],
                        x1T[dm][:, 512 * tg:512 * (tg + 1)],
                        [dw1[dm]] if tg == 0 else ()) for dm in range(8)]
                th = p.mmgroup(ps_[:, :], ops, deps_first=bk.wdeps(ps_.name),
                               tag=f"f0h{e}_{tg}_{hc}")
                tg_ = p.act(hsb[:, 512 * hc:512 * (hc + 1)], ps_[:, :], AF.Gelu,
                            bias=b1sb[:, 8 * hc + e:8 * hc + e + 1],
                            deps=[th, db1] + prev_pe, tag=f"f0g{e}_{tg}_{hc}")
                bk.read(ps_.name, tg_)
            for sub in range(4):
                for half in range(2):
                    ps_ = psY[(sub * 2 + half) % 2]
                    ops = [(hsb[:, 512 * hc + 128 * sub:512 * hc + 128 * (sub + 1)],
                            w2sb[:, 1024 * hc + 512 * half:1024 * hc + 512 * (half + 1)],
                            [dw2[hc]] if tg == 0 and e == 0 else ()) for hc in range(16)]
                    ty = p.mmgroup(ps_[:, :], ops,
                                   deps_first=bk.wdeps(ps_.name) + [p.fence("scalar")],
                                   tag=f"f0y{e}_{tg}_{sub}_{half}")
                    ysl = yacc[:, 1024 * (4 * tg + sub) + 512 * half:
                               1024 * (4 * tg + sub) + 512 * (half + 1)]
                    ta = p.tt(ysl, ysl, ps_[:, :], ALU.add, deps=[ty, ydep["last"]],
                              tag=f"f0a{e}_{tg}_{sub}_{half}")
                    bk.read(ps_.name, ta)
                    ydep["last"] = ta


def stage_S4(bd):
    p, C = bd.p, bd.C
    gTd = bd.din("w_f0gateT", (1024, 8))
    gbd = bd.din("w_f0gateb", (1, 8))
    rwd = bd.din("w_f0rmsw", (1, 8))
    b2d = bd.din("w_f0b2", (8, 1024))
    d_x2_own = bd.dout("d_x2_own", (HT, D))
    x1T, yacc = bd.x1T, bd.yacc
    bk = bd.bk

    gsb = bd.sb("s4_g", (128, 64), BF16)
    gbr = bd.sb("s4_gb", (1, 8), F32)
    rwr = bd.sb("s4_rw", (1, 8), F32)
    b2sb = bd.sb("s4_b2", (8, 1024), F32)
    ones8 = bd.sb("s4_on8", (8, 1), F32)
    b2row = bd.sb("s4_b2r", (1, 1024), F32)
    rwbc = bd.sb("s4_rwbc", (128, 8), F32)
    lg = bd.sb("s4_lg", (128, 8), F32)
    e2 = bd.sb("s4_e2", (128, 8), F32)
    cols = bd.sb("s4_cols", (128, 8), F32)
    x1c = [bd.sb(f"s4_x1c{i}", (128, 1024), F32) for i in range(2)]
    yb = bd.sb("s4_yb", (128, 1024), F32)
    sqt = bd.sb("s4_sqt", (128, 1024), BF16)
    x2f = bd.sb("s4_x2f", (128, 1024), F32)
    psA = bd.ps("s4psA", (128, 512))
    psB = bd.ps("s4psB", (128, 512))
    keps, kepst = bd.kcol(F32EPS)
    kb5, kb5t = bd.kcol(1e-5)

    dg = [p.dma("gpsimd", gsb[:, 8 * dm:8 * (dm + 1)], gTd[128 * dm:128 * (dm + 1), :])
          for dm in range(8)]
    dgb = p.dma("sync", gbr[:, :], gbd)
    drw = p.dma("sync", rwr[:, :], rwd)
    db2 = p.dma("sync", b2sb[:, :], b2d)
    t1s = p.memset(ones8[:, :], 1.0)
    tb1 = p.mm(psA[0:128, 0:8], C["c_ones_row_f"][:, :], rwr[:, :], True, True,
               deps=[drw] + bd.cdeps, tag="rwbc")
    tb2 = p.vcopy(rwbc[:, :], psA[0:128, 0:8], deps=[tb1])
    bk.read("s4psA", tb2)
    tb3 = p.mm(psB[0:1, 0:512], ones8[:, :], b2sb[:, 0:512], True, True,
               deps=[db2, t1s] + bk.wdeps("s4psB"), tag="b2r0")
    tb5 = p.vcopy(b2row[:, 0:512], psB[0:1, 0:512], deps=[tb3])
    tb4 = p.mm(psB[0:1, 0:512], ones8[:, :], b2sb[:, 512:1024], True, True,
               deps=[tb5], tag="b2r1")
    tb6 = p.vcopy(b2row[:, 512:1024], psB[0:1, 0:512], deps=[tb4])
    bk.read("s4psB", tb6)

    done = {}
    for tc in range(NTC):
        i = tc % 2
        dx1 = p.dma("sync", x1c[i][:, :], bd.d_x1_own[128 * tc:128 * (tc + 1), :],
                    deps=[done.get(("x1", i))])
        ops = [(x1T[dm][:, 128 * tc:128 * (tc + 1)], gsb[:, 8 * dm:8 * (dm + 1)],
                [dg[dm]] if tc == 0 else ()) for dm in range(8)]
        tm = p.mmgroup(psA[0:128, 0:8], ops, deps_first=bk.wdeps("s4psA"), tag=f"lg{tc}")
        tm2 = p.mm(psA[0:128, 0:8], C["c_ones_row_f"][0:1, :], gbr[:, :], False, True,
                   deps=[dgb], tag=f"lgb{tc}")
        te = p.vcopy(lg[:, :], psA[0:128, 0:8], deps=[tm2] + bk.wdeps("s4lg"))
        bk.read("s4psA", te)
        t4 = p.act(sqt[:, 0:8], lg[:, :], AF.Square, accum_out=cols[:, 0:1],
                   deps=[te, done.get("sq8")], tag=f"sq8{tc}")
        t5 = p.act(cols[:, 1:2], cols[:, 0:1], AF.Ln, bias=keps, scale=1.0 / 8,
                   deps=[t4, kepst])
        t6 = p.act(cols[:, 1:2], cols[:, 1:2], AF.Exp, scale=-0.5, deps=[t5])
        t7 = p.ts(lg[:, :], lg[:, :], cols[:, 1:2], None, ALU.mult, deps=[t6])
        bk.read("s4lg", t7)
        t8 = p.tt(lg[:, :], lg[:, :], rwbc[:, :], ALU.mult, deps=[t7, tb2])
        t9 = p.red(cols[:, 2:3], lg[:, :], ALU.max, deps=[t8])
        t10 = p.ts(cols[:, 2:3], cols[:, 2:3], -2.0, None, ALU.mult, deps=[t9])
        t11 = p.act(e2[:, :], lg[:, :], AF.Exp, bias=cols[:, 2:3], scale=2.0,
                    accum_out=cols[:, 3:4], deps=[t10, done.get("e2"), done.get("ng")],
                    tag=f"sm{tc}")
        done["e2"] = t11
        bk.read("s4lg", t11)
        t12 = p.ts(cols[:, 3:4], cols[:, 3:4], 1e-6, 1.0, ALU.mult, ALU.add, deps=[t11])
        t13 = p.recip(cols[:, 3:4], cols[:, 3:4], deps=[t12])
        ysl = yacc[:, 1024 * tc:1024 * (tc + 1)]
        h0 = p.mm(psA[:, 0:512], C["c_ones_row_f"][0:1, :], b2row[:, 0:512], True, True,
                  deps=[tb5] + bk.wdeps("s4psA"), tag=f"b2bc0_{tc}")
        h1 = p.mm(psB[:, 0:512], C["c_ones_row_f"][0:1, :], b2row[:, 512:1024], True, True,
                  deps=[tb6] + bk.wdeps("s4psB"), tag=f"b2bc1_{tc}")
        ta0 = p.tt(yb[:, 0:512], ysl[:, 0:512], psA[:, 0:512], ALU.add,
                   deps=[h0, done.get("yb")])
        ta1 = p.tt(yb[:, 512:1024], ysl[:, 512:1024], psB[:, 0:512], ALU.add, deps=[h1])
        bk.read("s4psA", ta0)
        bk.read("s4psB", ta1)
        ts_ = p.stt(yb[:, :], yb[:, :], cols[:, 3:4], x1c[i][:, :], ALU.mult, ALU.add,
                    deps=[ta0, ta1, t13, dx1], tag=f"comb{tc}")
        done[("x1", i)] = ts_
        done["yb"] = ts_
        done["ng"] = ts_
        l2 = p.red(cols[:, 4:5], yb[:, :], ALU.add, deps=[ts_, done.get("lncols")],
                   negate=True)
        l3 = p.ts(cols[:, 4:5], cols[:, 4:5], 1.0 / D, None, ALU.mult, deps=[l2])
        l4 = p.act(sqt[:, :], yb[:, :], AF.Square, bias=cols[:, 4:5],
                   accum_out=cols[:, 5:6], deps=[l3, done.get("sqt")])
        done["sqt"] = l4
        done["sq8"] = l4
        l5 = p.act(cols[:, 5:6], cols[:, 5:6], AF.Ln, bias=kb5, scale=1.0 / D,
                   deps=[l4, kb5t])
        l6 = p.act(cols[:, 5:6], cols[:, 5:6], AF.Exp, scale=-0.5, deps=[l5])
        l7 = p.ts(x2f[:, :], yb[:, :], cols[:, 4:5], cols[:, 5:6], ALU.add, ALU.mult,
                  deps=[l6, done.get("x2f")], tag=f"ln2_{tc}")
        done["lncols"] = l7
        d2 = p.dma("sync", d_x2_own[128 * tc:128 * (tc + 1), :], x2f[:, :], deps=[l7])
        done["x2f"] = d2


def stage_Ca(bd):
    p, C = bd.p, bd.C
    i_x2T = bd.din("i_x2T", (D, T))
    w_conv = bd.din("w_conv", (128, 32))
    w_gT = bd.din("w_l1gT", (D, D))
    w_gb = bd.din("w_l1gb", (128, 8))
    uT = bd.uT
    bk = bd.bk
    keps, kepst = bd.kcol(1e-6)

    xb = [bd.sb(f"ca_xb{i}", (128, 512), BF16) for i in range(2)]
    xnp = [bd.sb(f"ca_xnp{i}", (128, T + 4), BF16) for i in range(8)]
    gwT = bd.sb("ca_gwT", (128, 8 * 1024), BF16)
    gbc = bd.sb("ca_gbc", (128, 8), F32)
    cwc = bd.sb("ca_cw", (128, 32), F32)
    sq = bd.sb("ca_sq", (128, 512), BF16)
    rrow = bd.sb("ca_rrow", (1, T), F32)
    gsig = bd.sb("ca_gsig", (128, 512), BF16)
    cvf = bd.sb("ca_cvf", (128, 512), F32)
    cvs = bd.sb("ca_cvs", (128, 512), BF16)
    psA = bd.ps("capsA", (128, 512))
    psB = bd.ps("capsB", (128, 512))
    psR = bd.ps("capsR", (1, 512))

    dgw = [p.dma("gpsimd", gwT[:, 1024 * i:1024 * (i + 1)], w_gT[128 * i:128 * (i + 1), :])
           for i in range(8)]
    dgb = p.dma("sync", gbc[:, :], w_gb)
    dcw = p.dma("sync", cwc[:, :], w_conv)
    tpad = [p.memset(xnp[i][:, 0:4], 0.0) for i in range(8)]

    for tc in range(8):
        mmlast = None
        for dm in range(8):
            ib = dm % 2
            dx = p.dma("gpsimd", xb[ib][:, :],
                       i_x2T[128 * dm:128 * (dm + 1), 512 * tc:512 * (tc + 1)],
                       deps=bk.wdeps(("ca_xb", ib)), tag=f"x2p1_{tc}_{dm}")
            tq = p.act(sq[:, :], xb[ib][:, :], AF.Square,
                       deps=[dx] + bk.wdeps("ca_sq") + (bd.cdeps if dm == 0 and tc == 0 else []),
                       tag=f"sq{tc}_{dm}")
            bk.read(("ca_xb", ib), tq)
            mmlast = p.mm(psR[:, :], C["c_ones_col_b"][:, :], sq[:, :], dm == 0, dm == 7,
                          deps=[tq] + (bk.wdeps("capsR") if dm == 0 else []),
                          tag=f"ssq{tc}_{dm}")
            bk.read("ca_sq", mmlast)
        t1 = p.act(rrow[:, 512 * tc:512 * (tc + 1)], psR[:, :], AF.Ln, bias=keps[0:1, :],
                   scale=1.0 / D, deps=[mmlast, kepst])
        bk.read("capsR", t1)
        t2 = p.act(rrow[:, 512 * tc:512 * (tc + 1)], rrow[:, 512 * tc:512 * (tc + 1)],
                   AF.Exp, scale=-0.5, deps=[t1], tag=f"rstd{tc}")
    rstd_done = p.fence("scalar")

    for dm in range(8):
        for tc in range(8):
            ib = (dm * 8 + tc) % 2
            dx = p.dma("gpsimd", xb[ib][:, :],
                       i_x2T[128 * dm:128 * (dm + 1), 512 * tc:512 * (tc + 1)],
                       deps=bk.wdeps(("ca_xb", ib)), tag=f"x2p2_{dm}_{tc}")
            ps_ = psA if tc % 2 == 0 else psB
            tb = p.mm(ps_[:, :], C["c_ones_row_f"][:, :],
                      rrow[0:1, 512 * tc:512 * (tc + 1)], True, True,
                      deps=[rstd_done] + bk.wdeps(ps_.name), tag=f"rbc{dm}_{tc}")
            tx = p.tt(xnp[dm][:, 4 + 512 * tc:4 + 512 * (tc + 1)],
                      xb[ib][:, :], ps_[:, :], ALU.mult,
                      deps=[tb, dx], tag=f"xn{dm}_{tc}")
            bk.read(ps_.name, tx)
            bk.read(("ca_xb", ib), tx)

    xnp_done = [p.fence("vector")] + tpad
    for dm in range(8):
        for tc in range(8):
            gdeps = xnp_done
            ps_ = psA if tc % 2 == 0 else psB
            ops = [(gwT[:, 1024 * dmi + 128 * dm:1024 * dmi + 128 * (dm + 1)],
                    xnp[dmi][:, 4 + 512 * tc:4 + 512 * (tc + 1)],
                    [dgw[dmi]] if tc == 0 else ()) for dmi in range(8)]
            tgm = p.mmgroup(ps_[:, :], ops, deps_first=bk.wdeps(ps_.name) + gdeps,
                            tag=f"gate{dm}_{tc}")
            tsg = p.act(gsig[:, :], ps_[:, :], AF.Sigmoid, bias=gbc[:, dm:dm + 1],
                        deps=[tgm, dgb] + bk.wdeps("ca_gsig"), tag=f"sig{dm}_{tc}")
            bk.read(ps_.name, tsg)
            base = 4 + 512 * tc
            tcv = p.ts(cvf[:, :], xnp[dm][:, base - 3:base - 3 + 512], cwc[:, 4 * dm:4 * dm + 1],
                       None, ALU.mult, deps=[dcw] + gdeps + bk.wdeps("ca_cvf"),
                       tag=f"cv0_{dm}_{tc}")
            for j in range(1, 4):
                tcv = p.stt(cvf[:, :], xnp[dm][:, base - 3 + j:base - 3 + j + 512],
                            cwc[:, 4 * dm + j:4 * dm + j + 1], cvf[:, :], ALU.mult, ALU.add,
                            deps=[tcv], tag=f"cv{j}_{dm}_{tc}")
            tsl = p.act(cvs[:, :], cvf[:, :], AF.Silu, deps=[tcv] + bk.wdeps("ca_cvs"))
            bk.read("ca_cvf", tsl)
            tu = p.tt(uT[dm][:, 512 * tc:512 * (tc + 1)], cvs[:, :], gsig[:, :], ALU.mult,
                      deps=[tsl, tsg], tag=f"u{dm}_{tc}")
            bk.read("ca_cvs", tu)
            bk.read("ca_gsig", tu)


def stage_Cb(bd):
    p, C = bd.p, bd.C
    w_keysT = bd.din("w_keysT", (D, 96))
    w_bitsT = bd.din("w_bitsT", (D, 36))
    w_powR = bd.din("w_powR", (36, 1))
    w_powW = bd.din("w_powW", (36, 1))
    w_g3T = bd.din("w_g3T", (D, 3))
    w_g3b = bd.din("w_g3b", (1, 3))
    d_rows = bd.dout("d_rows", (2, T))
    d_wgl = bd.dint("d_wgl", (2, T))
    uT, bk = bd.uT, bd.bk

    kT = bd.sb("cb_kT", (128, 8 * 96), BF16)
    bT = bd.sb("cb_bT", (128, 8 * 36), BF16)
    g3 = bd.sb("cb_g3", (128, 8 * 3), BF16)
    powR = bd.sb("cb_powR", (36, 1), BF16)
    powW = bd.sb("cb_powW", (36, 1), BF16)
    g3b = bd.sb("cb_g3b", (1, 3), F32)
    bits01 = bd.sb("cb_b01", (36, 512), BF16)
    sg3 = bd.sb("cb_sg3", (128, 512), F32)
    zrow = bd.sb("cb_zr", (1, 512), F32)
    lrow = bd.sb("cb_lr", (1, 512), F32)
    trow = bd.sb("cb_tr", (1, 512), F32)
    psA = bd.ps("cbpsA", (128, 512))
    psB = bd.ps("cbpsB", (128, 512))

    dk = [p.dma("gpsimd", kT[:, 96 * i:96 * (i + 1)], w_keysT[128 * i:128 * (i + 1), :])
          for i in range(8)]
    db = [p.dma("gpsimd", bT[:, 36 * i:36 * (i + 1)], w_bitsT[128 * i:128 * (i + 1), :])
          for i in range(8)]
    dg3 = [p.dma("gpsimd", g3[:, 3 * i:3 * (i + 1)], w_g3T[128 * i:128 * (i + 1), :])
           for i in range(8)]
    dpR = p.dma("gpsimd", powR[:, :], w_powR)
    dpW = p.dma("gpsimd", powW[:, :], w_powW)
    dgb = p.dma("sync", g3b[:, :], w_g3b)

    for tc in range(8):
        blk = slice(512 * tc, 512 * (tc + 1))
        ops = [(kT[:, 96 * dm:96 * (dm + 1)], uT[dm][:, blk], [dk[dm]]) for dm in range(8)]
        tk = p.mmgroup(psA[0:96, :], ops, deps_first=bk.wdeps("cbpsA") + [p.fence("vector")],
                       tag=f"keys{tc}")
        e1 = p.act(bd.qkr[:, blk], psA[0:32, :], AF.Copy, scale=float(1.0 / np.sqrt(32.0)),
                   deps=[tk])
        e2 = p.scopy(bd.wkr[:, blk], psA[32:64, :], deps=[tk])
        e3 = p.scopy(bd.wvr[:, blk], psA[64:96, :], deps=[tk])
        for t_ in (e1, e2, e3):
            bk.read("cbpsA", t_)
        ops = [(bT[:, 36 * dm:36 * (dm + 1)], uT[dm][:, blk], [db[dm]]) for dm in range(8)]
        tb = p.mmgroup(psB[0:36, :], ops, deps_first=bk.wdeps("cbpsB"), tag=f"bits{tc}")
        t01 = p.ts(bits01[:, :], psB[0:36, :], 0.0, None, ALU.is_gt,
                   deps=[tb] + bk.wdeps("cb_b01"), tag=f"b01_{tc}")
        bk.read("cbpsB", t01)
        tR = p.mm(psA[0:1, :], powR[:, :], bits01[:, :], True, True,
                  deps=[dpR, t01] + bk.wdeps("cbpsA"), tag=f"ri{tc}")
        eR = p.vcopy(zrow[:, :], psA[0:1, :], deps=[tR] + bk.wdeps("cb_zr"))
        bk.read("cbpsA", eR)
        dR = p.dma("sync", bd.d_idx[0:1, blk], zrow[:, :], deps=[eR])
        tW = p.mm(psB[0:1, :], powW[:, :], bits01[:, :], True, True,
                  deps=[dpW] + bk.wdeps("cbpsB"), tag=f"wi{tc}")
        eW = p.vcopy(trow[:, :], psB[0:1, :], deps=[tW] + bk.wdeps("cb_tr"))
        bk.read("cbpsB", eW)
        dW = p.dma("sync", bd.d_idx[1:2, blk], trow[:, :], deps=[eW])
        bk.read("cb_zr", dR)
        bk.read("cb_tr", dW)
        bk.read("cb_b01", tW)
        ops = [(g3[:, 3 * dm:3 * (dm + 1)], uT[dm][:, blk], [dg3[dm]]) for dm in range(8)]
        tg3 = p.mmgroup(psA[0:3, :], ops, deps_first=bk.wdeps("cbpsA"), tag=f"g3{tc}")
        tg3b = p.mm(psA[0:3, :], g3b[:, :], C["c_ones_row512f"][:, :], False, True,
                    deps=[dgb], tag=f"g3b{tc}")
        tz = p.vcopy(zrow[:, :], psA[0:1, :], deps=[tg3b] + bk.wdeps("cb_zr"))
        tsg = p.act(sg3[0:3, :], psA[0:3, :], AF.Sigmoid,
                    deps=[tg3b] + bk.wdeps("cb_sg3"), tag=f"sg3{tc}")
        bk.read("cbpsA", tz)
        bk.read("cbpsA", tsg)
        dw1 = p.dma("sync", d_wgl[0:1, blk], sg3[0:1, :], deps=[tsg])
        dr1 = p.dma("sync", d_rows[0:1, blk], sg3[1:2, :], deps=[tsg])
        dr2 = p.dma("sync", d_rows[1:2, blk], sg3[2:3, :], deps=[tsg])
        bk.read("cb_sg3", dw1)
        bk.read("cb_sg3", dr1)
        bk.read("cb_sg3", dr2)
        l1 = p.act(lrow[:, :], zrow[:, :], AF.Abs, deps=[tz] + bk.wdeps("cb_lr"))
        l2 = p.act(lrow[:, :], lrow[:, :], AF.Exp, scale=-1.0, deps=[l1])
        kb1, kb1t = bd.kcol(1.0)
        l3 = p.act(lrow[:, :], lrow[:, :], AF.Ln, bias=kb1[0:1, :], deps=[l2, kb1t])
        l4 = p.ts(trow[:, :], zrow[:, :], 0.0, None, ALU.max, deps=[tz] + bk.wdeps("cb_tr"))
        bk.read("cb_zr", l4)
        l5 = p.tt(lrow[:, :], lrow[:, :], trow[:, :], ALU.add, deps=[l3, l4])
        bk.read("cb_tr", l5)
        l6 = p.ts(lrow[:, :], lrow[:, :], -1.0, -30.0, ALU.mult, ALU.max, deps=[l5])
        dw2 = p.dma("sync", d_wgl[1:2, blk], lrow[:, :], deps=[l6])
        bk.read("cb_lr", dw2)
    bd.d_rows = d_rows
    bd.d_wgl = d_wgl


def stage_Cc(bd):
    p, C = bd.p, bd.C
    w_sinT = bd.din("w_sinT", (D, 2048))
    w_soT = bd.din("w_soT", (2048, D))
    w_dec = bd.din("w_mydecay", (1, 2))
    d_s = bd.dint("d_s", (2048, T), BF16)
    d_longpart = bd.dout("d_longpart", (T, D))
    uT, bk = bd.uT, bd.bk

    sinb = [bd.sb(f"cc_sin{i}", (128, 1024), BF16) for i in range(2)]
    sob = bd.sb("cc_sob", (128, 16 * 1024), BF16)
    inpf = bd.sb("cc_inpf", (128, T), F32)
    sbf = bd.sb("cc_sbf", (128, T), BF16)
    sf = bd.sb("cc_sf", (128, T), F32)
    decrow = bd.sb("cc_dec", (1, 2), F32)
    dcol = bd.sb("cc_dcol", (128, 2), F32)
    dtile = [bd.sb(f"cc_dt{i}", (128, 512), F32) for i in range(2)]
    sblk = bd.sb("cc_sblk", (128, 16 * 512), BF16)
    stg = [bd.sb(f"cc_stg{i}", (128, 1024), F32) for i in range(2)]
    psA = bd.ps("ccpsA", (128, 512))
    psB = bd.ps("ccpsB", (128, 512))
    psC = bd.ps("ccpsC", (128, 512))

    dd = p.dma("sync", decrow[:, :], w_dec)
    t0 = p.act(decrow[:, :], decrow[:, :], AF.Sigmoid, deps=[dd])
    for kl in range(2):
        tb = p.mm(psC[:, 0:1], C["c_ones_row_f"][:, :], decrow[0:1, kl:kl + 1], True, True,
                  deps=[t0] + bk.wdeps("ccpsC"), tag=f"decbc{kl}")
        tc_ = p.vcopy(dcol[:, kl:kl + 1], psC[:, 0:1], deps=[tb])
        bk.read("ccpsC", tc_)
        tm = p.memset(dtile[kl][:, :], 1.0)
        tt_ = p.ts(dtile[kl][:, :], dtile[kl][:, :], dcol[:, kl:kl + 1], None, ALU.mult,
                   deps=[tc_, tm], tag=f"dtile{kl}")
        bk.read(f"cc_dt{kl}", tt_)

    sdone = []
    for kd in range(16):
        kl = kd // 8
        ib = kd % 2
        wdeps = bk.wdeps(f"cc_sin{ib}")
        dsin = [p.dma("gpsimd", sinb[ib][:, 128 * dm:128 * (dm + 1)],
                      w_sinT[128 * dm:128 * (dm + 1), 128 * kd:128 * (kd + 1)],
                      deps=wdeps if dm == 0 else ()) for dm in range(8)]
        scant = None
        for tc in range(8):
            ps_ = psA if tc % 2 == 0 else psB
            ops = [(sinb[ib][:, 128 * dm:128 * (dm + 1)], uT[dm][:, 512 * tc:512 * (tc + 1)],
                    [dsin[dm]]) for dm in range(8)]
            tm = p.mmgroup(ps_[:, :], ops, deps_first=bk.wdeps(ps_.name), tag=f"inp{kd}_{tc}")
            bk.read(f"cc_sin{ib}", tm)
            te = p.vcopy(inpf[:, 512 * tc:512 * (tc + 1)], ps_[:, :],
                         deps=[tm] + (bk.wdeps("cc_inpf") if tc == 0 else []))
            bk.read(ps_.name, te)
            init = 0.0 if tc == 0 else sf[:, 512 * tc - 1:512 * tc]
            scant = p.scan(sf[:, 512 * tc:512 * (tc + 1)], dtile[kl][:, :],
                           inpf[:, 512 * tc:512 * (tc + 1)], init,
                           deps=[te, scant] + (bk.wdeps("cc_sf") if tc == 0 else []),
                           tag=f"scan{kd}_{tc}")
            bk.read("cc_inpf", scant)
        tcast = p.vcopy(sbf[:, :], sf[:, :], deps=[scant] + bk.wdeps("cc_sbf"))
        bk.read("cc_sf", tcast)
        ds = p.dma("sync", d_s[128 * kd:128 * (kd + 1), :], sbf[:, :], deps=[tcast])
        bk.read("cc_sbf", ds)
        sdone.append(ds)
    dso = [p.dma("gpsimd", sob[:, 1024 * i:1024 * (i + 1)], w_soT[128 * i:128 * (i + 1), :])
           for i in range(16)]
    for tg in range(8):
        dblk = [p.dma("sync", sblk[:, 512 * kd:512 * (kd + 1)],
                      d_s[128 * kd:128 * (kd + 1), 512 * tg:512 * (tg + 1)],
                      deps=[sdone[kd]] + (bk.wdeps("cc_sblk") if kd == 0 else []))
                for kd in range(16)]
        for sub in range(4):
            sg = stg[sub % 2]
            sgdeps = bk.wdeps((f"cc_stg{sub % 2}",))
            for half in range(2):
                ps_ = psA if half == 0 else psB
                ops = [(sblk[:, 512 * kd + 128 * sub:512 * kd + 128 * (sub + 1)],
                        sob[:, 1024 * kd + 512 * half:1024 * kd + 512 * (half + 1)],
                        [dblk[kd], dso[kd]]) for kd in range(16)]
                tm = p.mmgroup(ps_[:, :], ops, deps_first=bk.wdeps(ps_.name),
                               tag=f"lp{tg}_{sub}_{half}")
                if sub == 3 and half == 1:
                    for kd in range(16):
                        bk.read("cc_sblk", tm)
                te = p.scopy(sg[:, 512 * half:512 * (half + 1)], ps_[:, :],
                             deps=[tm] + sgdeps)
                bk.read(ps_.name, te)
                sgdeps = []
            dl = p.dma("sync", d_longpart[512 * tg + 128 * sub:512 * tg + 128 * (sub + 1), :],
                       sg[:, :], deps=[p.fence("scalar")], tag=f"lout{tg}_{sub}")
            bk.read((f"cc_stg{sub % 2}",), dl)


def stage_Cd(bd):
    p, C = bd.p, bd.C
    d_mem = bd.dout("d_mem", (T, 66))
    d_wgl = bd.d_wgl
    bk = bd.bk
    BIG = 30000.0
    MAGIC = 12582912.0

    MKs = bd.sb("cd_MK", (128, 256), F32)
    MVs = bd.sb("cd_MV", (128, 256), F32)
    CNT = bd.sb("cd_CNT", (128, 4), F32)
    RTf = bd.sb("cd_RTf", (128, 512), F32)
    RTb = bd.sb("cd_RTb", (128, 512), BF16)
    WTf = bd.sb("cd_WTf", (128, 512), F32)
    WTb = bd.sb("cd_WTb", (128, 512), BF16)
    Wtf = bd.sb("cd_Wtf", (128, 512), F32)
    rib = bd.sb("cd_rib", (128, 128), F32)
    rirow = bd.sb("cd_rirow", (1, 128), F32)
    wirow = bd.sb("cd_wirow", (1, 128), F32)
    wglrow = bd.sb("cd_wglrow", (2, 128), F32)
    cols = bd.sb("cd_cols", (128, 16), F32)
    Wsc = bd.sb("cd_Wsc", (128, 512), F32)
    Lcs = bd.sb("cd_Lcs", (128, 512), F32)
    Lci = bd.sb("cd_Lci", (128, 512), F32)
    prod = bd.sb("cd_prod", (128, 128), F32)
    Dsb = bd.sb("cd_D", (128, 128), F32)
    Mful = bd.sb("cd_Mf", (128, 128), F32)
    expd = bd.sb("cd_expd", (128, 128), F32)
    P2b = bd.sb("cd_P2", (128, 128), BF16)
    arow = bd.sb("cd_arow", (1, 128), F32)
    kchT = bd.sb("cd_kchT", (128, 3 * 32), F32)
    wvTb = bd.sb("cd_wvTb", (128, 32), BF16)
    Wsc2 = bd.sb("cd_Wsc2", (128, 512), F32)
    vstate = bd.sb("cd_vst", (128, 32), F32)
    stream = bd.sb("cd_strm", (128, 66), F32)
    psE = bd.ps("cdpsE", (128, 128))
    psF = bd.ps("cdpsF", (128, 128))
    psG = bd.ps("cdpsG", (128, 128))
    psH = bd.ps("cdpsH", (128, 128))
    psQ = bd.ps("cdpsQ", (128, 128))
    psW = bd.ps("cdpsW", (128, 2))
    psT = bd.ps("cdpsT", (128, 128), BF16)
    psT2 = bd.ps("cdpsT2", (128, 32), BF16)

    tz = [p.memset(MKs[:, :], 0.0), p.memset(MVs[:, :], 0.0), p.memset(CNT[:, :], 0.0)]
    tz2z = p.memset(stream[:, :], 0.0)
    zt = p.fence("gpsimd")

    for ch in range(CD_CHUNKS):
        blk = slice(128 * ch, 128 * (ch + 1))
        chf = [p.fence("tensor"), p.fence("vector"), p.fence("scalar")]
        dgr = p.dma("sync", wglrow[:, :], d_wgl[:, blk], deps=chf + bk.wdeps("cd_wglrow"))
        tgt = p.tr(psW[:, :], wglrow[:, :], C["c_identf"][0:2, 0:2],
                   deps=[dgr] + bk.wdeps("cdpsW"), tag=f"wglT{ch}")
        bk.read("cd_wglrow", tgt)
        dg = p.vcopy(cols[:, 12:14], psW[:, :], deps=[tgt])
        bk.read("cdpsW", dg)
        dl = dg
        dri = p.dma("sync", rirow[:, :], bd.d_idx[0:1, blk], deps=chf)
        dwi = p.dma("sync", wirow[:, :], bd.d_idx[1:2, blk], deps=chf)
        t1 = p.mm(psE[:, :], C["c_ones_row_f"][:, :], rirow[:, :], True, True,
                  deps=chf + [dri] + bk.wdeps("cdpsE") + ([zt] if ch == 0 else []), tag=f"ribm{ch}")
        t2 = p.vcopy(rib[:, :], psE[:, :], deps=[t1] + bk.wdeps("cd_rib"))
        bk.read("cdpsE", t2)
        tr = []
        for g in range(4):
            tr.append(p.ts(RTf[:, 128 * g:128 * (g + 1)], rib[:, :], C["c_iotac"][:, g:g + 1],
                           None, ALU.is_equal, deps=[t2] + bk.wdeps(("cd_RTf", g)), tag=f"RT{ch}_{g}"))
        t3 = p.mm(psE[:, :], C["c_ones_row_f"][:, :], wirow[:, :], True, True,
                  deps=[dwi] + bk.wdeps("cdpsE"), tag=f"wibm{ch}")
        t4 = p.vcopy(rib[:, :], psE[:, :], deps=[t3] + tr)
        bk.read("cdpsE", t4)
        tw = []
        for g in range(4):
            tw.append(p.ts(WTf[:, 128 * g:128 * (g + 1)], rib[:, :], C["c_iotac"][:, g:g + 1],
                           None, ALU.is_equal, deps=[t4] + bk.wdeps(("cd_WTf", g)), tag=f"WT{ch}_{g}"))
        bk.read("cd_rib", tw[-1])
        tcb = p.vcopy(RTb[:, :], RTf[:, :], deps=tr + bk.wdeps("cd_RTb"))
        twb = p.vcopy(WTb[:, :], WTf[:, :], deps=tw + bk.wdeps("cd_WTb"))
        twt = []
        for g in range(4):
            a_ = p.tr(psT[:, 0:128], WTb[:, 128 * g:128 * (g + 1)], C["c_ident"][:, :],
                      deps=[twb] + chf + bk.wdeps("cdpsT"), tag=f"Wt{ch}_{g}")
            b_ = p.vcopy(Wtf[:, 128 * g:128 * (g + 1)], psT[:, 0:128],
                         deps=[a_] + bk.wdeps(("cd_Wtf", g)))
            bk.read("cdpsT", b_)
            twt.append(b_)
        tks = []
        for j, rowt in enumerate((bd.qkr, bd.wkr, bd.wvr)):
            a_ = p.tr(psT2[:, :], rowt[:, blk],
                      C["c_ident"][0:32, 0:32], deps=chf + bk.wdeps("cdpsT2"), tag=f"kT{ch}_{j}")
            b_ = p.vcopy(kchT[:, 32 * j:32 * (j + 1)], psT2[:, :],
                         deps=[a_] + bk.wdeps(("cd_kchT", j)))
            bk.read("cdpsT2", b_)
            tks.append(b_)
        tvb = p.vcopy(wvTb[:, :], kchT[:, 64:96], deps=[tks[2]] + bk.wdeps("cd_wvTb"))
        bk.read(("cd_kchT", 2), tvb)
        if CD_LEVEL < 2:
            dmz = p.dma("sync", d_mem[blk, :], stream[:, :], deps=[tz2z] + [tvb, twb, tcb] + twt + tks + [dg, c1z_dummy] if False else [tz2z, tvb, twb, tcb, dg] + twt + tks)
            continue
        cw = None
        for g in range(4):
            cw = p.mm(psF[:, 0:1], WTf[:, 128 * g:128 * (g + 1)], CNT[:, g:g + 1],
                      g == 0, g == 3, deps=([twb] if g == 0 else []) + (bk.wdeps("cdpsF") if g == 0 else []),
                      tag=f"cntw{ch}_{g}")
        c1 = p.vcopy(cols[:, 0:1], psF[:, 0:1], deps=[cw])
        bk.read("cdpsF", c1)
        eq = None
        for g in range(4):
            eq = p.mm(psF[:, 0:128], WTb[:, 128 * g:128 * (g + 1)], WTb[:, 128 * g:128 * (g + 1)],
                      g == 0, g == 3, deps=[c1] if g == 0 else [], tag=f"eqw{ch}_{g}")
        m1 = p.tt(prod[:, :], psF[:, 0:128], C["c_su_f"][:, :], ALU.mult,
                  deps=[eq] + bk.wdeps("cd_prod"), tag=f"mip{ch}")
        bk.read("cdpsF", m1)
        m2 = p.mm(psF[:, 0:1], prod[:, :], C["c_ones_col_f"][:, :], True, True,
                  deps=[m1] + bk.wdeps("cdpsF"), tag=f"ipc{ch}")
        bk.read("cd_prod", m2)
        m3 = p.vcopy(cols[:, 1:2], psF[:, 0:1], deps=[m2])
        bk.read("cdpsF", m3)
        m4 = p.tt(cols[:, 2:3], cols[:, 0:1], cols[:, 1:2], ALU.add, deps=[m3, c1])
        m5 = p.ts(cols[:, 2:3], cols[:, 2:3], 0.5, None, ALU.mult, deps=[m4])
        m6 = p.ts(cols[:, 3:4], cols[:, 2:3], MAGIC, None, ALU.add, deps=[m5])
        m7 = p.ts(cols[:, 3:4], cols[:, 3:4], MAGIC, None, ALU.subtract, deps=[m6])
        m8 = p.tt(cols[:, 4:5], cols[:, 2:3], cols[:, 3:4], ALU.subtract, deps=[m7])
        m9 = p.ts(cols[:, 4:5], cols[:, 4:5], 2.0, None, ALU.mult, deps=[m8])
        m10 = p.tt(cols[:, 4:5], cols[:, 4:5], cols[:, 4:5], ALU.mult, deps=[m9],
                   tag=f"sig{ch}")
        if CD_LEVEL < 3:
            dmz = p.dma("sync", d_mem[blk, :], stream[:, :], deps=[tz2z, m10])
            continue
        eqr = None
        for g in range(4):
            eqr = p.mm(psG[:, :], WTb[:, 128 * g:128 * (g + 1)], RTb[:, 128 * g:128 * (g + 1)],
                       g == 0, g == 3, deps=([tcb, twb] + bk.wdeps("cdpsG")) if g == 0 else [],
                       tag=f"eqr{ch}_{g}")
        qw = p.mm(psQ[:, :], bd.wkr[0:32, blk], bd.qkr[0:32, blk], True, True,
                  deps=bk.wdeps("cdpsQ"), tag=f"qw{ch}")
        scv = []
        for s_ in range(2):
            if s_ == 0:
                pp = p.ts(cols[:, 5:6], cols[:, 4:5], -1.0, 1.0, ALU.mult, ALU.add, deps=[m10])
            else:
                pp = p.vcopy(cols[:, 5:6], cols[:, 4:5], deps=[m10, scv[-1]])
            pl = p.tt(cols[:, 6:7], cols[:, 5:6], cols[:, 13:14], ALU.mult, deps=[pp, dl])
            ws = p.ts(Wsc[:, :], Wtf[:, :], cols[:, 6:7], None, ALU.mult,
                      deps=[pl] + twt + bk.wdeps("cd_Wsc"), tag=f"Wsc{ch}_{s_}")
            lt = []
            for g in range(4):
                a_ = p.mm(psE[:, :], Wsc[:, 128 * g:128 * (g + 1)], C["c_su_f"][:, :],
                          True, True, deps=[ws] + bk.wdeps("cdpsE"), tag=f"lcs{ch}_{s_}_{g}")
                b_ = p.vcopy(Lcs[:, 128 * g:128 * (g + 1)], psE[:, :],
                             deps=[a_] + bk.wdeps(("cd_Lcs", g)))
                bk.read("cdpsE", b_)
                a2 = p.mm(psE[:, :], Wsc[:, 128 * g:128 * (g + 1)], C["c_uti_f"][:, :],
                          True, True, deps=[b_] + bk.wdeps("cdpsE"), tag=f"lci{ch}_{s_}_{g}")
                b2 = p.vcopy(Lci[:, 128 * g:128 * (g + 1)], psE[:, :],
                             deps=[a2] + bk.wdeps(("cd_Lci", g)))
                bk.read("cdpsE", b2)
                lt.append((b_, b2))
            bk.read("cd_Wsc", lt[-1][1])
            am = None
            am2 = None
            for g in range(4):
                pr_ = p.tt(prod[:, :], RTf[:, 128 * g:128 * (g + 1)], Lcs[:, 128 * g:128 * (g + 1)],
                           ALU.mult, deps=[lt[g][0], tr[g]] + bk.wdeps("cd_prod"), tag=f"apr{ch}_{s_}_{g}")
                am = p.mm(psF[:, 0:1], prod[:, :], C["c_ones_col_f"][:, :], g == 0, g == 3,
                          deps=[pr_] + (bk.wdeps("cdpsF") if g == 0 else []))
                am2 = p.mm(psH[0:1, :], C["c_ones_col_f"][:, :], prod[:, :],
                           g == 0, g == 3, deps=[] if g else bk.wdeps("cdpsH"),
                           tag=f"arow{ch}_{s_}_{g}")
                bk.read("cd_prod", am2)
                bk.read(("cd_Lcs", g), am)
            a1 = p.vcopy(cols[:, 7:8], psF[:, 0:1], deps=[am])
            bk.read("cdpsF", a1)
            a2r = p.vcopy(arow[:, :], psH[0:1, :], deps=[am2] + bk.wdeps("cd_arow"))
            bk.read("cdpsH", a2r)
            aex = p.act(cols[:, 8:9], cols[:, 7:8], AF.Exp, deps=[a1])
            bm = None
            for g in range(4):
                pr_ = p.tt(prod[:, :], WTf[:, 128 * g:128 * (g + 1)], Lci[:, 128 * g:128 * (g + 1)],
                           ALU.mult, deps=[lt[g][1], tw[g]] + bk.wdeps("cd_prod"), tag=f"bpr{ch}_{s_}_{g}")
                bm = p.mm(psF[:, 0:1], prod[:, :], C["c_ones_col_f"][:, :], g == 0, g == 3,
                          deps=[pr_] + (bk.wdeps("cdpsF") if g == 0 else []))
                bk.read("cd_prod", bm)
            b1 = p.vcopy(cols[:, 9:10], psF[:, 0:1], deps=[bm])
            bk.read("cdpsF", b1)
            em = None
            for g in range(4):
                em = p.mm(psF[:, 0:1], WTf[:, 128 * g:128 * (g + 1)],
                          Lci[:, 128 * g + 127:128 * (g + 1)], g == 0, g == 3,
                          deps=([b1] + bk.wdeps("cdpsF")) if g == 0 else [],
                          tag=f"em{ch}_{s_}_{g}")
                bk.read(("cd_Lci", g), em)
            e1 = p.vcopy(cols[:, 10:11], psF[:, 0:1], deps=[em])
            bk.read("cdpsF", e1)
            d1 = p.tt(cols[:, 11:12], cols[:, 10:11], cols[:, 9:10], ALU.subtract,
                      deps=[e1, b1])
            d2 = p.act(cols[:, 11:12], cols[:, 11:12], AF.Exp, deps=[d1])
            d3 = p.tt(cols[:, 11:12], cols[:, 11:12], cols[:, 12:13], ALU.mult,
                      deps=[d2, dg])
            d4 = p.tt(cols[:, 11:12], cols[:, 11:12], cols[:, 5:6], ALU.mult, deps=[d3])
            ks = None
            for g in range(4):
                ks = p.mm(psF[:, 0:32], RTf[:, 128 * g:128 * (g + 1)],
                          MKs[:, 32 * (4 * s_ + g):32 * (4 * s_ + g + 1)], g == 0, g == 3,
                          deps=(bk.wdeps("cdpsF") if g == 0 else []), tag=f"ks{ch}_{s_}_{g}")
            p1 = p.tt(prod[:, 0:32], psF[:, 0:32], kchT[:, 0:32], ALU.mult,
                      deps=[ks, tks[0]] + bk.wdeps("cd_prod"))
            bk.read("cdpsF", p1)
            p2 = p.red(cols[:, 14:15], prod[:, 0:32], ALU.add, deps=[p1])
            bk.read("cd_prod", p2)
            p3 = p.tt(cols[:, 14:15], cols[:, 14:15], cols[:, 8:9], ALU.mult, deps=[p2, aex])
            vs = None
            for g in range(4):
                vs = p.mm(psF[:, 0:32], RTf[:, 128 * g:128 * (g + 1)],
                          MVs[:, 32 * (4 * s_ + g):32 * (4 * s_ + g + 1)], g == 0, g == 3,
                          deps=([p1] + bk.wdeps("cdpsF")) if g == 0 else [], tag=f"vs{ch}_{s_}_{g}")
            v1 = p.ts(vstate[:, :], psF[:, 0:32], cols[:, 8:9], None, ALU.mult,
                      deps=[vs, aex] + bk.wdeps("cd_vst"))
            bk.read("cdpsF", v1)
            db_ = p.mm(psE[:, :], C["c_ones_row_f"][:, :], arow[:, :], True, True,
                       deps=[a2r] + bk.wdeps("cdpsE"), tag=f"dbc{ch}_{s_}")
            bk.read("cd_arow", db_)
            dd_ = p.ts(Dsb[:, :], psE[:, :], cols[:, 9:10], None, ALU.subtract,
                       deps=[db_, b1] + bk.wdeps("cd_D"))
            bk.read("cdpsE", dd_)
            mf = p.stt(Mful[:, :], psG[:, :], cols[:, 5:6], C["c_su_f"][:, :],
                       ALU.mult, ALU.mult, deps=[eqr, pp] + bk.wdeps("cd_Mf"), tag=f"mf{ch}_{s_}")
            m1_ = p.stt(Dsb[:, :], Mful[:, :], BIG, Dsb[:, :], ALU.mult, ALU.add,
                        deps=[mf, dd_])
            bk.read("cd_Mf", m1_)
            m2_ = p.ts(Dsb[:, :], Dsb[:, :], -BIG, None, ALU.add, deps=[m1_])
            ex_ = p.act(expd[:, :], Dsb[:, :], AF.Exp, deps=[m2_] + bk.wdeps("cd_expd"))
            bk.read("cd_D", ex_)
            pb_ = p.ts(P2b[:, :], expd[:, :], cols[:, 12:13], None, ALU.mult,
                       deps=[ex_, dg] + bk.wdeps("cd_P2"))
            bk.read("cd_expd", pb_)
            pq = p.tt(prod[:, :], psQ[:, :], P2b[:, :], ALU.mult,
                      deps=[qw, pb_] + bk.wdeps("cd_prod"), tag=f"p2qw{ch}_{s_}")
            sci = p.mm(psF[:, 0:1], prod[:, :], C["c_ones_col_f"][:, :], True, True,
                       deps=[pq] + bk.wdeps("cdpsF"), tag=f"sci{ch}_{s_}")
            bk.read("cd_prod", sci)
            sc1 = p.tt(stream[:, s_:s_ + 1], psF[:, 0:1], cols[:, 14:15], ALU.add,
                       deps=[sci, p3] + bk.wdeps(("cd_strm", s_)))
            bk.read("cdpsF", sc1)
            vi = p.mm(psF[:, 0:32], P2b[:, :], wvTb[:, :], True, True,
                      deps=[pb_, tvb] + bk.wdeps("cdpsF"), tag=f"vi{ch}_{s_}")
            vv = p.tt(stream[:, 2 + 32 * s_:2 + 32 * (s_ + 1)], psF[:, 0:32], vstate[:, :],
                      ALU.add, deps=[vi, v1] + bk.wdeps(("cd_strmv", s_)))
            bk.read("cdpsF", vv)
            bk.read("cd_vst", vv)
            scv.append(vv)
            if CD_LEVEL < 4:
                continue
            ws2 = p.ts(Wsc2[:, :], Wtf[:, :], cols[:, 11:12], None, ALU.mult,
                       deps=[d4] + twt + bk.wdeps("cd_Wsc2"), tag=f"Wsc2{ch}_{s_}")
            for g in range(4):
                dc_ = p.act(cols[:, 15:16], Lci[:, 128 * g + 127:128 * (g + 1)], AF.Exp,
                            deps=[lt[g][1]] + bk.wdeps("cd_col15"),
                            tag=f"decf{ch}_{s_}_{g}")
                for (MM, kv, src, tksrc) in ((MKs, 0, kchT[:, 32:64], tks[1]),
                                             (MVs, 1, kchT[:, 64:96], tks[2])):
                    blkm = MM[:, 32 * (4 * s_ + g):32 * (4 * s_ + g + 1)]
                    u1 = p.ts(blkm, blkm, cols[:, 15:16], None, ALU.mult,
                              deps=[dc_, ks if kv == 0 else vs], tag=f"dec{ch}_{s_}_{g}_{kv}")
                    bk.read("cd_col15", u1)
                    im = p.mm(psF[:, 0:32], Wsc2[:, 128 * g:128 * (g + 1)], src,
                              True, True, deps=[ws2, tksrc] + bk.wdeps("cdpsF"),
                              tag=f"inj{ch}_{s_}_{g}_{kv}")
                    u2 = p.tt(blkm, blkm, psF[:, 0:32], ALU.add, deps=[im, u1])
                    bk.read("cdpsF", u2)
                    bk.read(("cd_MM", s_, g, kv), u2)
            bk.read("cd_Wsc2", u2)
        bk.read("cdpsG", scv[-1])
        bk.read("cdpsQ", scv[-1])
        if CD_LEVEL < 5:
            dmz = p.dma("sync", d_mem[blk, :], stream[:, :], deps=[tz2z, scv[0], scv[1]])
            continue
        for g in range(4):
            cm = p.mm(psF[:, 0:1], Wtf[:, 128 * g:128 * (g + 1)], C["c_ones_col_f"][:, :],
                      True, True, deps=[scv[-1] if g == 0 else None] + bk.wdeps("cdpsF"),
                      tag=f"cnt{ch}_{g}")
            cu = p.tt(CNT[:, g:g + 1], CNT[:, g:g + 1], psF[:, 0:1], ALU.add, deps=[cm])
            bk.read("cdpsF", cu)
            bk.read(("cd_Wtf", g), cu)
        dm_ = p.dma("sync", d_mem[blk, :], stream[:, :], deps=[scv[0], scv[1]])
        bk.read(("cd_strm", 0), dm_)
        bk.read(("cd_strm", 1), dm_)
        bk.read(("cd_strmv", 0), dm_)
        bk.read(("cd_strmv", 1), dm_)
        bk.read("cd_g", dm_)


C_STAGES = ("a", "b", "c", "d")
CD_CHUNKS = 32
CD_LEVEL = 9


def launch_C(bd):
    bd.uT = [bd.sb(f"uT{i}", (128, T), BF16) for i in range(8)]
    bd.qkr = bd.sb("qkr", (32, T), BF16)
    bd.wkr = bd.sb("wkr", (32, T), BF16)
    bd.wvr = bd.sb("wvr", (32, T), BF16)
    bd.d_idx = bd.nc.dram_tensor("d_idx", [2, T], F32, kind="Internal").ap()
    with bd.scoped():
        stage_Ca(bd)
    bd.p.barrier("b_ca")
    with bd.scoped():
        stage_Cb(bd)
    bd.p.barrier("b_cb")
    if "c" in C_STAGES:
        with bd.scoped():
            stage_Cc(bd)
        bd.p.barrier("b_cc")
    if "d" in C_STAGES:
        with bd.scoped():
            stage_Cd(bd)
    else:
        d_mem = bd.dout("d_mem", (T, 66))
        z = bd.sb("zz", (128, 66), F32)
        tz = bd.p.memset(z[:, :], 0.0)
        for i in range(32):
            bd.p.dma("sync", d_mem[128 * i:128 * (i + 1), :], z[:, :], deps=[tz])
    if "c" not in C_STAGES:
        d_longpart = bd.dout("d_longpart", (T, D))
        z2 = bd.sb("zz2", (128, 1024), F32)
        tz2 = bd.p.memset(z2[:, :], 0.0)
        for i in range(32):
            bd.p.dma("sync", d_longpart[128 * i:128 * (i + 1), :], z2[:, :], deps=[tz2])


def stage_D(bd):
    p, C = bd.p, bd.C
    i_x2 = bd.din("i_x2own2", (HT, D))
    i_lpa = bd.din("i_lpa", (HT, D))
    i_lpb = bd.din("i_lpb", (HT, D))
    i_ms0 = bd.din("i_ms0", (HT, 66))
    i_ms1 = bd.din("i_ms1", (HT, 66))
    i_gg = bd.din("i_glgm", (HT, 2))
    w_mupT = bd.din("w_mupT", (D, 4096))
    w_mupb = bd.din("w_mupb", (128, 32))
    w_mdT = bd.din("w_mdownT", (4096, D))
    w_mdb = bd.din("w_mdownb", (1, D))
    w_moT = bd.din("w_moT", (32, D))
    d_x3_own = bd.dout("d_x3_own", (HT, D))
    bk = bd.bk

    mup = bd.sb("d_mup", (128, 8 * 4096), BF16)
    mdn = bd.sb("d_mdn", (128, 32 * 1024), BF16)
    mdb = bd.sb("d_mdb", (1, 1024), F32)
    mupb = bd.sb("d_mupb", (128, 32), F32)
    moT = bd.sb("d_moT", (32, 1024), BF16)
    x2c = [bd.sb(f"d_x2c{i}", (128, 1024), F32) for i in range(2)]
    lpa = [bd.sb(f"d_lpa{i}", (128, 1024), F32) for i in range(2)]
    lpb = [bd.sb(f"d_lpb{i}", (128, 1024), F32) for i in range(2)]
    ms0 = [bd.sb(f"d_ms0{i}", (128, 66), F32) for i in range(2)]
    ms1 = [bd.sb(f"d_ms1{i}", (128, 66), F32) for i in range(2)]
    ggc = [bd.sb(f"d_gg{i}", (128, 2), F32) for i in range(2)]
    yb = [bd.sb(f"d_yb{i}", (128, 1024), F32) for i in range(2)]
    sc4 = [bd.sb(f"d_sc4{i}", (128, 8), F32) for i in range(2)]
    racc = [bd.sb(f"d_racc{i}", (128, 32), F32) for i in range(2)]
    rb16 = [bd.sb(f"d_rb16{i}", (128, 32), BF16) for i in range(2)]
    rdT = [bd.sb(f"d_rdT{i}", (32, 128), BF16) for i in range(2)]
    ynb = [bd.sb(f"d_ynb{i}", (128, 1024), BF16) for i in range(2)]
    ynT = [bd.sb(f"d_ynT{i}", (128, 1024), BF16) for i in range(2)]
    u2_ = bd.sb("d_u2", (128, 4096), BF16)
    u2 = [u2_, u2_]
    sqt_ = bd.sb("d_sqt", (128, 1024), BF16)
    sqt = [sqt_, sqt_]
    x3f = [bd.sb(f"d_x3f{i}", (128, 1024), F32) for i in range(2)]
    cols = [bd.sb(f"d_cols{i}", (128, 8), F32) for i in range(2)]
    psA = bd.ps("dpsA", (128, 512))
    psB = bd.ps("dpsB", (128, 512))
    psU = [bd.ps("dpsU0", (128, 512)), bd.ps("dpsU1", (128, 512))]
    psR = bd.ps("dpsR", (32, 128), BF16)
    psY = [bd.ps("dpsY0", (128, 128), BF16), bd.ps("dpsY1", (128, 128), BF16)]
    kb6, kb6t = bd.kcol(1e-6)
    kb5, kb5t = bd.kcol(1e-5)

    dmu = [p.dma("gpsimd", mup[:, 4096 * i:4096 * (i + 1)], w_mupT[128 * i:128 * (i + 1), :])
           for i in range(8)]
    dmd = [p.dma("gpsimd", mdn[:, 1024 * i:1024 * (i + 1)], w_mdT[128 * i:128 * (i + 1), :])
           for i in range(32)]
    dmb = p.dma("sync", mdb[:, :], w_mdb)
    dub = p.dma("sync", mupb[:, :], w_mupb)
    dmo = [p.dma("gpsimd", moT[0:32, 1024 * 0:1024 * 1], w_moT)]
    done = {}
    for tc in range(NTC):
        i = tc % 2
        blk = slice(128 * tc, 128 * (tc + 1))
        guard = [done.get(("in", i))]
        dx2 = p.dma("sync", x2c[i][:, :], i_x2[blk, :], deps=guard)
        dla = p.dma("sync", lpa[i][:, :], i_lpa[blk, :], deps=guard)
        dlb = p.dma("sync", lpb[i][:, :], i_lpb[blk, :], deps=guard)
        dm0 = p.dma("sync", ms0[i][:, :], i_ms0[blk, :], deps=guard)
        dm1 = p.dma("sync", ms1[i][:, :], i_ms1[blk, :], deps=guard)
        dgg = p.dma("sync", ggc[i][:, :], i_gg[blk, :], deps=guard)
        c0 = p.vcopy(sc4[i][:, 0:2], ms0[i][:, 0:2], deps=[dm0] + bk.wdeps(("d_sc4", i)))
        c1 = p.vcopy(sc4[i][:, 2:4], ms1[i][:, 0:2], deps=[dm1])
        r0 = p.red(cols[i][:, 0:1], sc4[i][:, 0:4], ALU.max,
                   deps=[c0, c1] + bk.wdeps(("d_cols", i)))
        r1 = p.ts(cols[i][:, 0:1], cols[i][:, 0:1], -1.0, None, ALU.mult, deps=[r0])
        r2 = p.act(sc4[i][:, 4:8], sc4[i][:, 0:4], AF.Exp, bias=cols[i][:, 0:1],
                   accum_out=cols[i][:, 1:2], deps=[r1, done.get("e4")], tag=f"sm4_{tc}")
        done["e4"] = r2
        r3 = p.recip(cols[i][:, 1:2], cols[i][:, 1:2], deps=[r2])
        r4 = p.ts(sc4[i][:, 4:8], sc4[i][:, 4:8], cols[i][:, 1:2], None, ALU.mult, deps=[r3])
        bk.read(("d_sc4", i), r4)
        v0 = p.ts(racc[i][:, :], ms0[i][:, 2:34], sc4[i][:, 4:5], None, ALU.mult,
                  deps=[r4, dm0] + bk.wdeps(("d_racc", i)), tag=f"rd0_{tc}")
        v1 = p.stt(racc[i][:, :], ms0[i][:, 34:66], sc4[i][:, 5:6], racc[i][:, :],
                   ALU.mult, ALU.add, deps=[v0])
        v2 = p.stt(racc[i][:, :], ms1[i][:, 2:34], sc4[i][:, 6:7], racc[i][:, :],
                   ALU.mult, ALU.add, deps=[v1, dm1])
        v3 = p.stt(racc[i][:, :], ms1[i][:, 34:66], sc4[i][:, 7:8], racc[i][:, :],
                   ALU.mult, ALU.add, deps=[v2], tag=f"rd3_{tc}")
        done[("in", i)] = v3
        v4 = p.vcopy(rb16[i][:, :], racc[i][:, :], deps=[v3] + bk.wdeps(("d_rb16", i)))
        bk.read(("d_racc", i), v4)
        tr1 = p.tr(psR[:, :], rb16[i][:, :], C["c_ident"][:, :], deps=[v4] + bk.wdeps("dpsR"),
                   tag=f"rT{tc}")
        tr2 = p.vcopy(rdT[i][:, :], psR[:, :], deps=[tr1] + bk.wdeps(("d_rdT", i)))
        bk.read("dpsR", tr2)
        bk.read(("d_rb16", i), tr2)
        for half in range(2):
            ps_ = psA if half == 0 else psB
            tm = p.mm(ps_[:, :], rdT[i][:, :], moT[0:32, 512 * half:512 * (half + 1)],
                      True, True, deps=[tr2, dmo[0]] + bk.wdeps(ps_.name), tag=f"mo{tc}_{half}")
            bk.read(("d_rdT", i), tm)
            hsl = slice(512 * half, 512 * (half + 1))
            t1 = p.stt(yb[i][:, hsl], ps_[:, :], ggc[i][:, 1:2], x2c[i][:, hsl],
                       ALU.mult, ALU.add,
                       deps=[tm, dgg, dx2] + (bk.wdeps(("d_yb", i)) if half == 0 else []),
                       tag=f"ymem{tc}_{half}")
            bk.read(ps_.name, t1)
        t2a = p.stt(yb[i][:, :], lpa[i][:, :], ggc[i][:, 0:1], yb[i][:, :], ALU.mult, ALU.add,
                    deps=[dla, t1], tag=f"ylonga{tc}")
        t2 = p.stt(yb[i][:, :], lpb[i][:, :], ggc[i][:, 0:1], yb[i][:, :], ALU.mult, ALU.add,
                   deps=[dlb, t2a], tag=f"ylongb{tc}")
        q1 = p.act(sqt[i][:, :], yb[i][:, :], AF.Square, accum_out=cols[i][:, 2:3],
                   deps=[t2] + bk.wdeps(("d_sqt",)))
        q2 = p.act(cols[i][:, 2:3], cols[i][:, 2:3], AF.Ln, bias=kb6, scale=1.0 / D,
                   deps=[q1, kb6t])
        q3 = p.act(cols[i][:, 2:3], cols[i][:, 2:3], AF.Exp, scale=-0.5, deps=[q2])
        q4 = p.ts(ynb[i][:, :], yb[i][:, :], cols[i][:, 2:3], None, ALU.mult,
                  deps=[q3] + bk.wdeps(("d_ynb", i)), tag=f"yn{tc}")
        trts = []
        for dm in range(8):
            psy = psY[dm % 2]
            w1_ = p.tr(psy[:, :], ynb[i][:, 128 * dm:128 * (dm + 1)], C["c_ident"][:, :],
                       deps=[q4] + bk.wdeps(psy.name), tag=f"ynT{tc}_{dm}")
            tt_ = p.vcopy(ynT[i][:, 128 * dm:128 * (dm + 1)], psy[:, :],
                          deps=[w1_] + (bk.wdeps(("d_ynT", i)) if dm == 0 else []))
            bk.read(psy.name, tt_)
            trts.append(tt_)
        bk.read(("d_ynb", i), trts[-1])
        for hc in range(32):
            psu = psU[hc % 2]
            ops = [(mup[:, 4096 * dm + 128 * hc:4096 * dm + 128 * (hc + 1)],
                    ynT[i][:, 128 * dm:128 * (dm + 1)],
                    [dmu[dm], trts[dm]] if hc == 0 else [dmu[dm]]) for dm in range(8)]
            tmu = p.mmgroup(psu[:, 0:128], ops, deps_first=bk.wdeps(psu.name),
                            tag=f"up{tc}_{hc}")
            bk.read(("d_ynT", i), tmu)
            tsl = p.act(u2[i][:, 128 * hc:128 * (hc + 1)], psu[:, 0:128],
                        AF.Silu, bias=mupb[:, hc:hc + 1],
                        deps=[tmu, dub] + bk.wdeps(("d_u2", hc)))
            bk.read(psu.name, tsl)
        upf = [p.fence("scalar")]
        for half in range(2):
            ps_ = psA if half == 0 else psB
            hsl = slice(512 * half, 512 * (half + 1))
            ops = [(u2[i][:, 128 * hc:128 * (hc + 1)],
                    mdn[:, 1024 * hc + 512 * half:1024 * hc + 512 * (half + 1)],
                    [dmd[hc]] if tc == 0 else ()) for hc in range(32)]
            tmd = p.mmgroup(ps_[:, :], ops, deps_first=bk.wdeps(ps_.name) + upf,
                            tag=f"dn{tc}_{half}")
            for hc in range(32):
                bk.read(("d_u2", hc), tmd)
            tb = p.mm(ps_[:, :], C["c_ones_row_f"][0:1, :], mdb[:, hsl], False, True,
                      deps=[dmb], tag=f"dnb{tc}_{half}")
            a1 = p.tt(x3f[i][:, hsl], ps_[:, :], yb[i][:, hsl], ALU.add,
                      deps=[tb] + (bk.wdeps(("d_x3f", i)) if half == 0 else []),
                      tag=f"res1_{tc}_{half}")
            a2 = p.tt(x3f[i][:, hsl], x3f[i][:, hsl], x2c[i][:, hsl], ALU.add, deps=[a1])
            bk.read(ps_.name, a2)
            done["yb"] = a2
        bk.read(("d_yb", i), done["yb"])
        l2 = p.red(cols[i][:, 4:5], x3f[i][:, :], ALU.add, deps=[done["yb"]], negate=True)
        l3 = p.ts(cols[i][:, 4:5], cols[i][:, 4:5], 1.0 / D, None, ALU.mult, deps=[l2])
        l4 = p.act(sqt[i][:, :], x3f[i][:, :], AF.Square, bias=cols[i][:, 4:5],
                   accum_out=cols[i][:, 5:6], deps=[l3, q1])
        l5 = p.act(cols[i][:, 5:6], cols[i][:, 5:6], AF.Ln, bias=kb5, scale=1.0 / D,
                   deps=[l4, kb5t])
        l6 = p.act(cols[i][:, 5:6], cols[i][:, 5:6], AF.Exp, scale=-0.5, deps=[l5])
        l7 = p.ts(x3f[i][:, :], x3f[i][:, :], cols[i][:, 4:5], cols[i][:, 5:6],
                  ALU.add, ALU.mult, deps=[l6], tag=f"lnx3_{tc}")
        bk.read(("d_x3f", i), l7)
        bk.read(("d_sqt",), l4)
        d3 = p.dma("sync", d_x3_own[blk, :], x3f[i][:, :], deps=[l7])
        bk.read(("d_x3f", i), d3)
        bk.read(("d_cols", i), l7)


def launch_D(bd):
    with bd.scoped():
        stage_D(bd)


def stage_E1(bd):
    p, C = bd.p, bd.C
    i_x3T = bd.din("i_x3T", (D, T))
    w_gT = bd.din("w_f1gT", (D, 8))
    bk = bd.bk
    x3F = [bd.sb(f"e1_x3F{i}", (128, T), BF16) for i in range(8)]
    gsb = bd.sb("e1_g", (128, 64), BF16)
    prc = bd.sb("e1_prc", (128, 8), F32)
    colz = bd.sb("e1_colz", (128, 2), F32)
    psA = bd.ps("e1psA", (128, 8))
    psT = bd.ps("e1psT", (8, 128))

    dx = [p.dma("gpsimd", x3F[i][:], i_x3T[128 * i:128 * (i + 1), :]) for i in range(8)]
    dg = [p.dma("gpsimd", gsb[:, 8 * i:8 * (i + 1)], w_gT[128 * i:128 * (i + 1), :])
          for i in range(8)]
    for tc in range(32):
        ops = [(x3F[dm][:, 128 * tc:128 * (tc + 1)], gsb[:, 8 * dm:8 * (dm + 1)],
                [dx[dm], dg[dm]]) for dm in range(8)]
        tm = p.mmgroup(psA[:, :], ops, deps_first=bk.wdeps("e1psA") + (bd.cdeps if tc == 0 else []),
                       tag=f"sc{tc}")
        te = p.vcopy(prc[:, :], psA[:, :], deps=[tm] + bk.wdeps("e1_prc"))
        bk.read("e1psA", te)
        r0 = p.red(colz[:, 0:1], prc[:, :], ALU.max, deps=[te] + bk.wdeps("e1_colz"))
        r1 = p.ts(colz[:, 0:1], colz[:, 0:1], -1.0, None, ALU.mult, deps=[r0])
        r2 = p.act(prc[:, :], prc[:, :], AF.Exp, bias=colz[:, 0:1], accum_out=colz[:, 1:2],
                   deps=[r1], tag=f"e1exp{tc}")
        r3 = p.recip(colz[:, 1:2], colz[:, 1:2], deps=[r2])
        r4 = p.ts(bd.probs[:, 8 * tc:8 * (tc + 1)], prc[:, :], colz[:, 1:2], None, ALU.mult,
                  deps=[r3] + (bk.wdeps("e_probs") if tc == 0 else []), tag=f"e1p{tc}")
        bk.read("e1_prc", r4)
        bk.read("e1_colz", r4)
        tt1 = p.tr(psT[:, :], bd.probs[:, 8 * tc:8 * (tc + 1)], C["c_identf"][:, :],
                   deps=[r4] + bk.wdeps("e1psT"), tag=f"e1T{tc}")
        tt2 = p.vcopy(bd.probsT[:, 128 * tc:128 * (tc + 1)], psT[:, :], deps=[tt1])
        bk.read("e1psT", tt2)
    cps = []
    for dm in range(8):
        cp = p.vcopy(bd.x3own[dm][:, :], x3F[dm][:, 0:HT], deps=[dx[dm]])
        cps.append(cp)
    bd.e1_done = cps


def stage_E2(bd):
    p, C = bd.p, bd.C
    bk = bd.bk
    lo = bd.sb("e2_lo", (8, 1), F32)
    hi = bd.sb("e2_hi", (8, 1), F32)
    tau = bd.sb("e2_tau", (8, 1), F32)
    cnt = bd.sb("e2_cnt", (8, 1), F32)
    sel = bd.sb("e2_sel", (8, 2), F32)
    dl = bd.sb("e2_dl", (8, 2), F32)
    cmp01 = bd.sb("e2_cmp", (8, T), BF16)
    t0 = p.step("vector", lambda v: v.memset(lo[:, :], 0.0))
    t1 = p.step("vector", lambda v: v.memset(hi[:, :], 2.0), deps=[t0])
    t2 = p.step("vector", lambda v: v.memset(tau[:, :], 1.0), deps=[t1])
    tprev = t2
    for it in range(40):
        c1 = p.ts(cmp01[:, :], bd.probsT[:, :], tau[:, :], None, ALU.is_ge,
                  deps=[tprev], tag=f"bis{it}")
        c2 = p.red(cnt[:, :], cmp01[:, :], ALU.add, deps=[c1])
        c3 = p.ts(sel[:, 0:1], cnt[:, :], 1024.0, None, ALU.is_ge, deps=[c2])
        c4 = p.ts(sel[:, 1:2], sel[:, 0:1], -1.0, 1.0, ALU.mult, ALU.add, deps=[c3])
        c5 = p.tt(dl[:, 0:1], tau[:, :], lo[:, :], ALU.subtract, deps=[c4])
        c6 = p.tt(dl[:, 1:2], tau[:, :], hi[:, :], ALU.subtract, deps=[c5])
        c7 = p.stt(lo[:, :], dl[:, 0:1], sel[:, 0:1], lo[:, :], ALU.mult, ALU.add, deps=[c6])
        c8 = p.stt(hi[:, :], dl[:, 1:2], sel[:, 1:2], hi[:, :], ALU.mult, ALU.add, deps=[c7])
        c9 = p.tt(tau[:, :], lo[:, :], hi[:, :], ALU.add, deps=[c8])
        tprev = p.ts(tau[:, :], tau[:, :], 0.5, None, ALU.mult, deps=[c9])
    bd.tau_lo = lo
    bd.e2_done = tprev


def stage_E3(bd):
    p, C = bd.p, bd.C
    w1d = bd.din("w_f1w1", (8 * 1024, 4096))
    w2d = bd.din("w_f1w2", (8 * 4096, 1024))
    i_x3own = bd.din("i_x3own", (HT, D))
    d_out = bd.dout("d_out", (HT, D))
    bk = bd.bk
    kb5, kb5t = bd.kcol(1e-5)

    w1sb = bd.sb("e3_w1", (128, 16384), BF16)
    w2sb = bd.sb("e3_w2", (128, 16384), BF16)
    hsb = bd.sb("e3_h", (128, 8192), BF16)
    yacc = bd.sb("e3_y", (128, 16 * 1024), BF16)
    tbc = bd.sb("e3_tbc", (128, 8), F32)
    wcol = bd.sb("e3_w", (128, 256), F32)
    taur = bd.sb("e3_taur", (1, 8), F32)
    x3c = [bd.sb(f"e3_x3c{i}", (128, 1024), F32) for i in range(2)]
    yf = bd.sb("e3_yf", (128, 1024), F32)
    sqt = bd.sb("e3_sqt", (128, 1024), BF16)
    cols = bd.sb("e3_cols", (128, 4), F32)
    psH = [bd.ps("e3psH0", (128, 512)), bd.ps("e3psH1", (128, 512))]
    psY = [bd.ps("e3psY0", (128, 512)), bd.ps("e3psY1", (128, 512))]
    psS = bd.ps("e3psS", (1, 8))
    psB2 = bd.ps("e3psB2", (128, 8))

    tz = p.step("vector", lambda v: v.memset(yacc[:, :], 0.0), deps=[])
    tt0 = p.tr(psS[:, :], bd.tau_lo[:, :], C["c_identf"][0:8, 0:8],
               deps=[bd.e2_done], tag="tauT")
    tt1 = p.vcopy(taur[:, :], psS[:, :], deps=[tt0])
    tt2 = p.mm(psB2[:, :], C["c_ones_row_f"][:, :], taur[:, :], True, True, deps=[tt1],
               tag="taubc")
    tt3 = p.vcopy(tbc[:, :], psB2[:, :], deps=[tt2])
    bk.read("e3psB2", tt3)
    wts = []
    for tc in range(NTC):
        m1 = p.tt(wcol[:, 8 * tc:8 * (tc + 1)], bd.probs[:, 8 * tc:8 * (tc + 1)],
                  tbc[:, :], ALU.is_ge, deps=[tt3], tag=f"msk{tc}")
        m2 = p.tt(wcol[:, 8 * tc:8 * (tc + 1)], wcol[:, 8 * tc:8 * (tc + 1)],
                  bd.probs[:, 8 * tc:8 * (tc + 1)], ALU.mult, deps=[m1])
        wts.append(m2)
    ydep = {"last": tz}
    for e in range(8):
        for hh in range(2):
            wfence = [p.fence("tensor")]
            dw1 = [p.dma("gpsimd", w1sb[:, 2048 * dm:2048 * (dm + 1)],
                         w1d[1024 * e + 128 * dm:1024 * e + 128 * (dm + 1),
                             2048 * hh:2048 * (hh + 1)],
                         deps=wfence if dm == 0 else (), tag=f"w1_{e}_{hh}")
                   for dm in range(8)]
            dw2 = [p.dma("gpsimd", w2sb[:, 1024 * hc:1024 * (hc + 1)],
                         w2d[4096 * e + 2048 * hh + 128 * hc:
                             4096 * e + 2048 * hh + 128 * (hc + 1), :],
                         deps=wfence if hc == 0 else (), tag=f"w2_{e}_{hh}")
                   for hc in range(16)]
            for tg in range(4):
                prev_pe = [p.fence("tensor")]
                for hc in range(16):
                    ps_ = psH[hc % 2]
                    ops = [(w1sb[:, 2048 * dm + 128 * hc:2048 * dm + 128 * (hc + 1)],
                            bd.x3own[dm][:, 512 * tg:512 * (tg + 1)],
                            [dw1[dm]] + (bd.e1_done if e == 0 and hh == 0 and tg == 0 else []))
                           for dm in range(8)]
                    th = p.mmgroup(ps_[:, :], ops, deps_first=bk.wdeps(ps_.name),
                                   tag=f"f1h{e}_{hh}_{tg}_{hc}")
                    tg_ = p.act(hsb[:, 512 * hc:512 * (hc + 1)], ps_[:, :], AF.Gelu,
                                deps=[th] + prev_pe, tag=f"f1g{e}_{hh}_{tg}_{hc}")
                    bk.read(ps_.name, tg_)
                for sub in range(4):
                    for half in range(2):
                        ps_ = psY[(sub * 2 + half) % 2]
                        ops = [(hsb[:, 512 * hc + 128 * sub:512 * hc + 128 * (sub + 1)],
                                w2sb[:, 1024 * hc + 512 * half:1024 * hc + 512 * (half + 1)],
                                [dw2[hc]] if tg == 0 else ()) for hc in range(16)]
                        ty = p.mmgroup(ps_[:, :], ops,
                                       deps_first=bk.wdeps(ps_.name) + [p.fence("scalar")],
                                       tag=f"f1y{e}_{hh}_{tg}_{sub}_{half}")
                        tcb = 4 * tg + sub
                        ysl = yacc[:, 1024 * tcb + 512 * half:1024 * tcb + 512 * (half + 1)]
                        ta = p.stt(ysl, ps_[:, :], wcol[:, 8 * tcb + e:8 * tcb + e + 1],
                                   ysl, ALU.mult, ALU.add,
                                   deps=[ty, ydep["last"], wts[tcb]],
                                   tag=f"f1a{e}_{hh}_{tg}_{sub}_{half}")
                        bk.read(ps_.name, ta)
                        ydep["last"] = ta
    done = {}
    for tc in range(NTC):
        i = tc % 2
        blk = slice(128 * tc, 128 * (tc + 1))
        dx3 = p.dma("sync", x3c[i][:, :], i_x3own[blk, :], deps=[done.get(("x3", i))])
        a1 = p.tt(yf[:, :], yacc[:, 1024 * tc:1024 * (tc + 1)], x3c[i][:, :], ALU.add,
                  deps=[dx3, ydep["last"]] + bk.wdeps("e3_yf"), tag=f"fin{tc}")
        done[("x3", i)] = a1
        l2 = p.red(cols[:, 0:1], yf[:, :], ALU.add, deps=[a1], negate=True)
        l3 = p.ts(cols[:, 0:1], cols[:, 0:1], 1.0 / D, None, ALU.mult, deps=[l2])
        l4 = p.act(sqt[:, :], yf[:, :], AF.Square, bias=cols[:, 0:1],
                   accum_out=cols[:, 1:2], deps=[l3, done.get("sqt")])
        done["sqt"] = l4
        l5 = p.act(cols[:, 1:2], cols[:, 1:2], AF.Ln, bias=kb5, scale=1.0 / D,
                   deps=[l4, kb5t])
        l6 = p.act(cols[:, 1:2], cols[:, 1:2], AF.Exp, scale=-0.5, deps=[l5])
        l7 = p.ts(yf[:, :], yf[:, :], cols[:, 0:1], cols[:, 1:2], ALU.add, ALU.mult,
                  deps=[l6], tag=f"lnx4_{tc}")
        bk.read("e3_yf", l7)
        d4 = p.dma("sync", d_out[blk, :], yf[:, :], deps=[l7])
        bk.read("e3_yf", d4)


def launch_E(bd):
    bd.x3own = [bd.sb(f"x3own{i}", (128, HT), BF16) for i in range(8)]
    bd.probs = bd.sb("e_probs", (128, 256), F32)
    bd.probsT = bd.sb("e_probsT", (8, T), F32)
    with bd.scoped():
        stage_E1(bd)
    bd.p.barrier("b_e1")
    with bd.scoped():
        stage_E2(bd)
    bd.p.barrier("b_e2")
    with bd.scoped():
        stage_E3(bd)


def _mk(builder, taps=()):
    nc = bass.Bass("TRN2", target_bir_lowering=False)
    st = contextlib.ExitStack()
    with st:
        bd = Build(nc, st, taps)
        bd.load_consts()
        builder(bd)
        deps = bd.p.all_fences()
        for (tap, ap) in bd.tap_pending:
            bd.p.dma("gpsimd", tap, ap, deps=deps, tag="tapcopy")
        fin = bd.p.op("sync", lambda e: None, tag="final")
        fin.waits += bd.p.all_fences()
        bd.p.emit(st)
    return nc


def launch_A(bd):
    with bd.scoped():
        stage_S1(bd)


def launch_B(bd):
    bd.x1T = [bd.sb(f"x1T{i}", (128, HT), BF16) for i in range(8)]
    bd.yacc = bd.sb("yacc", (128, 16 * 1024), F32)
    with bd.scoped():
        stage_S2(bd)
    bd.p.barrier("b_s2")
    with bd.scoped():
        stage_S3(bd)
    bd.p.barrier("b_s3")
    with bd.scoped():
        stage_S4(bd)


def kernel(**inputs):
    maps = host_prep(inputs)
    cores = list(range(8))
    resA = run_bass_kernel_spmd(_mk(launch_A), maps, core_ids=cores).results
    add_launchB_inputs(maps, resA)
    resB = run_bass_kernel_spmd(_mk(launch_B), maps, core_ids=cores).results
    add_launchC_inputs(maps, resB, inputs)
    resC = run_bass_kernel_spmd(_mk(launch_C), maps, core_ids=cores).results
    add_launchD_inputs(maps, resB, resC, inputs)
    resD = run_bass_kernel_spmd(_mk(launch_D), maps, core_ids=cores).results
    add_launchE_inputs(maps, resD, inputs)
    resE = run_bass_kernel_spmd(_mk(launch_E), maps, core_ids=cores).results
    out = np.empty((B, T, D), np.float32)
    for c in range(8):
        b, r = c // 2, c % 2
        out[b, r * HT:(r + 1) * HT] = resE[c]["d_out"]
    return out


# version 04
# speedup vs baseline: 1.0556x; 1.0302x over previous
import contextlib
import numpy as np
import ml_dtypes
import concourse.bass as bass
import concourse.mybir as mybir
from concourse.bass_utils import run_bass_kernel_spmd

F32 = mybir.dt.float32
BF16 = mybir.dt.bfloat16
I32 = mybir.dt.int32
AF = mybir.ActivationFunctionType
ALU = mybir.AluOpType
AX = mybir.AxisListType

ENGINES = ("tensor", "vector", "scalar", "gpsimd", "sync")
DMA_QUEUES = ("sync", "gpsimd", "scalar", "vector")
B, T, D = 4, 4096, 1024
HT = T // 2
NTC = HT // 128
GROUPS = [[0, 1], [2, 3], [4, 5], [6, 7]]
F32EPS = float(np.finfo(np.float32).eps)
BF = ml_dtypes.bfloat16


class Op:
    __slots__ = ("fn", "waits", "incs", "tag")

    def __init__(self, fn, tag=""):
        self.fn = fn
        self.waits = []
        self.incs = []
        self.tag = tag


class Prog:
    NDMASEM = 4

    def __init__(self, nc):
        self.nc = nc
        self.streams = {e: [] for e in ENGINES}
        self.counts = {e: 0 for e in ENGINES}
        self.dma_counts = {}
        self.dma_rr = {q: 0 for q in DMA_QUEUES}
        for q in DMA_QUEUES:
            for i in range(self.NDMASEM):
                self.dma_counts[f"dma_{q}{i}"] = 0
        self.dma_counts["cc"] = 0
        self.sem_handles = {}

    def op(self, eng, fn, tag=""):
        o = Op(fn, tag)
        self.streams[eng].append(o)
        return o

    def _tok(self, eng, o):
        self.counts[eng] += 1
        o.incs.append((eng, 1))
        return (eng, self.counts[eng])

    def step(self, eng, fn, deps=(), tag=""):
        o = self.op(eng, fn, tag)
        for d in deps:
            if d is not None:
                o.waits.append(d)
        return self._tok(eng, o)

    def mm(self, out, lhsT, rhs, start, stop, deps=(), tag=""):
        return self.step("tensor",
                         lambda t: t.matmul(out, lhsT, rhs, start=start, stop=stop,
                                            skip_group_check=True),
                         deps, tag)

    def mmgroup(self, out, ops, deps_first=(), tag=""):
        n = len(ops)
        tok = None
        for i, (lh, rh, dd) in enumerate(ops):
            deps = list(dd)
            if i == 0:
                deps += list(deps_first)
            tok = self.mm(out, lh, rh, i == 0, i == n - 1, deps, tag=f"{tag}[{i}]")
        return tok

    def tr(self, out_ps, in_, ident, deps=(), tag=""):
        return self.step("tensor", lambda t: t.transpose(out_ps, in_, ident), deps, tag)

    def act(self, out, in_, func, bias=0.0, scale=1.0, accum_out=None, deps=(), tag=""):
        return self.step("scalar", lambda s: s.activation(out, in_, func, bias=bias,
                                                          scale=scale, accum_out=accum_out),
                         deps, tag)

    def scopy(self, out, in_, deps=(), tag=""):
        return self.act(out, in_, AF.Copy, deps=deps, tag=tag)

    def vcopy(self, out, in_, deps=(), tag=""):
        return self.step("vector", lambda v: v.tensor_copy(out, in_), deps, tag)

    def gcopy(self, out, in_, deps=(), tag=""):
        return self.step("gpsimd", lambda g: g.tensor_copy(out, in_), deps, tag)

    def tt(self, out, a, b, alu, deps=(), tag=""):
        return self.step("vector", lambda v: v.tensor_tensor(out, a, b, alu), deps, tag)

    def ts(self, out, in0, s1, s2=None, op0=ALU.mult, op1=ALU.bypass, accum_out=None,
           deps=(), tag=""):
        return self.step("vector", lambda v: v.tensor_scalar(out, in0, s1, s2, op0, op1,
                                                             accum_out=accum_out), deps, tag)

    def stt(self, out, in0, scalar, in1, op0, op1, deps=(), tag=""):
        return self.step("vector", lambda v: v.scalar_tensor_tensor(out, in0, scalar, in1,
                                                                    op0, op1), deps, tag)

    def red(self, out, in_, alu, deps=(), negate=False, absv=False, tag=""):
        return self.step("vector", lambda v: v.tensor_reduce(out, in_, AX.X, alu,
                                                             apply_absolute_value=absv,
                                                             negate=negate), deps, tag)

    def recip(self, out, in_, deps=(), tag=""):
        return self.step("vector", lambda v: v.reciprocal(out, in_), deps, tag)

    def memset(self, ap, val, deps=(), tag=""):
        return self.step("gpsimd", lambda g: g.memset(ap, val), deps, tag)

    def scan(self, out, d0, d1, init, deps=(), tag=""):
        return self.step("vector", lambda v: v.tensor_tensor_scan(out, d0, d1, init,
                                                                  ALU.mult, ALU.add), deps, tag)

    def dma(self, q, out, in_, deps=(), tag="", sem=None, **kw):
        def fn(e):
            return e.dma_start(out=out, in_=in_, **kw)
        o = self.op(q, fn, tag)
        for d in deps:
            if d is not None:
                o.waits.append(d)
        if sem is None:
            i = self.dma_rr[q]
            self.dma_rr[q] = (i + 1) % self.NDMASEM
            sem = f"dma_{q}{i}"
        if self.dma_counts[sem] > 0:
            o.waits.append((sem, self.dma_counts[sem]))
        self.dma_counts[sem] += 16
        o.incs.append((sem, 16))
        return (sem, self.dma_counts[sem])

    def idma(self, out, out_off, in_, in_off, bounds, deps=(), tag=""):
        from concourse.bass import IndirectOffsetOnAxis
        oo = IndirectOffsetOnAxis(*out_off) if out_off is not None else None
        io = IndirectOffsetOnAxis(*in_off) if in_off is not None else None

        def fn(e):
            return e.indirect_dma_start(out, oo, in_, io, bounds_check=bounds,
                                        oob_is_err=False)
        o = self.op("gpsimd", fn, tag)
        for d in deps:
            if d is not None:
                o.waits.append(d)
        q = "gpsimd"
        i = self.dma_rr[q]
        self.dma_rr[q] = (i + 1) % self.NDMASEM
        sem = f"dma_{q}{i}"
        if self.dma_counts[sem] > 0:
            o.waits.append((sem, self.dma_counts[sem]))
        self.dma_counts[sem] += 16
        o.incs.append((sem, 16))
        return (sem, self.dma_counts[sem])

    def cc(self, kind, alu, ins_, outs_, deps=(), tag=""):
        def fn(e):
            return e.collective_compute(kind, alu, GROUPS, ins_, outs_)
        o = self.op("gpsimd", fn, tag)
        for d in deps:
            if d is not None:
                o.waits.append(d)
        if self.dma_counts["cc"] > 0:
            o.waits.append(("cc", self.dma_counts["cc"]))
        self.dma_counts["cc"] += 16
        o.incs.append(("cc", 16))
        return ("cc", self.dma_counts["cc"])

    def barrier(self, tag="bar"):
        toks = self.all_fences()
        for e in ENGINES:
            o = self.op(e, lambda en: en.nop(), tag=f"{tag}_{e}")
            o.waits += [t for t in toks if t[0] != e]

    def fence(self, eng):
        return (eng, self.counts[eng])

    def dma_fence(self, q):
        return [(f"dma_{q}{i}", self.dma_counts[f"dma_{q}{i}"])
                for i in range(self.NDMASEM) if self.dma_counts[f"dma_{q}{i}"] > 0]

    def all_fences(self):
        toks = [self.fence(e) for e in ENGINES if self.counts[e] > 0]
        for q in DMA_QUEUES:
            toks += self.dma_fence(q)
        if self.dma_counts["cc"] > 0:
            toks.append(("cc", self.dma_counts["cc"]))
        return toks

    def emit(self, stack):
        nc = self.nc
        for n in list(ENGINES) + list(self.dma_counts.keys()):
            self.sem_handles[n] = stack.enter_context(nc.semaphore("sem_" + n))
        blk = stack.enter_context(nc.Block())

        def play(eng_name):
            def run(e):
                for o in self.streams[eng_name]:
                    for (sn, v) in o.waits:
                        e.wait_ge(self.sem_handles[sn], v)
                    inst = o.fn(e)
                    if o.incs:
                        assert inst is not None, f"op {o.tag} returned None but has incs"
                        for (sn, k) in o.incs:
                            inst.then_inc(self.sem_handles[sn], k)
            return run

        blk.tensor(play("tensor"))
        blk.vector(play("vector"))
        blk.scalar(play("scalar"))
        blk.gpsimd(play("gpsimd"))
        blk.sync(play("sync"))



CONST_SPECS = {
    "c_ident": ((128, 128), BF16), "c_identf": ((128, 128), F32),
    "c_su_f": ((128, 128), F32), "c_uti_f": ((128, 128), F32),
    "c_su_b": ((128, 128), BF16), "c_attnmask": ((128, 2048), BF16),
    "c_ones_col_b": ((128, 1), BF16), "c_ones_col_f": ((128, 1), F32),
    "c_ones_row_b": ((1, 128), BF16), "c_ones_row_f": ((1, 128), F32),
    "c_halfsel": ((128, 2), BF16), "c_iotac": ((128, 4), F32),
    "c_iotarow": ((128, 128), F32), "c_pow2": ((18, 2), BF16),
    "c_ones_row512f": ((1, 512), F32),
}


def _consts():
    c = {}
    eye = np.eye(128, dtype=np.float32)
    c["c_ident"] = eye.astype(BF)
    c["c_identf"] = eye.copy()
    iu = np.triu(np.ones((128, 128), np.float32), 1)
    c["c_su_f"] = iu.copy()
    c["c_uti_f"] = np.triu(np.ones((128, 128), np.float32), 0)
    c["c_su_b"] = iu.astype(BF)
    ki = np.arange(128)[:, None]
    qi = np.arange(512)[None, :]
    m = np.stack([np.where(o * 128 + ki > qi, -1e9, 0.0) for o in range(4)])
    c["c_attnmask"] = np.ascontiguousarray(m.transpose(1, 0, 2).reshape(128, 2048)).astype(BF)
    c["c_ones_col_b"] = np.ones((128, 1), BF)
    c["c_ones_col_f"] = np.ones((128, 1), np.float32)
    c["c_ones_row_b"] = np.ones((1, 128), BF)
    c["c_ones_row_f"] = np.ones((1, 128), np.float32)
    hs = np.zeros((128, 2), np.float32)
    hs[:64, 0] = 1.0
    hs[64:, 1] = 1.0
    c["c_halfsel"] = hs.astype(BF)
    c["c_iotac"] = (np.arange(128, dtype=np.float32)[:, None]
                    + 128.0 * np.arange(4, dtype=np.float32)[None, :])
    c["c_iotarow"] = np.broadcast_to(np.arange(128, dtype=np.float32), (128, 128)).copy()
    c["c_ones_row512f"] = np.ones((1, 512), np.float32)
    ps = np.zeros((18, 2), np.float32)
    for i in range(18):
        ps[i, i // 9] = float(2 ** (i % 9))
    c["c_pow2"] = ps.astype(BF)
    for k in c:
        assert c[k].shape == CONST_SPECS[k][0], k
    return c


def host_prep(ins):
    f32 = np.float32
    ct = lambda a: np.ascontiguousarray(a, dtype=f32)
    x = ct(ins["L0_attn_hidden_states"])
    consts = _consts()
    maps = []
    for c in range(8):
        b, r = c // 2, c % 2
        m = dict(consts)
        xb = x[b]
        m["i_xT"] = ct(xb.T)
        m["i_x_own"] = ct(xb[r * HT:(r + 1) * HT])
        m["w_qT"] = ct(ins["L0_attn_Wq"][512 * r:512 * (r + 1)].T)
        m["w_kT"] = ct(ins["L0_attn_Wk"][128 * r:128 * (r + 1)].T)
        m["w_vT"] = ct(ins["L0_attn_Wv"][128 * r:128 * (r + 1)].T)
        m["w_oT"] = ct(ins["L0_attn_Wo"].T[512 * r:512 * (r + 1)])
        m["w_f0w1"] = ct(ins["L0_ffn_w1"].reshape(8 * 1024, 2048))
        m["w_f0w2"] = ct(ins["L0_ffn_w2"].reshape(8 * 2048, 1024))
        m["w_f0b1T"] = ct(ins["L0_ffn_b1"].T)
        m["w_f0b2"] = ct(ins["L0_ffn_b2"])
        m["w_f0gateT"] = ct(ins["L0_ffn_gate_w"].T)
        m["w_f0gateb"] = ct(ins["L0_ffn_gate_b"].reshape(1, 8))
        m["w_f0rmsw"] = ct(ins["L0_ffn_rms_w"].reshape(1, 8))
        maps.append(m)
    return maps


def add_launchC_inputs(maps, resB, ins):
    f32 = np.float32
    ct = lambda a: np.ascontiguousarray(a, dtype=f32)
    x2 = {}
    for b in range(4):
        x2[b] = np.concatenate([resB[2 * b]["d_x2_own"], resB[2 * b + 1]["d_x2_own"]], axis=0)
    for c in range(8):
        b, r = c // 2, c % 2
        m = maps[c]
        m["i_x2T"] = ct(x2[b].T)
        m["w_conv"] = ct(ins["L1_attn_conv_w"].reshape(8, 128, 4).transpose(1, 0, 2).reshape(128, 32))
        m["w_l1gT"] = ct(ins["L1_attn_gate_w"].T)
        m["w_l1gb"] = ct(ins["L1_attn_gate_b"].reshape(8, 128).T)
        m["w_sinT"] = ct(ins["L1_attn_state_in_w"][2048 * r:2048 * (r + 1)].T)
        m["w_soT"] = ct(ins["L1_attn_state_out_w"].T[2048 * r:2048 * (r + 1)])
        m["w_mydecay"] = ct(ins["L1_attn_state_decay_logit"][2 * r:2 * (r + 1)].reshape(1, 2))
        m["w_keysT"] = ct(np.concatenate([ins["L1_attn_qkey_w"], ins["L1_attn_wkey_w"],
                                          ins["L1_attn_value_w"]], axis=0).T)
        m["w_bitsT"] = ct(np.concatenate([ins["L1_attn_read_bits_w"],
                                          ins["L1_attn_write_bits_w"]], axis=0).T)
        pR = np.zeros((36, 1), f32)
        pW = np.zeros((36, 1), f32)
        for i in range(9):
            pR[9 * r + i, 0] = float(2 ** i)
            pW[18 + 9 * r + i, 0] = float(2 ** i)
        m["w_powR"] = pR
        m["w_powW"] = pW
        m["w_g3T"] = ct(np.concatenate([ins["L1_attn_write_gate_w"], ins["L1_attn_gate_long_w"],
                                        ins["L1_attn_gate_mem_w"]], axis=0).T)
        m["w_g3b"] = ct(np.concatenate([ins["L1_attn_write_gate_b"], ins["L1_attn_gate_long_b"],
                                        ins["L1_attn_gate_mem_b"]]).reshape(1, 3))


def add_launchD_inputs(maps, resB, resC, ins):
    f32 = np.float32
    ct = lambda a: np.ascontiguousarray(a, dtype=f32)
    for c in range(8):
        b, r = c // 2, c % 2
        m = maps[c]
        sl = slice(r * HT, (r + 1) * HT)
        m["i_x2own2"] = resB[c]["d_x2_own"]
        m["i_lpa"] = ct(resC[2 * b]["d_longpart"][sl])
        m["i_lpb"] = ct(resC[2 * b + 1]["d_longpart"][sl])
        m["i_ms0"] = ct(resC[2 * b]["d_mem"][sl])
        m["i_ms1"] = ct(resC[2 * b + 1]["d_mem"][sl])
        m["i_glgm"] = ct(resC[c]["d_rows"][:, sl].T)
        m["w_mupT"] = ct(ins["L1_attn_mlp_up_w"].T)
        m["w_mupb"] = ct(ins["L1_attn_mlp_up_b"].reshape(32, 128).T)
        m["w_mdownT"] = ct(ins["L1_attn_mlp_down_w"].T)
        m["w_mdownb"] = ct(ins["L1_attn_mlp_down_b"].reshape(1, D))
        m["w_moT"] = ct(ins["L1_attn_mem_out_w"].T)


def add_launchE_inputs(maps, resD, ins):
    f32 = np.float32
    ct = lambda a: np.ascontiguousarray(a, dtype=f32)
    x3 = {}
    for b in range(4):
        x3[b] = np.concatenate([resD[2 * b]["d_x3_own"], resD[2 * b + 1]["d_x3_own"]], axis=0)
    for c in range(8):
        b, r = c // 2, c % 2
        m = maps[c]
        own = x3[b][r * HT:(r + 1) * HT]
        oth = x3[b][(1 - r) * HT:(2 - r) * HT]
        m["i_x3T"] = ct(np.concatenate([own, oth], axis=0).T)
        m["i_x3own"] = ct(own)
        m["w_f1gT"] = ct(ins["L1_ffn_gate_w"].T)
        m["w_f1w1"] = ct(ins["L1_ffn_w1"].reshape(8 * 1024, 4096))
        m["w_f1w2"] = ct(ins["L1_ffn_w2"].reshape(8 * 4096, 1024))


def add_launchB_inputs(maps, resA):
    for c in range(8):
        b, r = c // 2, c % 2
        pa = resA[2 * b]["d_projpart"][r * HT:(r + 1) * HT]
        pb = resA[2 * b + 1]["d_projpart"][r * HT:(r + 1) * HT]
        maps[c]["i_proj_a"] = np.ascontiguousarray(pa)
        maps[c]["i_proj_b"] = np.ascontiguousarray(pb)



class Banks:

    def __init__(self):
        self.readers = {}

    def wdeps(self, name):
        return self.readers.pop(name, [])

    def read(self, name, tok):
        self.readers.setdefault(name, []).append(tok)


class Build:
    def __init__(self, nc, stack, taps=()):
        self.nc = nc
        self.p = Prog(nc)
        self.st = stack
        self.taps = set(taps)
        self.C = {}
        self.bk = Banks()
        self.tap_pending = []

    @contextlib.contextmanager
    def scoped(self):
        outer = self.st
        with contextlib.ExitStack() as sst:
            self.st = sst
            try:
                yield
            finally:
                self.st = outer

    def din(self, name, shape, dtype=F32):
        return self.nc.dram_tensor(name, list(shape), dtype, kind="ExternalInput").ap()

    def dint(self, name, shape, dtype=F32):
        ap = self.nc.dram_tensor(name, list(shape), dtype, kind="Internal").ap()
        if name in self.taps:
            tap = self.nc.dram_tensor("tap_" + name, list(shape), dtype,
                                      kind="ExternalOutput").ap()
            self.tap_pending.append((tap, ap))
        return ap

    def dout(self, name, shape, dtype=F32):
        return self.nc.dram_tensor(name, list(shape), dtype, kind="ExternalOutput").ap()

    def sb(self, name, shape, dtype=F32):
        return self.st.enter_context(self.nc.sbuf_tensor(name, list(shape), dtype))

    def ps(self, name, shape, dtype=F32):
        return self.st.enter_context(self.nc.psum_tensor(name, list(shape), dtype))

    def load_consts(self):
        deps = []
        for n, (shp, dt) in CONST_SPECS.items():
            d = self.din(n, shp, dt)
            self.C[n] = self.sb("s" + n, shp, dt)
            deps.append(self.p.dma("sync", self.C[n][:], d))
        self.cdeps = deps
        self.ktile = self.sb("sk_consts", (128, 32), F32)
        self.kvals = {}

    def kcol(self, val):
        if val not in self.kvals:
            i = len(self.kvals)
            assert i < 32
            ap = self.ktile[:, i:i + 1]
            tok = self.p.memset(ap, float(val), tag=f"kcol{i}")
            self.kvals[val] = (ap, tok)
        return self.kvals[val]


def stage_S1(bd):
    p, C = bd.p, bd.C
    i_xT = bd.din("i_xT", (D, T))
    w_qT = bd.din("w_qT", (D, 512))
    w_kT = bd.din("w_kT", (D, 128))
    w_vT = bd.din("w_vT", (D, 128))
    w_oT = bd.din("w_oT", (512, D))
    d_projpart = bd.dout("d_projpart", (T, D))

    xT = [bd.sb(f"s1_xT{i}", (128, T), BF16) for i in range(8)]
    qp = [bd.sb(f"s1_qp{i}", (128, T), BF16) for i in range(4)]
    kaug = [bd.sb(f"s1_ka{i}", (65, T), BF16) for i in range(2)]
    vt = bd.sb("s1_vt", (128, 130 * 32), BF16)
    wq = bd.sb("s1_wq", (128, 8 * 512), BF16)
    wk = bd.sb("s1_wk", (128, 8 * 128), BF16)
    wv = bd.sb("s1_wv", (128, 8 * 128), BF16)
    wo = bd.sb("s1_wo", (128, 4 * 1024), BF16)
    qaug = [bd.sb(f"s1_qaug{j}", (65, 512), BF16) for j in range(2)]
    ptile = [[bd.sb(f"s1_pt{j}_{i}", (128, 512), BF16) for i in range(2)] for j in range(2)]
    sq = bd.sb("s1_sq", (128, 512), BF16)
    mhat = bd.sb("s1_mhat", (1, 8), F32)
    mtmp = bd.sb("s1_mtmp", (1, 1), F32)
    augrow = bd.sb("s1_augrow", (1, 512), BF16)
    rrow = [bd.sb(f"s1_rrow{j}", (1, 512), F32) for j in range(2)]
    rb = [bd.sb(f"s1_rb{j}", (64, 512), F32) for j in range(2)]
    ao = [bd.sb(f"s1_ao{i}", (128, T), BF16) for i in range(4)]
    stg = [bd.sb(f"s1_stg{i}", (128, 1024), F32) for i in range(2)]

    psA = bd.ps("s1psA", (128, 512))
    psB = bd.ps("s1psB", (128, 512))
    psC = [bd.ps("s1psC0", (128, 512)), bd.ps("s1psC1", (128, 512))]
    psD = [bd.ps("s1psD0", (128, 512)), bd.ps("s1psD1", (128, 512))]

    dx = [p.dma("gpsimd", xT[i][:], i_xT[128 * i:128 * (i + 1), :]) for i in range(8)]
    dwq = [p.dma("gpsimd", wq[:, 512 * i:512 * (i + 1)], w_qT[128 * i:128 * (i + 1), :])
           for i in range(8)]
    dwk = [p.dma("gpsimd", wk[:, 128 * i:128 * (i + 1)], w_kT[128 * i:128 * (i + 1), :])
           for i in range(8)]
    dwv = [p.dma("gpsimd", wv[:, 128 * i:128 * (i + 1)], w_vT[128 * i:128 * (i + 1), :])
           for i in range(8)]
    dwo = [p.dma("gpsimd", wo[:, 1024 * i:1024 * (i + 1)], w_oT[128 * i:128 * (i + 1), :])
           for i in range(4)]

    tk1 = p.memset(kaug[0][64:65, :], 1.0)
    tk2 = p.memset(kaug[1][64:65, :], 1.0)
    tv0 = p.memset(vt[:, :], 0.0)
    tvones = [p.memset(vt[:, 130 * kc + cc:130 * kc + cc + 1], 1.0, deps=[tv0])
              for kc in range(32) for cc in (64, 129)]
    tmh = p.memset(mhat[:, :], 0.0)

    bk = bd.bk

    ev_q = {}
    mhat_tok = tmh
    sqname = "s1sq"
    for pr in range(4):
        for qc in range(8):
            ops = [(wq[:, 512 * dm + 128 * pr:512 * dm + 128 * (pr + 1)],
                    xT[dm][:, 512 * qc:512 * (qc + 1)], [dwq[dm], dx[dm]])
                   for dm in range(8)]
            ps_ = psA if (pr * 8 + qc) % 2 == 0 else psB
            tqm = p.mmgroup(ps_[:, :], ops, deps_first=bk.wdeps(ps_.name), tag=f"qproj{pr}_{qc}")
            te = p.act(qp[pr][:, 512 * qc:512 * (qc + 1)], ps_[:, :], AF.Copy,
                       deps=[tqm], tag=f"qev{pr}_{qc}")
            bk.read(ps_.name, te)
            tsq = p.act(sq[:, :], ps_[:, :], AF.Square,
                        deps=[tqm] + bk.wdeps(sqname), tag="qsq")
            bk.read(ps_.name, tsq)
            tm1 = p.mm(psD[0][0:1, :], C["c_ones_col_b"][:, :], sq[:, :], True, True,
                       deps=[tsq] + bk.wdeps("s1psD0") + (bd.cdeps if pr == 0 and qc == 0 else []),
                       tag="qn")
            bk.read(sqname, tm1)
            tm2 = p.red(mtmp[:, :], psD[0][0:1, :], ALU.max, deps=[tm1, mhat_tok], tag="qmax")
            bk.read("s1psD0", tm2)
            mhat_tok = p.tt(mhat[0:1, 0:1], mhat[0:1, 0:1], mtmp[:, :], ALU.max,
                            deps=[tm2], tag="qmaxacc")
            ev_q[(pr, qc)] = te

    ev_k = {}
    for qc in range(8):
        ops = [(wk[:, 128 * dm:128 * (dm + 1)], xT[dm][:, 512 * qc:512 * (qc + 1)],
                [dwk[dm], dx[dm]]) for dm in range(8)]
        ps_ = psA if qc % 2 == 0 else psB
        tkm = p.mmgroup(ps_[:, :], ops, deps_first=bk.wdeps(ps_.name), tag=f"kproj{qc}")
        te0 = p.act(kaug[0][0:64, 512 * qc:512 * (qc + 1)], ps_[0:64, :], AF.Copy,
                    deps=[tkm, tk1 if qc == 0 else None])
        te1 = p.act(kaug[1][0:64, 512 * qc:512 * (qc + 1)], ps_[64:128, :], AF.Copy,
                    deps=[tkm, tk2 if qc == 0 else None])
        bk.read(ps_.name, te0)
        bk.read(ps_.name, te1)
        tsq = p.act(sq[:, :], ps_[:, :], AF.Square, deps=[tkm] + bk.wdeps(sqname), tag="ksq")
        bk.read(ps_.name, tsq)
        tm1 = p.mm(psD[0][0:1, :], C["c_ones_col_b"][:, :], sq[:, :], True, True,
                   deps=[tsq] + bk.wdeps("s1psD0"))
        bk.read(sqname, tm1)
        tm2 = p.red(mtmp[:, :], psD[0][0:1, :], ALU.max, deps=[tm1, mhat_tok])
        bk.read("s1psD0", tm2)
        mhat_tok = p.tt(mhat[0:1, 1:2], mhat[0:1, 1:2], mtmp[:, :], ALU.max, deps=[tm2])
        ev_k[qc] = (te0, te1)

    vdeps = list(tvones)
    for tc in range(32):
        ops = [(xT[dm][:, 128 * tc:128 * (tc + 1)], wv[:, 128 * dm:128 * (dm + 1)],
                [dwv[dm], dx[dm]]) for dm in range(8)]
        pv_ = psC[tc % 2]
        tvm = p.mmgroup(pv_[:, 0:128], ops,
                        deps_first=bk.wdeps(pv_.name) + ([tv0] if tc == 0 else []),
                        tag=f"vproj{tc}")
        tva = p.act(vt[:, 130 * tc:130 * tc + 64], pv_[:, 0:64], AF.Copy, deps=[tvm])
        tvb = p.act(vt[:, 130 * tc + 65:130 * tc + 129], pv_[:, 64:128], AF.Copy, deps=[tvm])
        bk.read(pv_.name, tva)
        bk.read(pv_.name, tvb)
        vdeps += [tva, tvb]

    t1 = p.tt(mhat[0:1, 2:3], mhat[0:1, 0:1], mhat[0:1, 1:2], ALU.mult, deps=[mhat_tok])
    t2 = p.act(mhat[0:1, 3:4], mhat[0:1, 2:3], AF.Sqrt, deps=[t1])
    t3 = p.ts(mhat[0:1, 4:5], mhat[0:1, 3:4], -1.0, None, ALU.mult, deps=[t2])
    t4 = p.memset(augrow[:, :], 1.0)
    t5 = p.ts(augrow[:, :], augrow[:, :], mhat[0:1, 4:5], None, ALU.mult, deps=[t3, t4])
    taug = [p.vcopy(qaug[j][64:65, 0:512], augrow[:, :], deps=[t5]) for j in range(2)]

    first_att = {0: True, 1: True}

    def emit_evac(st):
        j, tpv, pr_, lo_, qc_, h_ = st
        pcj, pdj = psC[j], psD[j]
        trc = p.recip(rrow[j][:, :], pcj[64:65, :], deps=[tpv] + bk.wdeps(("rrow", j)),
                      tag=f"rc{h_}_{qc_}")
        bk.read(pcj.name, trc)
        tbc = p.mm(pdj[0:64, :], C["c_ones_row_f"][0:1, 0:64], rrow[j][:, :], True, True,
                   deps=[trc] + bk.wdeps(pdj.name), tag=f"bc{h_}_{qc_}")
        bk.read(("rrow", j), tbc)
        tev = p.vcopy(rb[j][:, :], pdj[0:64, :], deps=[tbc] + bk.wdeps(("rb", j)))
        bk.read(pdj.name, tev)
        tao = p.tt(ao[pr_][lo_:lo_ + 64, 512 * qc_:512 * (qc_ + 1)], pcj[0:64, :],
                   rb[j][:, :], ALU.mult, deps=[tev], tag=f"aoev{h_}_{qc_}")
        bk.read(pcj.name, tao)
        bk.read(("rb", j), tao)
        bk.read(("qaug", j), tao)

    pending = None
    for h in range(8):
        pr, lo = h // 2, (h % 2) * 64
        kv = h // 4
        for qc in range(8):
            j = (h * 8 + qc) % 2
            qa, pcj = qaug[j], psC[j]
            tq = p.vcopy(qa[0:64, :], qp[pr][lo:lo + 64, 512 * qc:512 * (qc + 1)],
                         deps=[ev_q[(pr, qc)], taug[j]] + bk.wdeps(("qaug", j)),
                         tag=f"qa{h}_{qc}")
            nkc = 4 * qc + 4
            tpv = None
            for kc in range(nkc):
                ps_ = psA if (kc + j) % 2 == 0 else psB
                diag = kc >= 4 * qc
                sdeps = [tq, ev_k[kc // 4][kv]] + bk.wdeps(ps_.name)
                tsm = p.mm(ps_[:, :], kaug[kv][:, 128 * kc:128 * (kc + 1)], qa[:, :],
                           True, not diag, deps=sdeps, tag=f"att_s{h}_{qc}_{kc}")
                if diag:
                    tsm = p.mm(ps_[:, :], C["c_ident"][:, :],
                               C["c_attnmask"][:, 512 * (kc - 4 * qc):512 * (kc - 4 * qc + 1)],
                               False, True, deps=[], tag=f"att_m{h}_{qc}_{kc}")
                pt = ptile[j][kc % 2]
                ptn = ("s1pt", j, kc % 2)
                texp = p.act(pt[:, :], ps_[:, :], AF.Exp, scale=0.125,
                             deps=[tsm] + bk.wdeps(ptn), tag=f"exp{h}_{qc}_{kc}")
                bk.read(ps_.name, texp)
                bk.read(("qaug", j), texp)
                pvdeps = [texp]
                if kc == 0:
                    pvdeps += bk.wdeps(pcj.name)
                    if first_att[j]:
                        pvdeps += vdeps
                        first_att[j] = False
                tpv = p.mm(pcj[0:65, :], vt[:, 130 * kc + 65 * kv:130 * kc + 65 * kv + 65],
                           pt[:, :], kc == 0, kc == nkc - 1, deps=pvdeps,
                           tag=f"pv{h}_{qc}_{kc}")
                bk.read(ptn, tpv)
                if kc == min(1, nkc - 1) and pending is not None:
                    emit_evac(pending)
                    pending = None
            pending = (j, tpv, pr, lo, qc, h)
    if pending is not None:
        emit_evac(pending)
        pending = None

    ao_done = [("vector", p.counts["vector"])]
    for tc in range(32):
        sg = stg[tc % 2]
        sgev = []
        for half in range(2):
            ps_ = psA if half == 0 else psB
            ops = [(ao[pr][:, 128 * tc:128 * (tc + 1)],
                    wo[:, 1024 * pr + 512 * half:1024 * pr + 512 * (half + 1)],
                    [dwo[pr]]) for pr in range(4)]
            tpm = p.mmgroup(ps_[:, :], ops,
                            deps_first=bk.wdeps(ps_.name) + (ao_done if tc == 0 else []),
                            tag=f"oproj{tc}_{half}")
            tse = p.scopy(sg[:, 512 * half:512 * (half + 1)], ps_[:, :],
                          deps=[tpm] + bk.wdeps((f"stg{tc % 2}", half)))
            bk.read(ps_.name, tse)
            sgev.append(tse)
        tdo = p.dma("sync", d_projpart[128 * tc:128 * (tc + 1), :], sg[:, :],
                    deps=sgev, tag=f"projout{tc}")
        bk.read((f"stg{tc % 2}", 0), tdo)
        bk.read((f"stg{tc % 2}", 1), tdo)
    bd.s1_out = d_projpart
    bd.s1_done = p.all_fences()


def stage_S2(bd):
    p, C = bd.p, bd.C
    i_pa = bd.din("i_proj_a", (HT, D))
    i_pb = bd.din("i_proj_b", (HT, D))
    i_x_own = bd.din("i_x_own", (HT, D))
    d_x1_own = bd.dint("d_x1_own", (HT, D))

    pa = [bd.sb(f"s2_pa{i}", (128, 1024), F32) for i in range(2)]
    pb = [bd.sb(f"s2_pb{i}", (128, 1024), F32) for i in range(2)]
    xo = [bd.sb(f"s2_xo{i}", (128, 1024), F32) for i in range(2)]
    s_ = bd.sb("s2_s", (128, 1024), F32)
    sqt = bd.sb("s2_sqt", (128, 1024), BF16)
    x1f = bd.sb("s2_x1f", (128, 1024), F32)
    colm = bd.sb("s2_colm", (128, 4), F32)
    x1T = bd.x1T
    x1tm = bd.sb("s2_x1tm", (128, 1024), BF16)
    psT = bd.ps("s2psT", (128, 512), BF16)
    kb, kbt = bd.kcol(1e-5)

    done = {}
    trtoks = []
    for tc in range(NTC):
        i = tc % 2
        dpa = p.dma("sync", pa[i][:, :], i_pa[128 * tc:128 * (tc + 1), :],
                    deps=[done.get(("pa", i))])
        dpb = p.dma("sync", pb[i][:, :], i_pb[128 * tc:128 * (tc + 1), :],
                    deps=[done.get(("pa", i))])
        dxo = p.dma("sync", xo[i][:, :], i_x_own[128 * tc:128 * (tc + 1), :],
                    deps=[done.get(("pa", i))])
        t0 = p.tt(s_[:, :], pa[i][:, :], pb[i][:, :], ALU.add,
                  deps=[dpa, dpb, done.get("s")])
        t1 = p.tt(s_[:, :], s_[:, :], xo[i][:, :], ALU.add, deps=[dxo, t0])
        t2 = p.red(colm[:, 0:1], s_[:, :], ALU.add, deps=[t1], negate=True)
        t3 = p.ts(colm[:, 0:1], colm[:, 0:1], 1.0 / D, None, ALU.mult, deps=[t2])
        t4 = p.act(sqt[:, :], s_[:, :], AF.Square, bias=colm[:, 0:1], accum_out=colm[:, 1:2],
                   deps=[t3, done.get("sq")])
        t5 = p.act(colm[:, 2:3], colm[:, 1:2], AF.Ln, bias=kb, scale=1.0 / D,
                   deps=[t4, kbt])
        t6 = p.act(colm[:, 2:3], colm[:, 2:3], AF.Exp, scale=-0.5, deps=[t5])
        t7 = p.ts(x1f[:, :], s_[:, :], colm[:, 0:1], colm[:, 2:3], ALU.add, ALU.mult,
                  deps=[t6, done.get("x1f")])
        t8 = p.vcopy(x1tm[:, :], x1f[:, :], deps=[t7, done.get("x1tm")])
        d1 = p.dma("sync", d_x1_own[128 * tc:128 * (tc + 1), :], x1f[:, :], deps=[t7])
        trt = []
        for dm in range(8):
            tt1 = p.tr(psT[:, 0:128], x1tm[:, 128 * dm:128 * (dm + 1)], C["c_ident"][:, :],
                       deps=[t8] + bd.bk.wdeps("s2psT"))
            tt2 = p.vcopy(x1T[dm][:, 128 * tc:128 * (tc + 1)], psT[:, 0:128], deps=[tt1])
            bd.bk.read("s2psT", tt2)
            trt.append(tt2)
        done[("pa", i)] = t1
        done["s"] = t7
        done["sq"] = t5
        done["x1f"] = d1
        done["x1tm"] = trt[-1]
        trtoks += trt
    bd.d_x1_own = d_x1_own
    bd.x1T = x1T
    bd.x1T_toks = trtoks


def stage_S3(bd):
    p, C = bd.p, bd.C
    w1d = bd.din("w_f0w1", (8 * 1024, 2048))
    w2d = bd.din("w_f0w2", (8 * 2048, 1024))
    b1d = bd.din("w_f0b1T", (2048, 8))
    x1T = bd.x1T
    yacc = bd.yacc
    bk = bd.bk

    w1sb = bd.sb("s3_w1", (128, 16384), BF16)
    w2sb = bd.sb("s3_w2", (128, 16384), BF16)
    hsb = bd.sb("s3_h", (128, 8192), BF16)
    b1sb = bd.sb("s3_b1", (128, 128), F32)
    psH = [bd.ps("s3psH0", (128, 512)), bd.ps("s3psH1", (128, 512))]
    psY = [bd.ps("s3psY0", (128, 512)), bd.ps("s3psY1", (128, 512))]

    tz = p.step("vector", lambda v: v.memset(yacc[:, :], 0.0), tag="yzero")
    db1 = p.dma("sync", b1sb[:, :].rearrange("p (c e) -> p c e", e=8),
                b1d.rearrange("(c p) e -> p c e", p=128), tag="b1")
    ydep = {"last": tz}
    for e in range(8):
        wfence = [p.fence("tensor")]
        dw1 = [p.dma("gpsimd", w1sb[:, 2048 * dm:2048 * (dm + 1)],
                     w1d[1024 * e + 128 * dm:1024 * e + 128 * (dm + 1), :],
                     deps=wfence if dm == 0 else (), tag=f"w1_{e}")
               for dm in range(8)]
        dw2 = [p.dma("gpsimd", w2sb[:, 1024 * hc:1024 * (hc + 1)],
                     w2d[2048 * e + 128 * hc:2048 * e + 128 * (hc + 1), :],
                     deps=wfence if hc == 0 else (), tag=f"w2_{e}")
               for hc in range(16)]
        for tg in range(4):
            prev_pe = [p.fence("tensor")]
            for hc in range(16):
                ps_ = psH[hc % 2]
                ops = [(w1sb[:, 2048 * dm + 128 * hc:2048 * dm + 128 * (hc + 1)],
                        x1T[dm][:, 512 * tg:512 * (tg + 1)],
                        [dw1[dm]] if tg == 0 else ()) for dm in range(8)]
                th = p.mmgroup(ps_[:, :], ops, deps_first=bk.wdeps(ps_.name),
                               tag=f"f0h{e}_{tg}_{hc}")
                tg_ = p.act(hsb[:, 512 * hc:512 * (hc + 1)], ps_[:, :], AF.Gelu,
                            bias=b1sb[:, 8 * hc + e:8 * hc + e + 1],
                            deps=[th, db1] + prev_pe, tag=f"f0g{e}_{tg}_{hc}")
                bk.read(ps_.name, tg_)
            for sub in range(4):
                for half in range(2):
                    ps_ = psY[(sub * 2 + half) % 2]
                    ops = [(hsb[:, 512 * hc + 128 * sub:512 * hc + 128 * (sub + 1)],
                            w2sb[:, 1024 * hc + 512 * half:1024 * hc + 512 * (half + 1)],
                            [dw2[hc]] if tg == 0 and e == 0 else ()) for hc in range(16)]
                    ty = p.mmgroup(ps_[:, :], ops,
                                   deps_first=bk.wdeps(ps_.name) + [p.fence("scalar")],
                                   tag=f"f0y{e}_{tg}_{sub}_{half}")
                    ysl = yacc[:, 1024 * (4 * tg + sub) + 512 * half:
                               1024 * (4 * tg + sub) + 512 * (half + 1)]
                    ta = p.tt(ysl, ysl, ps_[:, :], ALU.add, deps=[ty, ydep["last"]],
                              tag=f"f0a{e}_{tg}_{sub}_{half}")
                    bk.read(ps_.name, ta)
                    ydep["last"] = ta


def stage_S4(bd):
    p, C = bd.p, bd.C
    gTd = bd.din("w_f0gateT", (1024, 8))
    gbd = bd.din("w_f0gateb", (1, 8))
    rwd = bd.din("w_f0rmsw", (1, 8))
    b2d = bd.din("w_f0b2", (8, 1024))
    d_x2_own = bd.dout("d_x2_own", (HT, D))
    x1T, yacc = bd.x1T, bd.yacc
    bk = bd.bk

    gsb = bd.sb("s4_g", (128, 64), BF16)
    gbr = bd.sb("s4_gb", (1, 8), F32)
    rwr = bd.sb("s4_rw", (1, 8), F32)
    b2sb = bd.sb("s4_b2", (8, 1024), F32)
    ones8 = bd.sb("s4_on8", (8, 1), F32)
    b2row = bd.sb("s4_b2r", (1, 1024), F32)
    rwbc = bd.sb("s4_rwbc", (128, 8), F32)
    lg = bd.sb("s4_lg", (128, 8), F32)
    e2 = bd.sb("s4_e2", (128, 8), F32)
    cols = bd.sb("s4_cols", (128, 8), F32)
    x1c = [bd.sb(f"s4_x1c{i}", (128, 1024), F32) for i in range(2)]
    yb = bd.sb("s4_yb", (128, 1024), F32)
    sqt = bd.sb("s4_sqt", (128, 1024), BF16)
    x2f = bd.sb("s4_x2f", (128, 1024), F32)
    psA = bd.ps("s4psA", (128, 512))
    psB = bd.ps("s4psB", (128, 512))
    keps, kepst = bd.kcol(F32EPS)
    kb5, kb5t = bd.kcol(1e-5)

    dg = [p.dma("gpsimd", gsb[:, 8 * dm:8 * (dm + 1)], gTd[128 * dm:128 * (dm + 1), :])
          for dm in range(8)]
    dgb = p.dma("sync", gbr[:, :], gbd)
    drw = p.dma("sync", rwr[:, :], rwd)
    db2 = p.dma("sync", b2sb[:, :], b2d)
    t1s = p.memset(ones8[:, :], 1.0)
    tb1 = p.mm(psA[0:128, 0:8], C["c_ones_row_f"][:, :], rwr[:, :], True, True,
               deps=[drw] + bd.cdeps, tag="rwbc")
    tb2 = p.vcopy(rwbc[:, :], psA[0:128, 0:8], deps=[tb1])
    bk.read("s4psA", tb2)
    tb3 = p.mm(psB[0:1, 0:512], ones8[:, :], b2sb[:, 0:512], True, True,
               deps=[db2, t1s] + bk.wdeps("s4psB"), tag="b2r0")
    tb5 = p.vcopy(b2row[:, 0:512], psB[0:1, 0:512], deps=[tb3])
    tb4 = p.mm(psB[0:1, 0:512], ones8[:, :], b2sb[:, 512:1024], True, True,
               deps=[tb5], tag="b2r1")
    tb6 = p.vcopy(b2row[:, 512:1024], psB[0:1, 0:512], deps=[tb4])
    bk.read("s4psB", tb6)

    done = {}
    for tc in range(NTC):
        i = tc % 2
        dx1 = p.dma("sync", x1c[i][:, :], bd.d_x1_own[128 * tc:128 * (tc + 1), :],
                    deps=[done.get(("x1", i))])
        ops = [(x1T[dm][:, 128 * tc:128 * (tc + 1)], gsb[:, 8 * dm:8 * (dm + 1)],
                [dg[dm]] if tc == 0 else ()) for dm in range(8)]
        tm = p.mmgroup(psA[0:128, 0:8], ops, deps_first=bk.wdeps("s4psA"), tag=f"lg{tc}")
        tm2 = p.mm(psA[0:128, 0:8], C["c_ones_row_f"][0:1, :], gbr[:, :], False, True,
                   deps=[dgb], tag=f"lgb{tc}")
        te = p.vcopy(lg[:, :], psA[0:128, 0:8], deps=[tm2] + bk.wdeps("s4lg"))
        bk.read("s4psA", te)
        t4 = p.act(sqt[:, 0:8], lg[:, :], AF.Square, accum_out=cols[:, 0:1],
                   deps=[te, done.get("sq8")], tag=f"sq8{tc}")
        t5 = p.act(cols[:, 1:2], cols[:, 0:1], AF.Ln, bias=keps, scale=1.0 / 8,
                   deps=[t4, kepst])
        t6 = p.act(cols[:, 1:2], cols[:, 1:2], AF.Exp, scale=-0.5, deps=[t5])
        t7 = p.ts(lg[:, :], lg[:, :], cols[:, 1:2], None, ALU.mult, deps=[t6])
        bk.read("s4lg", t7)
        t8 = p.tt(lg[:, :], lg[:, :], rwbc[:, :], ALU.mult, deps=[t7, tb2])
        t9 = p.red(cols[:, 2:3], lg[:, :], ALU.max, deps=[t8])
        t10 = p.ts(cols[:, 2:3], cols[:, 2:3], -2.0, None, ALU.mult, deps=[t9])
        t11 = p.act(e2[:, :], lg[:, :], AF.Exp, bias=cols[:, 2:3], scale=2.0,
                    accum_out=cols[:, 3:4], deps=[t10, done.get("e2"), done.get("ng")],
                    tag=f"sm{tc}")
        done["e2"] = t11
        bk.read("s4lg", t11)
        t12 = p.ts(cols[:, 3:4], cols[:, 3:4], 1e-6, 1.0, ALU.mult, ALU.add, deps=[t11])
        t13 = p.recip(cols[:, 3:4], cols[:, 3:4], deps=[t12])
        ysl = yacc[:, 1024 * tc:1024 * (tc + 1)]
        h0 = p.mm(psA[:, 0:512], C["c_ones_row_f"][0:1, :], b2row[:, 0:512], True, True,
                  deps=[tb5] + bk.wdeps("s4psA"), tag=f"b2bc0_{tc}")
        h1 = p.mm(psB[:, 0:512], C["c_ones_row_f"][0:1, :], b2row[:, 512:1024], True, True,
                  deps=[tb6] + bk.wdeps("s4psB"), tag=f"b2bc1_{tc}")
        ta0 = p.tt(yb[:, 0:512], ysl[:, 0:512], psA[:, 0:512], ALU.add,
                   deps=[h0, done.get("yb")])
        ta1 = p.tt(yb[:, 512:1024], ysl[:, 512:1024], psB[:, 0:512], ALU.add, deps=[h1])
        bk.read("s4psA", ta0)
        bk.read("s4psB", ta1)
        ts_ = p.stt(yb[:, :], yb[:, :], cols[:, 3:4], x1c[i][:, :], ALU.mult, ALU.add,
                    deps=[ta0, ta1, t13, dx1], tag=f"comb{tc}")
        done[("x1", i)] = ts_
        done["yb"] = ts_
        done["ng"] = ts_
        l2 = p.red(cols[:, 4:5], yb[:, :], ALU.add, deps=[ts_, done.get("lncols")],
                   negate=True)
        l3 = p.ts(cols[:, 4:5], cols[:, 4:5], 1.0 / D, None, ALU.mult, deps=[l2])
        l4 = p.act(sqt[:, :], yb[:, :], AF.Square, bias=cols[:, 4:5],
                   accum_out=cols[:, 5:6], deps=[l3, done.get("sqt")])
        done["sqt"] = l4
        done["sq8"] = l4
        l5 = p.act(cols[:, 5:6], cols[:, 5:6], AF.Ln, bias=kb5, scale=1.0 / D,
                   deps=[l4, kb5t])
        l6 = p.act(cols[:, 5:6], cols[:, 5:6], AF.Exp, scale=-0.5, deps=[l5])
        l7 = p.ts(x2f[:, :], yb[:, :], cols[:, 4:5], cols[:, 5:6], ALU.add, ALU.mult,
                  deps=[l6, done.get("x2f")], tag=f"ln2_{tc}")
        done["lncols"] = l7
        d2 = p.dma("sync", d_x2_own[128 * tc:128 * (tc + 1), :], x2f[:, :], deps=[l7])
        done["x2f"] = d2


def stage_Ca(bd):
    p, C = bd.p, bd.C
    i_x2T = bd.din("i_x2T", (D, T))
    w_conv = bd.din("w_conv", (128, 32))
    w_gT = bd.din("w_l1gT", (D, D))
    w_gb = bd.din("w_l1gb", (128, 8))
    uT = bd.uT
    bk = bd.bk
    keps, kepst = bd.kcol(1e-6)

    xb = [bd.sb(f"ca_xb{i}", (128, 512), BF16) for i in range(2)]
    xnp = [bd.sb(f"ca_xnp{i}", (128, T + 4), BF16) for i in range(8)]
    gwT = bd.sb("ca_gwT", (128, 8 * 1024), BF16)
    gbc = bd.sb("ca_gbc", (128, 8), F32)
    cwc = bd.sb("ca_cw", (128, 32), F32)
    sq = bd.sb("ca_sq", (128, 512), BF16)
    rrow = bd.sb("ca_rrow", (1, T), F32)
    gsig = bd.sb("ca_gsig", (128, 512), BF16)
    cvf = bd.sb("ca_cvf", (128, 512), F32)
    cvs = bd.sb("ca_cvs", (128, 512), BF16)
    psA = bd.ps("capsA", (128, 512))
    psB = bd.ps("capsB", (128, 512))
    psR = bd.ps("capsR", (1, 512))

    dgw = [p.dma("gpsimd", gwT[:, 1024 * i:1024 * (i + 1)], w_gT[128 * i:128 * (i + 1), :])
           for i in range(8)]
    dgb = p.dma("sync", gbc[:, :], w_gb)
    dcw = p.dma("sync", cwc[:, :], w_conv)
    tpad = [p.memset(xnp[i][:, 0:4], 0.0) for i in range(8)]

    for tc in range(8):
        mmlast = None
        for dm in range(8):
            ib = dm % 2
            dx = p.dma("gpsimd", xb[ib][:, :],
                       i_x2T[128 * dm:128 * (dm + 1), 512 * tc:512 * (tc + 1)],
                       deps=bk.wdeps(("ca_xb", ib)), tag=f"x2p1_{tc}_{dm}")
            tq = p.act(sq[:, :], xb[ib][:, :], AF.Square,
                       deps=[dx] + bk.wdeps("ca_sq") + (bd.cdeps if dm == 0 and tc == 0 else []),
                       tag=f"sq{tc}_{dm}")
            bk.read(("ca_xb", ib), tq)
            mmlast = p.mm(psR[:, :], C["c_ones_col_b"][:, :], sq[:, :], dm == 0, dm == 7,
                          deps=[tq] + (bk.wdeps("capsR") if dm == 0 else []),
                          tag=f"ssq{tc}_{dm}")
            bk.read("ca_sq", mmlast)
        t1 = p.act(rrow[:, 512 * tc:512 * (tc + 1)], psR[:, :], AF.Ln, bias=keps[0:1, :],
                   scale=1.0 / D, deps=[mmlast, kepst])
        bk.read("capsR", t1)
        t2 = p.act(rrow[:, 512 * tc:512 * (tc + 1)], rrow[:, 512 * tc:512 * (tc + 1)],
                   AF.Exp, scale=-0.5, deps=[t1], tag=f"rstd{tc}")
    rstd_done = p.fence("scalar")

    for dm in range(8):
        for tc in range(8):
            ib = (dm * 8 + tc) % 2
            dx = p.dma("gpsimd", xb[ib][:, :],
                       i_x2T[128 * dm:128 * (dm + 1), 512 * tc:512 * (tc + 1)],
                       deps=bk.wdeps(("ca_xb", ib)), tag=f"x2p2_{dm}_{tc}")
            ps_ = psA if tc % 2 == 0 else psB
            tb = p.mm(ps_[:, :], C["c_ones_row_f"][:, :],
                      rrow[0:1, 512 * tc:512 * (tc + 1)], True, True,
                      deps=[rstd_done] + bk.wdeps(ps_.name), tag=f"rbc{dm}_{tc}")
            tx = p.tt(xnp[dm][:, 4 + 512 * tc:4 + 512 * (tc + 1)],
                      xb[ib][:, :], ps_[:, :], ALU.mult,
                      deps=[tb, dx], tag=f"xn{dm}_{tc}")
            bk.read(ps_.name, tx)
            bk.read(("ca_xb", ib), tx)

    xnp_done = [p.fence("vector")] + tpad
    for dm in range(8):
        for tc in range(8):
            gdeps = xnp_done
            ps_ = psA if tc % 2 == 0 else psB
            ops = [(gwT[:, 1024 * dmi + 128 * dm:1024 * dmi + 128 * (dm + 1)],
                    xnp[dmi][:, 4 + 512 * tc:4 + 512 * (tc + 1)],
                    [dgw[dmi]] if tc == 0 else ()) for dmi in range(8)]
            tgm = p.mmgroup(ps_[:, :], ops, deps_first=bk.wdeps(ps_.name) + gdeps,
                            tag=f"gate{dm}_{tc}")
            tsg = p.act(gsig[:, :], ps_[:, :], AF.Sigmoid, bias=gbc[:, dm:dm + 1],
                        deps=[tgm, dgb] + bk.wdeps("ca_gsig"), tag=f"sig{dm}_{tc}")
            bk.read(ps_.name, tsg)
            base = 4 + 512 * tc
            tcv = p.ts(cvf[:, :], xnp[dm][:, base - 3:base - 3 + 512], cwc[:, 4 * dm:4 * dm + 1],
                       None, ALU.mult, deps=[dcw] + gdeps + bk.wdeps("ca_cvf"),
                       tag=f"cv0_{dm}_{tc}")
            for j in range(1, 4):
                tcv = p.stt(cvf[:, :], xnp[dm][:, base - 3 + j:base - 3 + j + 512],
                            cwc[:, 4 * dm + j:4 * dm + j + 1], cvf[:, :], ALU.mult, ALU.add,
                            deps=[tcv], tag=f"cv{j}_{dm}_{tc}")
            tsl = p.act(cvs[:, :], cvf[:, :], AF.Silu, deps=[tcv] + bk.wdeps("ca_cvs"))
            bk.read("ca_cvf", tsl)
            tu = p.tt(uT[dm][:, 512 * tc:512 * (tc + 1)], cvs[:, :], gsig[:, :], ALU.mult,
                      deps=[tsl, tsg], tag=f"u{dm}_{tc}")
            bk.read("ca_cvs", tu)
            bk.read("ca_gsig", tu)


def stage_Cb(bd):
    p, C = bd.p, bd.C
    w_keysT = bd.din("w_keysT", (D, 96))
    w_bitsT = bd.din("w_bitsT", (D, 36))
    w_powR = bd.din("w_powR", (36, 1))
    w_powW = bd.din("w_powW", (36, 1))
    w_g3T = bd.din("w_g3T", (D, 3))
    w_g3b = bd.din("w_g3b", (1, 3))
    d_rows = bd.dout("d_rows", (2, T))
    d_wgl = bd.dint("d_wgl", (2, T))
    uT, bk = bd.uT, bd.bk

    kT = bd.sb("cb_kT", (128, 8 * 96), BF16)
    bT = bd.sb("cb_bT", (128, 8 * 36), BF16)
    g3 = bd.sb("cb_g3", (128, 8 * 3), BF16)
    powR = bd.sb("cb_powR", (36, 1), BF16)
    powW = bd.sb("cb_powW", (36, 1), BF16)
    g3b = bd.sb("cb_g3b", (1, 3), F32)
    bits01 = bd.sb("cb_b01", (36, 512), BF16)
    sg3 = bd.sb("cb_sg3", (128, 512), F32)
    zrow = bd.sb("cb_zr", (1, 512), F32)
    lrow = bd.sb("cb_lr", (1, 512), F32)
    trow = bd.sb("cb_tr", (1, 512), F32)
    psA = bd.ps("cbpsA", (128, 512))
    psB = bd.ps("cbpsB", (128, 512))

    dk = [p.dma("gpsimd", kT[:, 96 * i:96 * (i + 1)], w_keysT[128 * i:128 * (i + 1), :])
          for i in range(8)]
    db = [p.dma("gpsimd", bT[:, 36 * i:36 * (i + 1)], w_bitsT[128 * i:128 * (i + 1), :])
          for i in range(8)]
    dg3 = [p.dma("gpsimd", g3[:, 3 * i:3 * (i + 1)], w_g3T[128 * i:128 * (i + 1), :])
           for i in range(8)]
    dpR = p.dma("gpsimd", powR[:, :], w_powR)
    dpW = p.dma("gpsimd", powW[:, :], w_powW)
    dgb = p.dma("sync", g3b[:, :], w_g3b)

    for tc in range(8):
        blk = slice(512 * tc, 512 * (tc + 1))
        ops = [(kT[:, 96 * dm:96 * (dm + 1)], uT[dm][:, blk], [dk[dm]]) for dm in range(8)]
        tk = p.mmgroup(psA[0:96, :], ops, deps_first=bk.wdeps("cbpsA") + [p.fence("vector")],
                       tag=f"keys{tc}")
        e1 = p.act(bd.qkr[:, blk], psA[0:32, :], AF.Copy, scale=float(1.0 / np.sqrt(32.0)),
                   deps=[tk])
        e2 = p.scopy(bd.wkr[:, blk], psA[32:64, :], deps=[tk])
        e3 = p.scopy(bd.wvr[:, blk], psA[64:96, :], deps=[tk])
        for t_ in (e1, e2, e3):
            bk.read("cbpsA", t_)
        ops = [(bT[:, 36 * dm:36 * (dm + 1)], uT[dm][:, blk], [db[dm]]) for dm in range(8)]
        tb = p.mmgroup(psB[0:36, :], ops, deps_first=bk.wdeps("cbpsB"), tag=f"bits{tc}")
        t01 = p.ts(bits01[:, :], psB[0:36, :], 0.0, None, ALU.is_gt,
                   deps=[tb] + bk.wdeps("cb_b01"), tag=f"b01_{tc}")
        bk.read("cbpsB", t01)
        tR = p.mm(psA[0:1, :], powR[:, :], bits01[:, :], True, True,
                  deps=[dpR, t01] + bk.wdeps("cbpsA"), tag=f"ri{tc}")
        eR = p.vcopy(zrow[:, :], psA[0:1, :], deps=[tR] + bk.wdeps("cb_zr"))
        bk.read("cbpsA", eR)
        dR = p.dma("sync", bd.d_idx[0:1, blk], zrow[:, :], deps=[eR])
        tW = p.mm(psB[0:1, :], powW[:, :], bits01[:, :], True, True,
                  deps=[dpW] + bk.wdeps("cbpsB"), tag=f"wi{tc}")
        eW = p.vcopy(trow[:, :], psB[0:1, :], deps=[tW] + bk.wdeps("cb_tr"))
        bk.read("cbpsB", eW)
        dW = p.dma("sync", bd.d_idx[1:2, blk], trow[:, :], deps=[eW])
        bk.read("cb_zr", dR)
        bk.read("cb_tr", dW)
        bk.read("cb_b01", tW)
        ops = [(g3[:, 3 * dm:3 * (dm + 1)], uT[dm][:, blk], [dg3[dm]]) for dm in range(8)]
        tg3 = p.mmgroup(psA[0:3, :], ops, deps_first=bk.wdeps("cbpsA"), tag=f"g3{tc}")
        tg3b = p.mm(psA[0:3, :], g3b[:, :], C["c_ones_row512f"][:, :], False, True,
                    deps=[dgb], tag=f"g3b{tc}")
        tz = p.vcopy(zrow[:, :], psA[0:1, :], deps=[tg3b] + bk.wdeps("cb_zr"))
        tsg = p.act(sg3[0:3, :], psA[0:3, :], AF.Sigmoid,
                    deps=[tg3b] + bk.wdeps("cb_sg3"), tag=f"sg3{tc}")
        bk.read("cbpsA", tz)
        bk.read("cbpsA", tsg)
        dw1 = p.dma("sync", d_wgl[0:1, blk], sg3[0:1, :], deps=[tsg])
        dr1 = p.dma("sync", d_rows[0:1, blk], sg3[1:2, :], deps=[tsg])
        dr2 = p.dma("sync", d_rows[1:2, blk], sg3[2:3, :], deps=[tsg])
        bk.read("cb_sg3", dw1)
        bk.read("cb_sg3", dr1)
        bk.read("cb_sg3", dr2)
        l1 = p.act(lrow[:, :], zrow[:, :], AF.Abs, deps=[tz] + bk.wdeps("cb_lr"))
        l2 = p.act(lrow[:, :], lrow[:, :], AF.Exp, scale=-1.0, deps=[l1])
        kb1, kb1t = bd.kcol(1.0)
        l3 = p.act(lrow[:, :], lrow[:, :], AF.Ln, bias=kb1[0:1, :], deps=[l2, kb1t])
        l4 = p.ts(trow[:, :], zrow[:, :], 0.0, None, ALU.max, deps=[tz] + bk.wdeps("cb_tr"))
        bk.read("cb_zr", l4)
        l5 = p.tt(lrow[:, :], lrow[:, :], trow[:, :], ALU.add, deps=[l3, l4])
        bk.read("cb_tr", l5)
        l6 = p.ts(lrow[:, :], lrow[:, :], -1.0, -30.0, ALU.mult, ALU.max, deps=[l5])
        dw2 = p.dma("sync", d_wgl[1:2, blk], lrow[:, :], deps=[l6])
        bk.read("cb_lr", dw2)
    bd.d_rows = d_rows
    bd.d_wgl = d_wgl


def stage_Cc(bd):
    p, C = bd.p, bd.C
    w_sinT = bd.din("w_sinT", (D, 2048))
    w_soT = bd.din("w_soT", (2048, D))
    w_dec = bd.din("w_mydecay", (1, 2))
    d_s = bd.dint("d_s", (2048, T), BF16)
    d_longpart = bd.dout("d_longpart", (T, D))
    uT, bk = bd.uT, bd.bk

    sinb = [bd.sb(f"cc_sin{i}", (128, 1024), BF16) for i in range(2)]
    sob = bd.sb("cc_sob", (128, 16 * 1024), BF16)
    inpf = bd.sb("cc_inpf", (128, T), F32)
    sbf = bd.sb("cc_sbf", (128, T), BF16)
    sf = bd.sb("cc_sf", (128, T), F32)
    decrow = bd.sb("cc_dec", (1, 2), F32)
    dcol = bd.sb("cc_dcol", (128, 2), F32)
    dtile = [bd.sb(f"cc_dt{i}", (128, 512), F32) for i in range(2)]
    sblk = bd.sb("cc_sblk", (128, 16 * 512), BF16)
    stg = [bd.sb(f"cc_stg{i}", (128, 1024), F32) for i in range(2)]
    psA = bd.ps("ccpsA", (128, 512))
    psB = bd.ps("ccpsB", (128, 512))
    psC = bd.ps("ccpsC", (128, 512))

    dd = p.dma("sync", decrow[:, :], w_dec)
    t0 = p.act(decrow[:, :], decrow[:, :], AF.Sigmoid, deps=[dd])
    for kl in range(2):
        tb = p.mm(psC[:, 0:1], C["c_ones_row_f"][:, :], decrow[0:1, kl:kl + 1], True, True,
                  deps=[t0] + bk.wdeps("ccpsC"), tag=f"decbc{kl}")
        tc_ = p.vcopy(dcol[:, kl:kl + 1], psC[:, 0:1], deps=[tb])
        bk.read("ccpsC", tc_)
        tm = p.memset(dtile[kl][:, :], 1.0)
        tt_ = p.ts(dtile[kl][:, :], dtile[kl][:, :], dcol[:, kl:kl + 1], None, ALU.mult,
                   deps=[tc_, tm], tag=f"dtile{kl}")
        bk.read(f"cc_dt{kl}", tt_)

    sdone = []
    for kd in range(16):
        kl = kd // 8
        ib = kd % 2
        wdeps = bk.wdeps(f"cc_sin{ib}")
        dsin = [p.dma("gpsimd", sinb[ib][:, 128 * dm:128 * (dm + 1)],
                      w_sinT[128 * dm:128 * (dm + 1), 128 * kd:128 * (kd + 1)],
                      deps=wdeps if dm == 0 else ()) for dm in range(8)]
        scant = None
        for tc in range(8):
            ps_ = psA if tc % 2 == 0 else psB
            ops = [(sinb[ib][:, 128 * dm:128 * (dm + 1)], uT[dm][:, 512 * tc:512 * (tc + 1)],
                    [dsin[dm]]) for dm in range(8)]
            tm = p.mmgroup(ps_[:, :], ops, deps_first=bk.wdeps(ps_.name), tag=f"inp{kd}_{tc}")
            bk.read(f"cc_sin{ib}", tm)
            te = p.vcopy(inpf[:, 512 * tc:512 * (tc + 1)], ps_[:, :],
                         deps=[tm] + (bk.wdeps("cc_inpf") if tc == 0 else []))
            bk.read(ps_.name, te)
            init = 0.0 if tc == 0 else sf[:, 512 * tc - 1:512 * tc]
            scant = p.scan(sf[:, 512 * tc:512 * (tc + 1)], dtile[kl][:, :],
                           inpf[:, 512 * tc:512 * (tc + 1)], init,
                           deps=[te, scant] + (bk.wdeps("cc_sf") if tc == 0 else []),
                           tag=f"scan{kd}_{tc}")
            bk.read("cc_inpf", scant)
        tcast = p.vcopy(sbf[:, :], sf[:, :], deps=[scant] + bk.wdeps("cc_sbf"))
        bk.read("cc_sf", tcast)
        ds = p.dma("sync", d_s[128 * kd:128 * (kd + 1), :], sbf[:, :], deps=[tcast])
        bk.read("cc_sbf", ds)
        sdone.append(ds)
    dso = [p.dma("gpsimd", sob[:, 1024 * i:1024 * (i + 1)], w_soT[128 * i:128 * (i + 1), :])
           for i in range(16)]
    for tg in range(8):
        dblk = [p.dma("sync", sblk[:, 512 * kd:512 * (kd + 1)],
                      d_s[128 * kd:128 * (kd + 1), 512 * tg:512 * (tg + 1)],
                      deps=[sdone[kd]] + (bk.wdeps("cc_sblk") if kd == 0 else []))
                for kd in range(16)]
        for sub in range(4):
            sg = stg[sub % 2]
            sgdeps = bk.wdeps((f"cc_stg{sub % 2}",))
            for half in range(2):
                ps_ = psA if half == 0 else psB
                ops = [(sblk[:, 512 * kd + 128 * sub:512 * kd + 128 * (sub + 1)],
                        sob[:, 1024 * kd + 512 * half:1024 * kd + 512 * (half + 1)],
                        [dblk[kd], dso[kd]]) for kd in range(16)]
                tm = p.mmgroup(ps_[:, :], ops, deps_first=bk.wdeps(ps_.name),
                               tag=f"lp{tg}_{sub}_{half}")
                if sub == 3 and half == 1:
                    for kd in range(16):
                        bk.read("cc_sblk", tm)
                te = p.scopy(sg[:, 512 * half:512 * (half + 1)], ps_[:, :],
                             deps=[tm] + sgdeps)
                bk.read(ps_.name, te)
                sgdeps = []
            dl = p.dma("sync", d_longpart[512 * tg + 128 * sub:512 * tg + 128 * (sub + 1), :],
                       sg[:, :], deps=[p.fence("scalar")], tag=f"lout{tg}_{sub}")
            bk.read((f"cc_stg{sub % 2}",), dl)


def stage_Cd(bd):
    p, C = bd.p, bd.C
    d_mem = bd.dout("d_mem", (T, 66))
    d_wgl = bd.d_wgl
    bk = bd.bk
    BIG = 30000.0
    MAGIC = 12582912.0

    MKs = bd.sb("cd_MK", (128, 256), F32)
    MVs = bd.sb("cd_MV", (128, 256), F32)
    CNT = bd.sb("cd_CNT", (128, 4), F32)
    RTf = bd.sb("cd_RTf", (128, 512), F32)
    RTb = bd.sb("cd_RTb", (128, 512), BF16)
    WTf = bd.sb("cd_WTf", (128, 512), F32)
    WTb = bd.sb("cd_WTb", (128, 512), BF16)
    Wtf = bd.sb("cd_Wtf", (128, 512), F32)
    rib = bd.sb("cd_rib", (128, 128), F32)
    rirow = bd.sb("cd_rirow", (1, 128), F32)
    wirow = bd.sb("cd_wirow", (1, 128), F32)
    wglrow = bd.sb("cd_wglrow", (2, 128), F32)
    cols = bd.sb("cd_cols", (128, 16), F32)
    Wsc = bd.sb("cd_Wsc", (128, 512), F32)
    Lcs = bd.sb("cd_Lcs", (128, 512), F32)
    Lci = bd.sb("cd_Lci", (128, 512), F32)
    prod = bd.sb("cd_prod", (128, 128), F32)
    Dsb = bd.sb("cd_D", (128, 128), F32)
    Mful = bd.sb("cd_Mf", (128, 128), F32)
    expd = bd.sb("cd_expd", (128, 128), F32)
    P2b = bd.sb("cd_P2", (128, 128), BF16)
    arow = bd.sb("cd_arow", (1, 128), F32)
    kchT = bd.sb("cd_kchT", (128, 3 * 32), F32)
    wvTb = bd.sb("cd_wvTb", (128, 32), BF16)
    Wsc2 = bd.sb("cd_Wsc2", (128, 512), F32)
    vstate = bd.sb("cd_vst", (128, 32), F32)
    stream = bd.sb("cd_strm", (128, 66), F32)
    psE = bd.ps("cdpsE", (128, 128))
    psF = bd.ps("cdpsF", (128, 128))
    psG = bd.ps("cdpsG", (128, 128))
    psH = bd.ps("cdpsH", (128, 128))
    psQ = bd.ps("cdpsQ", (128, 128))
    psW = bd.ps("cdpsW", (128, 2))
    psT = bd.ps("cdpsT", (128, 128), BF16)
    psT2 = bd.ps("cdpsT2", (128, 32), BF16)

    tz = [p.memset(MKs[:, :], 0.0), p.memset(MVs[:, :], 0.0), p.memset(CNT[:, :], 0.0)]
    tz2z = p.memset(stream[:, :], 0.0)
    zt = p.fence("gpsimd")

    for ch in range(CD_CHUNKS):
        blk = slice(128 * ch, 128 * (ch + 1))
        chf = [p.fence("tensor"), p.fence("vector"), p.fence("scalar")]
        dgr = p.dma("sync", wglrow[:, :], d_wgl[:, blk], deps=chf + bk.wdeps("cd_wglrow"))
        tgt = p.tr(psW[:, :], wglrow[:, :], C["c_identf"][0:2, 0:2],
                   deps=[dgr] + bk.wdeps("cdpsW"), tag=f"wglT{ch}")
        bk.read("cd_wglrow", tgt)
        dg = p.vcopy(cols[:, 12:14], psW[:, :], deps=[tgt])
        bk.read("cdpsW", dg)
        dl = dg
        dri = p.dma("sync", rirow[:, :], bd.d_idx[0:1, blk], deps=chf)
        dwi = p.dma("sync", wirow[:, :], bd.d_idx[1:2, blk], deps=chf)
        t1 = p.mm(psE[:, :], C["c_ones_row_f"][:, :], rirow[:, :], True, True,
                  deps=chf + [dri] + bk.wdeps("cdpsE") + ([zt] if ch == 0 else []), tag=f"ribm{ch}")
        t2 = p.vcopy(rib[:, :], psE[:, :], deps=[t1] + bk.wdeps("cd_rib"))
        bk.read("cdpsE", t2)
        tr = []
        for g in range(4):
            tr.append(p.ts(RTf[:, 128 * g:128 * (g + 1)], rib[:, :], C["c_iotac"][:, g:g + 1],
                           None, ALU.is_equal, deps=[t2] + bk.wdeps(("cd_RTf", g)), tag=f"RT{ch}_{g}"))
        t3 = p.mm(psE[:, :], C["c_ones_row_f"][:, :], wirow[:, :], True, True,
                  deps=[dwi] + bk.wdeps("cdpsE"), tag=f"wibm{ch}")
        t4 = p.vcopy(rib[:, :], psE[:, :], deps=[t3] + tr)
        bk.read("cdpsE", t4)
        tw = []
        for g in range(4):
            tw.append(p.ts(WTf[:, 128 * g:128 * (g + 1)], rib[:, :], C["c_iotac"][:, g:g + 1],
                           None, ALU.is_equal, deps=[t4] + bk.wdeps(("cd_WTf", g)), tag=f"WT{ch}_{g}"))
        bk.read("cd_rib", tw[-1])
        tcb = p.vcopy(RTb[:, :], RTf[:, :], deps=tr + bk.wdeps("cd_RTb"))
        twb = p.vcopy(WTb[:, :], WTf[:, :], deps=tw + bk.wdeps("cd_WTb"))
        twt = []
        for g in range(4):
            a_ = p.tr(psT[:, 0:128], WTb[:, 128 * g:128 * (g + 1)], C["c_ident"][:, :],
                      deps=[twb] + chf + bk.wdeps("cdpsT"), tag=f"Wt{ch}_{g}")
            b_ = p.vcopy(Wtf[:, 128 * g:128 * (g + 1)], psT[:, 0:128],
                         deps=[a_] + bk.wdeps(("cd_Wtf", g)))
            bk.read("cdpsT", b_)
            twt.append(b_)
        tks = []
        for j, rowt in enumerate((bd.qkr, bd.wkr, bd.wvr)):
            a_ = p.tr(psT2[:, :], rowt[:, blk],
                      C["c_ident"][0:32, 0:32], deps=chf + bk.wdeps("cdpsT2"), tag=f"kT{ch}_{j}")
            b_ = p.vcopy(kchT[:, 32 * j:32 * (j + 1)], psT2[:, :],
                         deps=[a_] + bk.wdeps(("cd_kchT", j)))
            bk.read("cdpsT2", b_)
            tks.append(b_)
        tvb = p.vcopy(wvTb[:, :], kchT[:, 64:96], deps=[tks[2]] + bk.wdeps("cd_wvTb"))
        bk.read(("cd_kchT", 2), tvb)
        if CD_LEVEL < 2:
            dmz = p.dma("sync", d_mem[blk, :], stream[:, :], deps=[tz2z] + [tvb, twb, tcb] + twt + tks + [dg, c1z_dummy] if False else [tz2z, tvb, twb, tcb, dg] + twt + tks)
            continue
        cw = None
        for g in range(4):
            cw = p.mm(psF[:, 0:1], WTf[:, 128 * g:128 * (g + 1)], CNT[:, g:g + 1],
                      g == 0, g == 3, deps=([twb] if g == 0 else []) + (bk.wdeps("cdpsF") if g == 0 else []),
                      tag=f"cntw{ch}_{g}")
        c1 = p.vcopy(cols[:, 0:1], psF[:, 0:1], deps=[cw])
        bk.read("cdpsF", c1)
        eq = None
        for g in range(4):
            eq = p.mm(psF[:, 0:128], WTb[:, 128 * g:128 * (g + 1)], WTb[:, 128 * g:128 * (g + 1)],
                      g == 0, g == 3, deps=[c1] if g == 0 else [], tag=f"eqw{ch}_{g}")
        m1 = p.tt(prod[:, :], psF[:, 0:128], C["c_su_f"][:, :], ALU.mult,
                  deps=[eq] + bk.wdeps("cd_prod"), tag=f"mip{ch}")
        bk.read("cdpsF", m1)
        m2 = p.mm(psF[:, 0:1], prod[:, :], C["c_ones_col_f"][:, :], True, True,
                  deps=[m1] + bk.wdeps("cdpsF"), tag=f"ipc{ch}")
        bk.read("cd_prod", m2)
        m3 = p.vcopy(cols[:, 1:2], psF[:, 0:1], deps=[m2])
        bk.read("cdpsF", m3)
        m4 = p.tt(cols[:, 2:3], cols[:, 0:1], cols[:, 1:2], ALU.add, deps=[m3, c1])
        m5 = p.ts(cols[:, 2:3], cols[:, 2:3], 0.5, None, ALU.mult, deps=[m4])
        m6 = p.ts(cols[:, 3:4], cols[:, 2:3], MAGIC, None, ALU.add, deps=[m5])
        m7 = p.ts(cols[:, 3:4], cols[:, 3:4], MAGIC, None, ALU.subtract, deps=[m6])
        m8 = p.tt(cols[:, 4:5], cols[:, 2:3], cols[:, 3:4], ALU.subtract, deps=[m7])
        m9 = p.ts(cols[:, 4:5], cols[:, 4:5], 2.0, None, ALU.mult, deps=[m8])
        m10 = p.tt(cols[:, 4:5], cols[:, 4:5], cols[:, 4:5], ALU.mult, deps=[m9],
                   tag=f"sig{ch}")
        if CD_LEVEL < 3:
            dmz = p.dma("sync", d_mem[blk, :], stream[:, :], deps=[tz2z, m10])
            continue
        eqr = None
        for g in range(4):
            eqr = p.mm(psG[:, :], WTb[:, 128 * g:128 * (g + 1)], RTb[:, 128 * g:128 * (g + 1)],
                       g == 0, g == 3, deps=([tcb, twb] + bk.wdeps("cdpsG")) if g == 0 else [],
                       tag=f"eqr{ch}_{g}")
        qw = p.mm(psQ[:, :], bd.wkr[0:32, blk], bd.qkr[0:32, blk], True, True,
                  deps=bk.wdeps("cdpsQ"), tag=f"qw{ch}")
        scv = []
        for s_ in range(2):
            if s_ == 0:
                pp = p.ts(cols[:, 5:6], cols[:, 4:5], -1.0, 1.0, ALU.mult, ALU.add, deps=[m10])
            else:
                pp = p.vcopy(cols[:, 5:6], cols[:, 4:5], deps=[m10, scv[-1]])
            pl = p.tt(cols[:, 6:7], cols[:, 5:6], cols[:, 13:14], ALU.mult, deps=[pp, dl])
            ws = p.ts(Wsc[:, :], Wtf[:, :], cols[:, 6:7], None, ALU.mult,
                      deps=[pl] + twt + bk.wdeps("cd_Wsc"), tag=f"Wsc{ch}_{s_}")
            lt = []
            for g in range(4):
                a_ = p.mm(psE[:, :], Wsc[:, 128 * g:128 * (g + 1)], C["c_su_f"][:, :],
                          True, True, deps=[ws] + bk.wdeps("cdpsE"), tag=f"lcs{ch}_{s_}_{g}")
                b_ = p.vcopy(Lcs[:, 128 * g:128 * (g + 1)], psE[:, :],
                             deps=[a_] + bk.wdeps(("cd_Lcs", g)))
                bk.read("cdpsE", b_)
                a2 = p.mm(psE[:, :], Wsc[:, 128 * g:128 * (g + 1)], C["c_uti_f"][:, :],
                          True, True, deps=[b_] + bk.wdeps("cdpsE"), tag=f"lci{ch}_{s_}_{g}")
                b2 = p.vcopy(Lci[:, 128 * g:128 * (g + 1)], psE[:, :],
                             deps=[a2] + bk.wdeps(("cd_Lci", g)))
                bk.read("cdpsE", b2)
                lt.append((b_, b2))
            bk.read("cd_Wsc", lt[-1][1])
            am = None
            am2 = None
            for g in range(4):
                pr_ = p.tt(prod[:, :], RTf[:, 128 * g:128 * (g + 1)], Lcs[:, 128 * g:128 * (g + 1)],
                           ALU.mult, deps=[lt[g][0], tr[g]] + bk.wdeps("cd_prod"), tag=f"apr{ch}_{s_}_{g}")
                am = p.mm(psF[:, 0:1], prod[:, :], C["c_ones_col_f"][:, :], g == 0, g == 3,
                          deps=[pr_] + (bk.wdeps("cdpsF") if g == 0 else []))
                am2 = p.mm(psH[0:1, :], C["c_ones_col_f"][:, :], prod[:, :],
                           g == 0, g == 3, deps=[] if g else bk.wdeps("cdpsH"),
                           tag=f"arow{ch}_{s_}_{g}")
                bk.read("cd_prod", am2)
                bk.read(("cd_Lcs", g), am)
            a1 = p.vcopy(cols[:, 7:8], psF[:, 0:1], deps=[am])
            bk.read("cdpsF", a1)
            a2r = p.vcopy(arow[:, :], psH[0:1, :], deps=[am2] + bk.wdeps("cd_arow"))
            bk.read("cdpsH", a2r)
            aex = p.act(cols[:, 8:9], cols[:, 7:8], AF.Exp, deps=[a1])
            bm = None
            for g in range(4):
                pr_ = p.tt(prod[:, :], WTf[:, 128 * g:128 * (g + 1)], Lci[:, 128 * g:128 * (g + 1)],
                           ALU.mult, deps=[lt[g][1], tw[g]] + bk.wdeps("cd_prod"), tag=f"bpr{ch}_{s_}_{g}")
                bm = p.mm(psF[:, 0:1], prod[:, :], C["c_ones_col_f"][:, :], g == 0, g == 3,
                          deps=[pr_] + (bk.wdeps("cdpsF") if g == 0 else []))
                bk.read("cd_prod", bm)
            b1 = p.vcopy(cols[:, 9:10], psF[:, 0:1], deps=[bm])
            bk.read("cdpsF", b1)
            em = None
            for g in range(4):
                em = p.mm(psF[:, 0:1], WTf[:, 128 * g:128 * (g + 1)],
                          Lci[:, 128 * g + 127:128 * (g + 1)], g == 0, g == 3,
                          deps=([b1] + bk.wdeps("cdpsF")) if g == 0 else [],
                          tag=f"em{ch}_{s_}_{g}")
                bk.read(("cd_Lci", g), em)
            e1 = p.vcopy(cols[:, 10:11], psF[:, 0:1], deps=[em])
            bk.read("cdpsF", e1)
            d1 = p.tt(cols[:, 11:12], cols[:, 10:11], cols[:, 9:10], ALU.subtract,
                      deps=[e1, b1])
            d2 = p.act(cols[:, 11:12], cols[:, 11:12], AF.Exp, deps=[d1])
            d3 = p.tt(cols[:, 11:12], cols[:, 11:12], cols[:, 12:13], ALU.mult,
                      deps=[d2, dg])
            d4 = p.tt(cols[:, 11:12], cols[:, 11:12], cols[:, 5:6], ALU.mult, deps=[d3])
            ks = None
            for g in range(4):
                ks = p.mm(psF[:, 0:32], RTf[:, 128 * g:128 * (g + 1)],
                          MKs[:, 32 * (4 * s_ + g):32 * (4 * s_ + g + 1)], g == 0, g == 3,
                          deps=(bk.wdeps("cdpsF") if g == 0 else []), tag=f"ks{ch}_{s_}_{g}")
            p1 = p.tt(prod[:, 0:32], psF[:, 0:32], kchT[:, 0:32], ALU.mult,
                      deps=[ks, tks[0]] + bk.wdeps("cd_prod"))
            bk.read("cdpsF", p1)
            p2 = p.red(cols[:, 14:15], prod[:, 0:32], ALU.add, deps=[p1])
            bk.read("cd_prod", p2)
            p3 = p.tt(cols[:, 14:15], cols[:, 14:15], cols[:, 8:9], ALU.mult, deps=[p2, aex])
            vs = None
            for g in range(4):
                vs = p.mm(psF[:, 0:32], RTf[:, 128 * g:128 * (g + 1)],
                          MVs[:, 32 * (4 * s_ + g):32 * (4 * s_ + g + 1)], g == 0, g == 3,
                          deps=([p1] + bk.wdeps("cdpsF")) if g == 0 else [], tag=f"vs{ch}_{s_}_{g}")
            v1 = p.ts(vstate[:, :], psF[:, 0:32], cols[:, 8:9], None, ALU.mult,
                      deps=[vs, aex] + bk.wdeps("cd_vst"))
            bk.read("cdpsF", v1)
            db_ = p.mm(psE[:, :], C["c_ones_row_f"][:, :], arow[:, :], True, True,
                       deps=[a2r] + bk.wdeps("cdpsE"), tag=f"dbc{ch}_{s_}")
            bk.read("cd_arow", db_)
            dd_ = p.ts(Dsb[:, :], psE[:, :], cols[:, 9:10], None, ALU.subtract,
                       deps=[db_, b1] + bk.wdeps("cd_D"))
            bk.read("cdpsE", dd_)
            mf = p.stt(Mful[:, :], psG[:, :], cols[:, 5:6], C["c_su_f"][:, :],
                       ALU.mult, ALU.mult, deps=[eqr, pp] + bk.wdeps("cd_Mf"), tag=f"mf{ch}_{s_}")
            m1_ = p.stt(Dsb[:, :], Mful[:, :], BIG, Dsb[:, :], ALU.mult, ALU.add,
                        deps=[mf, dd_])
            bk.read("cd_Mf", m1_)
            m2_ = p.ts(Dsb[:, :], Dsb[:, :], -BIG, None, ALU.add, deps=[m1_])
            ex_ = p.act(expd[:, :], Dsb[:, :], AF.Exp, deps=[m2_] + bk.wdeps("cd_expd"))
            bk.read("cd_D", ex_)
            pb_ = p.ts(P2b[:, :], expd[:, :], cols[:, 12:13], None, ALU.mult,
                       deps=[ex_, dg] + bk.wdeps("cd_P2"))
            bk.read("cd_expd", pb_)
            pq = p.tt(prod[:, :], psQ[:, :], P2b[:, :], ALU.mult,
                      deps=[qw, pb_] + bk.wdeps("cd_prod"), tag=f"p2qw{ch}_{s_}")
            sci = p.mm(psF[:, 0:1], prod[:, :], C["c_ones_col_f"][:, :], True, True,
                       deps=[pq] + bk.wdeps("cdpsF"), tag=f"sci{ch}_{s_}")
            bk.read("cd_prod", sci)
            sc1 = p.tt(stream[:, s_:s_ + 1], psF[:, 0:1], cols[:, 14:15], ALU.add,
                       deps=[sci, p3] + bk.wdeps(("cd_strm", s_)))
            bk.read("cdpsF", sc1)
            vi = p.mm(psF[:, 0:32], P2b[:, :], wvTb[:, :], True, True,
                      deps=[pb_, tvb] + bk.wdeps("cdpsF"), tag=f"vi{ch}_{s_}")
            vv = p.tt(stream[:, 2 + 32 * s_:2 + 32 * (s_ + 1)], psF[:, 0:32], vstate[:, :],
                      ALU.add, deps=[vi, v1] + bk.wdeps(("cd_strmv", s_)))
            bk.read("cdpsF", vv)
            bk.read("cd_vst", vv)
            scv.append(vv)
            if CD_LEVEL < 4:
                continue
            ws2 = p.ts(Wsc2[:, :], Wtf[:, :], cols[:, 11:12], None, ALU.mult,
                       deps=[d4] + twt + bk.wdeps("cd_Wsc2"), tag=f"Wsc2{ch}_{s_}")
            for g in range(4):
                dc_ = p.act(cols[:, 15:16], Lci[:, 128 * g + 127:128 * (g + 1)], AF.Exp,
                            deps=[lt[g][1]] + bk.wdeps("cd_col15"),
                            tag=f"decf{ch}_{s_}_{g}")
                for (MM, kv, src, tksrc) in ((MKs, 0, kchT[:, 32:64], tks[1]),
                                             (MVs, 1, kchT[:, 64:96], tks[2])):
                    blkm = MM[:, 32 * (4 * s_ + g):32 * (4 * s_ + g + 1)]
                    u1 = p.ts(blkm, blkm, cols[:, 15:16], None, ALU.mult,
                              deps=[dc_, ks if kv == 0 else vs], tag=f"dec{ch}_{s_}_{g}_{kv}")
                    bk.read("cd_col15", u1)
                    im = p.mm(psF[:, 0:32], Wsc2[:, 128 * g:128 * (g + 1)], src,
                              True, True, deps=[ws2, tksrc] + bk.wdeps("cdpsF"),
                              tag=f"inj{ch}_{s_}_{g}_{kv}")
                    u2 = p.tt(blkm, blkm, psF[:, 0:32], ALU.add, deps=[im, u1])
                    bk.read("cdpsF", u2)
                    bk.read(("cd_MM", s_, g, kv), u2)
            bk.read("cd_Wsc2", u2)
        bk.read("cdpsG", scv[-1])
        bk.read("cdpsQ", scv[-1])
        if CD_LEVEL < 5:
            dmz = p.dma("sync", d_mem[blk, :], stream[:, :], deps=[tz2z, scv[0], scv[1]])
            continue
        for g in range(4):
            cm = p.mm(psF[:, 0:1], Wtf[:, 128 * g:128 * (g + 1)], C["c_ones_col_f"][:, :],
                      True, True, deps=[scv[-1] if g == 0 else None] + bk.wdeps("cdpsF"),
                      tag=f"cnt{ch}_{g}")
            cu = p.tt(CNT[:, g:g + 1], CNT[:, g:g + 1], psF[:, 0:1], ALU.add, deps=[cm])
            bk.read("cdpsF", cu)
            bk.read(("cd_Wtf", g), cu)
        dm_ = p.dma("sync", d_mem[blk, :], stream[:, :], deps=[scv[0], scv[1]])
        bk.read(("cd_strm", 0), dm_)
        bk.read(("cd_strm", 1), dm_)
        bk.read(("cd_strmv", 0), dm_)
        bk.read(("cd_strmv", 1), dm_)
        bk.read("cd_g", dm_)


C_STAGES = ("a", "b", "c", "d")
CD_CHUNKS = 32
CD_LEVEL = 9


def launch_C(bd):
    bd.uT = [bd.sb(f"uT{i}", (128, T), BF16) for i in range(8)]
    bd.qkr = bd.sb("qkr", (32, T), BF16)
    bd.wkr = bd.sb("wkr", (32, T), BF16)
    bd.wvr = bd.sb("wvr", (32, T), BF16)
    bd.d_idx = bd.nc.dram_tensor("d_idx", [2, T], F32, kind="Internal").ap()
    with bd.scoped():
        stage_Ca(bd)
    bd.p.barrier("b_ca")
    with bd.scoped():
        stage_Cb(bd)
    bd.p.barrier("b_cb")
    if "c" in C_STAGES:
        with bd.scoped():
            stage_Cc(bd)
        bd.p.barrier("b_cc")
    if "d" in C_STAGES:
        with bd.scoped():
            stage_Cd(bd)
    else:
        d_mem = bd.dout("d_mem", (T, 66))
        z = bd.sb("zz", (128, 66), F32)
        tz = bd.p.memset(z[:, :], 0.0)
        for i in range(32):
            bd.p.dma("sync", d_mem[128 * i:128 * (i + 1), :], z[:, :], deps=[tz])
    if "c" not in C_STAGES:
        d_longpart = bd.dout("d_longpart", (T, D))
        z2 = bd.sb("zz2", (128, 1024), F32)
        tz2 = bd.p.memset(z2[:, :], 0.0)
        for i in range(32):
            bd.p.dma("sync", d_longpart[128 * i:128 * (i + 1), :], z2[:, :], deps=[tz2])


def stage_D(bd):
    p, C = bd.p, bd.C
    i_x2 = bd.din("i_x2own2", (HT, D))
    i_lpa = bd.din("i_lpa", (HT, D))
    i_lpb = bd.din("i_lpb", (HT, D))
    i_ms0 = bd.din("i_ms0", (HT, 66))
    i_ms1 = bd.din("i_ms1", (HT, 66))
    i_gg = bd.din("i_glgm", (HT, 2))
    w_mupT = bd.din("w_mupT", (D, 4096))
    w_mupb = bd.din("w_mupb", (128, 32))
    w_mdT = bd.din("w_mdownT", (4096, D))
    w_mdb = bd.din("w_mdownb", (1, D))
    w_moT = bd.din("w_moT", (32, D))
    d_x3_own = bd.dout("d_x3_own", (HT, D))
    bk = bd.bk

    mup = bd.sb("d_mup", (128, 8 * 4096), BF16)
    mdn = bd.sb("d_mdn", (128, 32 * 1024), BF16)
    mdb = bd.sb("d_mdb", (1, 1024), F32)
    mupb = bd.sb("d_mupb", (128, 32), F32)
    moT = bd.sb("d_moT", (32, 1024), BF16)
    x2c = [bd.sb(f"d_x2c{i}", (128, 1024), F32) for i in range(2)]
    lpa = [bd.sb(f"d_lpa{i}", (128, 1024), F32) for i in range(2)]
    lpb = [bd.sb(f"d_lpb{i}", (128, 1024), F32) for i in range(2)]
    ms0 = [bd.sb(f"d_ms0{i}", (128, 66), F32) for i in range(2)]
    ms1 = [bd.sb(f"d_ms1{i}", (128, 66), F32) for i in range(2)]
    ggc = [bd.sb(f"d_gg{i}", (128, 2), F32) for i in range(2)]
    yb = [bd.sb(f"d_yb{i}", (128, 1024), F32) for i in range(2)]
    sc4 = [bd.sb(f"d_sc4{i}", (128, 8), F32) for i in range(2)]
    racc = [bd.sb(f"d_racc{i}", (128, 32), F32) for i in range(2)]
    rb16 = [bd.sb(f"d_rb16{i}", (128, 32), BF16) for i in range(2)]
    rdT = [bd.sb(f"d_rdT{i}", (32, 128), BF16) for i in range(2)]
    ynb = [bd.sb(f"d_ynb{i}", (128, 1024), BF16) for i in range(2)]
    ynT = [bd.sb(f"d_ynT{i}", (128, 1024), BF16) for i in range(2)]
    u2_ = bd.sb("d_u2", (128, 4096), BF16)
    u2 = [u2_, u2_]
    sqt_ = bd.sb("d_sqt", (128, 1024), BF16)
    sqt = [sqt_, sqt_]
    x3f = [bd.sb(f"d_x3f{i}", (128, 1024), F32) for i in range(2)]
    cols = [bd.sb(f"d_cols{i}", (128, 8), F32) for i in range(2)]
    psA = bd.ps("dpsA", (128, 512))
    psB = bd.ps("dpsB", (128, 512))
    psU = [bd.ps("dpsU0", (128, 512)), bd.ps("dpsU1", (128, 512))]
    psR = bd.ps("dpsR", (32, 128), BF16)
    psY = [bd.ps("dpsY0", (128, 128), BF16), bd.ps("dpsY1", (128, 128), BF16)]
    kb6, kb6t = bd.kcol(1e-6)
    kb5, kb5t = bd.kcol(1e-5)

    dmu = [p.dma("gpsimd", mup[:, 4096 * i:4096 * (i + 1)], w_mupT[128 * i:128 * (i + 1), :])
           for i in range(8)]
    dmd = [p.dma("gpsimd", mdn[:, 1024 * i:1024 * (i + 1)], w_mdT[128 * i:128 * (i + 1), :])
           for i in range(32)]
    dmb = p.dma("sync", mdb[:, :], w_mdb)
    dub = p.dma("sync", mupb[:, :], w_mupb)
    dmo = [p.dma("gpsimd", moT[0:32, 1024 * 0:1024 * 1], w_moT)]
    done = {}
    for tc in range(NTC):
        i = tc % 2
        blk = slice(128 * tc, 128 * (tc + 1))
        guard = [done.get(("in", i))]
        dx2 = p.dma("sync", x2c[i][:, :], i_x2[blk, :], deps=guard)
        dla = p.dma("sync", lpa[i][:, :], i_lpa[blk, :], deps=guard)
        dlb = p.dma("sync", lpb[i][:, :], i_lpb[blk, :], deps=guard)
        dm0 = p.dma("sync", ms0[i][:, :], i_ms0[blk, :], deps=guard)
        dm1 = p.dma("sync", ms1[i][:, :], i_ms1[blk, :], deps=guard)
        dgg = p.dma("sync", ggc[i][:, :], i_gg[blk, :], deps=guard)
        c0 = p.vcopy(sc4[i][:, 0:2], ms0[i][:, 0:2], deps=[dm0] + bk.wdeps(("d_sc4", i)))
        c1 = p.vcopy(sc4[i][:, 2:4], ms1[i][:, 0:2], deps=[dm1])
        r0 = p.red(cols[i][:, 0:1], sc4[i][:, 0:4], ALU.max,
                   deps=[c0, c1] + bk.wdeps(("d_cols", i)))
        r1 = p.ts(cols[i][:, 0:1], cols[i][:, 0:1], -1.0, None, ALU.mult, deps=[r0])
        r2 = p.act(sc4[i][:, 4:8], sc4[i][:, 0:4], AF.Exp, bias=cols[i][:, 0:1],
                   accum_out=cols[i][:, 1:2], deps=[r1, done.get("e4")], tag=f"sm4_{tc}")
        done["e4"] = r2
        r3 = p.recip(cols[i][:, 1:2], cols[i][:, 1:2], deps=[r2])
        r4 = p.ts(sc4[i][:, 4:8], sc4[i][:, 4:8], cols[i][:, 1:2], None, ALU.mult, deps=[r3])
        bk.read(("d_sc4", i), r4)
        v0 = p.ts(racc[i][:, :], ms0[i][:, 2:34], sc4[i][:, 4:5], None, ALU.mult,
                  deps=[r4, dm0] + bk.wdeps(("d_racc", i)), tag=f"rd0_{tc}")
        v1 = p.stt(racc[i][:, :], ms0[i][:, 34:66], sc4[i][:, 5:6], racc[i][:, :],
                   ALU.mult, ALU.add, deps=[v0])
        v2 = p.stt(racc[i][:, :], ms1[i][:, 2:34], sc4[i][:, 6:7], racc[i][:, :],
                   ALU.mult, ALU.add, deps=[v1, dm1])
        v3 = p.stt(racc[i][:, :], ms1[i][:, 34:66], sc4[i][:, 7:8], racc[i][:, :],
                   ALU.mult, ALU.add, deps=[v2], tag=f"rd3_{tc}")
        done[("in", i)] = v3
        v4 = p.vcopy(rb16[i][:, :], racc[i][:, :], deps=[v3] + bk.wdeps(("d_rb16", i)))
        bk.read(("d_racc", i), v4)
        tr1 = p.tr(psR[:, :], rb16[i][:, :], C["c_ident"][:, :], deps=[v4] + bk.wdeps("dpsR"),
                   tag=f"rT{tc}")
        tr2 = p.vcopy(rdT[i][:, :], psR[:, :], deps=[tr1] + bk.wdeps(("d_rdT", i)))
        bk.read("dpsR", tr2)
        bk.read(("d_rb16", i), tr2)
        for half in range(2):
            ps_ = psA if half == 0 else psB
            tm = p.mm(ps_[:, :], rdT[i][:, :], moT[0:32, 512 * half:512 * (half + 1)],
                      True, True, deps=[tr2, dmo[0]] + bk.wdeps(ps_.name), tag=f"mo{tc}_{half}")
            bk.read(("d_rdT", i), tm)
            hsl = slice(512 * half, 512 * (half + 1))
            t1 = p.stt(yb[i][:, hsl], ps_[:, :], ggc[i][:, 1:2], x2c[i][:, hsl],
                       ALU.mult, ALU.add,
                       deps=[tm, dgg, dx2] + (bk.wdeps(("d_yb", i)) if half == 0 else []),
                       tag=f"ymem{tc}_{half}")
            bk.read(ps_.name, t1)
        t2a = p.stt(yb[i][:, :], lpa[i][:, :], ggc[i][:, 0:1], yb[i][:, :], ALU.mult, ALU.add,
                    deps=[dla, t1], tag=f"ylonga{tc}")
        t2 = p.stt(yb[i][:, :], lpb[i][:, :], ggc[i][:, 0:1], yb[i][:, :], ALU.mult, ALU.add,
                   deps=[dlb, t2a], tag=f"ylongb{tc}")
        q1 = p.act(sqt[i][:, :], yb[i][:, :], AF.Square, accum_out=cols[i][:, 2:3],
                   deps=[t2] + bk.wdeps(("d_sqt",)))
        q2 = p.act(cols[i][:, 2:3], cols[i][:, 2:3], AF.Ln, bias=kb6, scale=1.0 / D,
                   deps=[q1, kb6t])
        q3 = p.act(cols[i][:, 2:3], cols[i][:, 2:3], AF.Exp, scale=-0.5, deps=[q2])
        q4 = p.ts(ynb[i][:, :], yb[i][:, :], cols[i][:, 2:3], None, ALU.mult,
                  deps=[q3] + bk.wdeps(("d_ynb", i)), tag=f"yn{tc}")
        trts = []
        for dm in range(8):
            psy = psY[dm % 2]
            w1_ = p.tr(psy[:, :], ynb[i][:, 128 * dm:128 * (dm + 1)], C["c_ident"][:, :],
                       deps=[q4] + bk.wdeps(psy.name), tag=f"ynT{tc}_{dm}")
            tt_ = p.vcopy(ynT[i][:, 128 * dm:128 * (dm + 1)], psy[:, :],
                          deps=[w1_] + (bk.wdeps(("d_ynT", i)) if dm == 0 else []))
            bk.read(psy.name, tt_)
            trts.append(tt_)
        bk.read(("d_ynb", i), trts[-1])
        for hc in range(32):
            psu = psU[hc % 2]
            ops = [(mup[:, 4096 * dm + 128 * hc:4096 * dm + 128 * (hc + 1)],
                    ynT[i][:, 128 * dm:128 * (dm + 1)],
                    [dmu[dm], trts[dm]] if hc == 0 else [dmu[dm]]) for dm in range(8)]
            tmu = p.mmgroup(psu[:, 0:128], ops, deps_first=bk.wdeps(psu.name),
                            tag=f"up{tc}_{hc}")
            bk.read(("d_ynT", i), tmu)
            tsl = p.act(u2[i][:, 128 * hc:128 * (hc + 1)], psu[:, 0:128],
                        AF.Silu, bias=mupb[:, hc:hc + 1],
                        deps=[tmu, dub] + bk.wdeps(("d_u2", hc)))
            bk.read(psu.name, tsl)
        upf = [p.fence("scalar")]
        for half in range(2):
            ps_ = psA if half == 0 else psB
            hsl = slice(512 * half, 512 * (half + 1))
            ops = [(u2[i][:, 128 * hc:128 * (hc + 1)],
                    mdn[:, 1024 * hc + 512 * half:1024 * hc + 512 * (half + 1)],
                    [dmd[hc]] if tc == 0 else ()) for hc in range(32)]
            tmd = p.mmgroup(ps_[:, :], ops, deps_first=bk.wdeps(ps_.name) + upf,
                            tag=f"dn{tc}_{half}")
            for hc in range(32):
                bk.read(("d_u2", hc), tmd)
            tb = p.mm(ps_[:, :], C["c_ones_row_f"][0:1, :], mdb[:, hsl], False, True,
                      deps=[dmb], tag=f"dnb{tc}_{half}")
            a1 = p.tt(x3f[i][:, hsl], ps_[:, :], yb[i][:, hsl], ALU.add,
                      deps=[tb] + (bk.wdeps(("d_x3f", i)) if half == 0 else []),
                      tag=f"res1_{tc}_{half}")
            a2 = p.tt(x3f[i][:, hsl], x3f[i][:, hsl], x2c[i][:, hsl], ALU.add, deps=[a1])
            bk.read(ps_.name, a2)
            done["yb"] = a2
        bk.read(("d_yb", i), done["yb"])
        l2 = p.red(cols[i][:, 4:5], x3f[i][:, :], ALU.add, deps=[done["yb"]], negate=True)
        l3 = p.ts(cols[i][:, 4:5], cols[i][:, 4:5], 1.0 / D, None, ALU.mult, deps=[l2])
        l4 = p.act(sqt[i][:, :], x3f[i][:, :], AF.Square, bias=cols[i][:, 4:5],
                   accum_out=cols[i][:, 5:6], deps=[l3, q1])
        l5 = p.act(cols[i][:, 5:6], cols[i][:, 5:6], AF.Ln, bias=kb5, scale=1.0 / D,
                   deps=[l4, kb5t])
        l6 = p.act(cols[i][:, 5:6], cols[i][:, 5:6], AF.Exp, scale=-0.5, deps=[l5])
        l7 = p.ts(x3f[i][:, :], x3f[i][:, :], cols[i][:, 4:5], cols[i][:, 5:6],
                  ALU.add, ALU.mult, deps=[l6], tag=f"lnx3_{tc}")
        bk.read(("d_x3f", i), l7)
        bk.read(("d_sqt",), l4)
        d3 = p.dma("sync", d_x3_own[blk, :], x3f[i][:, :], deps=[l7])
        bk.read(("d_x3f", i), d3)
        bk.read(("d_cols", i), l7)


def launch_D(bd):
    with bd.scoped():
        stage_D(bd)


def stage_E1(bd):
    p, C = bd.p, bd.C
    i_x3T = bd.din("i_x3T", (D, T))
    w_gT = bd.din("w_f1gT", (D, 8))
    bk = bd.bk
    x3F = [bd.sb(f"e1_x3F{i}", (128, T), BF16) for i in range(8)]
    gsb = bd.sb("e1_g", (128, 64), BF16)
    prc = bd.sb("e1_prc", (128, 8), F32)
    colz = bd.sb("e1_colz", (128, 2), F32)
    psA = bd.ps("e1psA", (128, 8))
    psT = bd.ps("e1psT", (8, 128))

    dx = [p.dma("gpsimd", x3F[i][:], i_x3T[128 * i:128 * (i + 1), :]) for i in range(8)]
    dg = [p.dma("gpsimd", gsb[:, 8 * i:8 * (i + 1)], w_gT[128 * i:128 * (i + 1), :])
          for i in range(8)]
    for tc in range(32):
        ops = [(x3F[dm][:, 128 * tc:128 * (tc + 1)], gsb[:, 8 * dm:8 * (dm + 1)],
                [dx[dm], dg[dm]]) for dm in range(8)]
        tm = p.mmgroup(psA[:, :], ops, deps_first=bk.wdeps("e1psA") + (bd.cdeps if tc == 0 else []),
                       tag=f"sc{tc}")
        te = p.vcopy(prc[:, :], psA[:, :], deps=[tm] + bk.wdeps("e1_prc"))
        bk.read("e1psA", te)
        r0 = p.red(colz[:, 0:1], prc[:, :], ALU.max, deps=[te] + bk.wdeps("e1_colz"))
        r1 = p.ts(colz[:, 0:1], colz[:, 0:1], -1.0, None, ALU.mult, deps=[r0])
        r2 = p.act(prc[:, :], prc[:, :], AF.Exp, bias=colz[:, 0:1], accum_out=colz[:, 1:2],
                   deps=[r1], tag=f"e1exp{tc}")
        r3 = p.recip(colz[:, 1:2], colz[:, 1:2], deps=[r2])
        r4 = p.ts(bd.probs[:, 8 * tc:8 * (tc + 1)], prc[:, :], colz[:, 1:2], None, ALU.mult,
                  deps=[r3] + (bk.wdeps("e_probs") if tc == 0 else []), tag=f"e1p{tc}")
        bk.read("e1_prc", r4)
        bk.read("e1_colz", r4)
        tt1 = p.tr(psT[:, :], bd.probs[:, 8 * tc:8 * (tc + 1)], C["c_identf"][:, :],
                   deps=[r4] + bk.wdeps("e1psT"), tag=f"e1T{tc}")
        tt2 = p.vcopy(bd.probsT[:, 128 * tc:128 * (tc + 1)], psT[:, :], deps=[tt1])
        bk.read("e1psT", tt2)
    cps = []
    for dm in range(8):
        cp = p.vcopy(bd.x3own[dm][:, :], x3F[dm][:, 0:HT], deps=[dx[dm]])
        cps.append(cp)
    bd.e1_done = cps


def stage_E2(bd):
    p, C = bd.p, bd.C
    bk = bd.bk
    lo = bd.sb("e2_lo", (8, 1), F32)
    hi = bd.sb("e2_hi", (8, 1), F32)
    tau = bd.sb("e2_tau", (8, 1), F32)
    cnt = bd.sb("e2_cnt", (8, 1), F32)
    sel = bd.sb("e2_sel", (8, 2), F32)
    dl = bd.sb("e2_dl", (8, 2), F32)
    cmp01 = bd.sb("e2_cmp", (8, T), BF16)
    t0 = p.step("vector", lambda v: v.memset(lo[:, :], 0.0))
    t1 = p.step("vector", lambda v: v.memset(hi[:, :], 2.0), deps=[t0])
    t2 = p.step("vector", lambda v: v.memset(tau[:, :], 1.0), deps=[t1])
    tprev = t2
    for it in range(40):
        c1 = p.ts(cmp01[:, :], bd.probsT[:, :], tau[:, :], None, ALU.is_ge,
                  deps=[tprev], tag=f"bis{it}")
        c2 = p.red(cnt[:, :], cmp01[:, :], ALU.add, deps=[c1])
        c3 = p.ts(sel[:, 0:1], cnt[:, :], 1024.0, None, ALU.is_ge, deps=[c2])
        c4 = p.ts(sel[:, 1:2], sel[:, 0:1], -1.0, 1.0, ALU.mult, ALU.add, deps=[c3])
        c5 = p.tt(dl[:, 0:1], tau[:, :], lo[:, :], ALU.subtract, deps=[c4])
        c6 = p.tt(dl[:, 1:2], tau[:, :], hi[:, :], ALU.subtract, deps=[c5])
        c7 = p.stt(lo[:, :], dl[:, 0:1], sel[:, 0:1], lo[:, :], ALU.mult, ALU.add, deps=[c6])
        c8 = p.stt(hi[:, :], dl[:, 1:2], sel[:, 1:2], hi[:, :], ALU.mult, ALU.add, deps=[c7])
        c9 = p.tt(tau[:, :], lo[:, :], hi[:, :], ALU.add, deps=[c8])
        tprev = p.ts(tau[:, :], tau[:, :], 0.5, None, ALU.mult, deps=[c9])
    bd.tau_lo = lo
    bd.e2_done = tprev


def stage_E3(bd):
    p, C = bd.p, bd.C
    w1d = bd.din("w_f1w1", (8 * 1024, 4096))
    w2d = bd.din("w_f1w2", (8 * 4096, 1024))
    i_x3own = bd.din("i_x3own", (HT, D))
    d_out = bd.dout("d_out", (HT, D))
    bk = bd.bk
    kb5, kb5t = bd.kcol(1e-5)

    w1sb = bd.sb("e3_w1", (128, 16384), BF16)
    w2sb = bd.sb("e3_w2", (128, 16384), BF16)
    hsb = bd.sb("e3_h", (128, 8192), BF16)
    yacc = bd.sb("e3_y", (128, 16 * 1024), BF16)
    tbc = bd.sb("e3_tbc", (128, 8), F32)
    wcol = bd.sb("e3_w", (128, 256), F32)
    taur = bd.sb("e3_taur", (1, 8), F32)
    x3c = [bd.sb(f"e3_x3c{i}", (128, 1024), F32) for i in range(2)]
    yf = bd.sb("e3_yf", (128, 1024), F32)
    sqt = bd.sb("e3_sqt", (128, 1024), BF16)
    cols = bd.sb("e3_cols", (128, 4), F32)
    psH = [bd.ps("e3psH0", (128, 512)), bd.ps("e3psH1", (128, 512))]
    psY = [bd.ps("e3psY0", (128, 512)), bd.ps("e3psY1", (128, 512))]
    psS = bd.ps("e3psS", (1, 8))
    psB2 = bd.ps("e3psB2", (128, 8))

    tz = p.step("vector", lambda v: v.memset(yacc[:, :], 0.0), deps=[])
    tt0 = p.tr(psS[:, :], bd.tau_lo[:, :], C["c_identf"][0:8, 0:8],
               deps=[bd.e2_done], tag="tauT")
    tt1 = p.vcopy(taur[:, :], psS[:, :], deps=[tt0])
    tt2 = p.mm(psB2[:, :], C["c_ones_row_f"][:, :], taur[:, :], True, True, deps=[tt1],
               tag="taubc")
    tt3 = p.vcopy(tbc[:, :], psB2[:, :], deps=[tt2])
    bk.read("e3psB2", tt3)
    wts = []
    for tc in range(NTC):
        m1 = p.tt(wcol[:, 8 * tc:8 * (tc + 1)], bd.probs[:, 8 * tc:8 * (tc + 1)],
                  tbc[:, :], ALU.is_ge, deps=[tt3], tag=f"msk{tc}")
        m2 = p.tt(wcol[:, 8 * tc:8 * (tc + 1)], wcol[:, 8 * tc:8 * (tc + 1)],
                  bd.probs[:, 8 * tc:8 * (tc + 1)], ALU.mult, deps=[m1])
        wts.append(m2)
    ydep = {"last": tz}
    for e in range(8):
        for hh in range(2):
            wfence = [p.fence("tensor")]
            dw1 = [p.dma("gpsimd", w1sb[:, 2048 * dm:2048 * (dm + 1)],
                         w1d[1024 * e + 128 * dm:1024 * e + 128 * (dm + 1),
                             2048 * hh:2048 * (hh + 1)],
                         deps=wfence if dm == 0 else (), tag=f"w1_{e}_{hh}")
                   for dm in range(8)]
            dw2 = [p.dma("gpsimd", w2sb[:, 1024 * hc:1024 * (hc + 1)],
                         w2d[4096 * e + 2048 * hh + 128 * hc:
                             4096 * e + 2048 * hh + 128 * (hc + 1), :],
                         deps=wfence if hc == 0 else (), tag=f"w2_{e}_{hh}")
                   for hc in range(16)]
            for tg in range(4):
                prev_pe = [p.fence("tensor")]
                for hc in range(16):
                    ps_ = psH[hc % 2]
                    ops = [(w1sb[:, 2048 * dm + 128 * hc:2048 * dm + 128 * (hc + 1)],
                            bd.x3own[dm][:, 512 * tg:512 * (tg + 1)],
                            [dw1[dm]] + (bd.e1_done if e == 0 and hh == 0 and tg == 0 else []))
                           for dm in range(8)]
                    th = p.mmgroup(ps_[:, :], ops, deps_first=bk.wdeps(ps_.name),
                                   tag=f"f1h{e}_{hh}_{tg}_{hc}")
                    tg_ = p.act(hsb[:, 512 * hc:512 * (hc + 1)], ps_[:, :], AF.Gelu,
                                deps=[th] + prev_pe, tag=f"f1g{e}_{hh}_{tg}_{hc}")
                    bk.read(ps_.name, tg_)
                for sub in range(4):
                    for half in range(2):
                        ps_ = psY[(sub * 2 + half) % 2]
                        ops = [(hsb[:, 512 * hc + 128 * sub:512 * hc + 128 * (sub + 1)],
                                w2sb[:, 1024 * hc + 512 * half:1024 * hc + 512 * (half + 1)],
                                [dw2[hc]] if tg == 0 else ()) for hc in range(16)]
                        ty = p.mmgroup(ps_[:, :], ops,
                                       deps_first=bk.wdeps(ps_.name) + [p.fence("scalar")],
                                       tag=f"f1y{e}_{hh}_{tg}_{sub}_{half}")
                        tcb = 4 * tg + sub
                        ysl = yacc[:, 1024 * tcb + 512 * half:1024 * tcb + 512 * (half + 1)]
                        ta = p.stt(ysl, ps_[:, :], wcol[:, 8 * tcb + e:8 * tcb + e + 1],
                                   ysl, ALU.mult, ALU.add,
                                   deps=[ty, ydep["last"], wts[tcb]],
                                   tag=f"f1a{e}_{hh}_{tg}_{sub}_{half}")
                        bk.read(ps_.name, ta)
                        ydep["last"] = ta
    done = {}
    for tc in range(NTC):
        i = tc % 2
        blk = slice(128 * tc, 128 * (tc + 1))
        dx3 = p.dma("sync", x3c[i][:, :], i_x3own[blk, :], deps=[done.get(("x3", i))])
        a1 = p.tt(yf[:, :], yacc[:, 1024 * tc:1024 * (tc + 1)], x3c[i][:, :], ALU.add,
                  deps=[dx3, ydep["last"]] + bk.wdeps("e3_yf"), tag=f"fin{tc}")
        done[("x3", i)] = a1
        l2 = p.red(cols[:, 0:1], yf[:, :], ALU.add, deps=[a1], negate=True)
        l3 = p.ts(cols[:, 0:1], cols[:, 0:1], 1.0 / D, None, ALU.mult, deps=[l2])
        l4 = p.act(sqt[:, :], yf[:, :], AF.Square, bias=cols[:, 0:1],
                   accum_out=cols[:, 1:2], deps=[l3, done.get("sqt")])
        done["sqt"] = l4
        l5 = p.act(cols[:, 1:2], cols[:, 1:2], AF.Ln, bias=kb5, scale=1.0 / D,
                   deps=[l4, kb5t])
        l6 = p.act(cols[:, 1:2], cols[:, 1:2], AF.Exp, scale=-0.5, deps=[l5])
        l7 = p.ts(yf[:, :], yf[:, :], cols[:, 0:1], cols[:, 1:2], ALU.add, ALU.mult,
                  deps=[l6], tag=f"lnx4_{tc}")
        bk.read("e3_yf", l7)
        d4 = p.dma("sync", d_out[blk, :], yf[:, :], deps=[l7])
        bk.read("e3_yf", d4)


def launch_E(bd):
    bd.x3own = [bd.sb(f"x3own{i}", (128, HT), BF16) for i in range(8)]
    bd.probs = bd.sb("e_probs", (128, 256), F32)
    bd.probsT = bd.sb("e_probsT", (8, T), F32)
    with bd.scoped():
        stage_E1(bd)
    bd.p.barrier("b_e1")
    with bd.scoped():
        stage_E2(bd)
    bd.p.barrier("b_e2")
    with bd.scoped():
        stage_E3(bd)


def _mk(builder, taps=()):
    nc = bass.Bass("TRN2", target_bir_lowering=False)
    st = contextlib.ExitStack()
    with st:
        bd = Build(nc, st, taps)
        bd.load_consts()
        builder(bd)
        deps = bd.p.all_fences()
        for (tap, ap) in bd.tap_pending:
            bd.p.dma("gpsimd", tap, ap, deps=deps, tag="tapcopy")
        fin = bd.p.op("sync", lambda e: None, tag="final")
        fin.waits += bd.p.all_fences()
        bd.p.emit(st)
    return nc


def launch_A(bd):
    with bd.scoped():
        stage_S1(bd)


def launch_B(bd):
    bd.x1T = [bd.sb(f"x1T{i}", (128, HT), BF16) for i in range(8)]
    bd.yacc = bd.sb("yacc", (128, 16 * 1024), F32)
    with bd.scoped():
        stage_S2(bd)
    bd.p.barrier("b_s2")
    with bd.scoped():
        stage_S3(bd)
    bd.p.barrier("b_s3")
    with bd.scoped():
        stage_S4(bd)


def kernel(**inputs):
    maps = host_prep(inputs)
    cores = list(range(8))
    resA = run_bass_kernel_spmd(_mk(launch_A), maps, core_ids=cores).results
    add_launchB_inputs(maps, resA)
    resB = run_bass_kernel_spmd(_mk(launch_B), maps, core_ids=cores).results
    add_launchC_inputs(maps, resB, inputs)
    resC = run_bass_kernel_spmd(_mk(launch_C), maps, core_ids=cores).results
    add_launchD_inputs(maps, resB, resC, inputs)
    resD = run_bass_kernel_spmd(_mk(launch_D), maps, core_ids=cores).results
    add_launchE_inputs(maps, resD, inputs)
    resE = run_bass_kernel_spmd(_mk(launch_E), maps, core_ids=cores).results
    out = np.empty((B, T, D), np.float32)
    for c in range(8):
        b, r = c // 2, c % 2
        out[b, r * HT:(r + 1) * HT] = resE[c]["d_out"]
    return out
